# version 02
# baseline (speedup 1.0000x reference)
import math
import numpy as np
import ml_dtypes
import concourse.bass as bass
import concourse.mybir as mybir
from concourse.bass_utils import run_bass_kernel_spmd
from contextlib import ExitStack

F32 = mybir.dt.float32
BF16 = mybir.dt.bfloat16
I32 = mybir.dt.int32
AF = mybir.ActivationFunctionType
ALU = mybir.AluOpType
AX = mybir.AxisListType

NCORES = 8
D = 4096
S = 4096
B = 2
TC = 1024
NKT = D // 128
NEG = float(np.finfo(np.float32).min)
PI = float(np.pi)
TWO_PI = 2.0 * PI
K_TOP = 410


class Mi(dict):
    strict = False

    def __missing__(self, key):
        if self.strict:
            raise KeyError(key)
        return 0


class FakeInstr:
    def then_inc(self, *a, **k):
        return self


class FakeCEng:

    def __init__(self):
        self.n = 0

    def wait(self, other, val=None):
        pass

    def un(self, name):
        return lambda *a, **k: FakeInstr()

    def __getattr__(self, name):
        def call(*a, **k):
            self.n += 1
            return FakeInstr()
        return call


class CEng:

    def __init__(self, e, semh, auto=False):
        self.__dict__["_e"] = e
        self.__dict__["sem"] = semh
        self.__dict__["n"] = 0
        self.__dict__["auto"] = auto

    def wait(self, other, val=None):
        if isinstance(other, CEng):
            self._e.wait_ge(other.sem, other.n if val is None else val)
        else:
            self._e.wait_ge(other, val)

    def un(self, name):
        return getattr(self._e, name)

    def __getattr__(self, name):
        fn = getattr(self._e, name)

        def call(*a, **k):
            if self.auto and self.n > 0:
                self._e.wait_ge(self.sem, self.n)
            instr = fn(*a, **k)
            instr.then_inc(self.sem, 1)
            self.__dict__["n"] += 1
            return instr
        return call


def _w_prep_lhsT(WT):
    K, M = WT.shape
    assert M % 128 == 0 and K % 128 == 0
    v = WT.reshape(K // 128, 128, M // 128, 128).transpose(2, 1, 0, 3)
    return np.ascontiguousarray(v.reshape(M // 128, 128, K)).astype(np.float32)


def _range_reduce(V, x, shift, out, qint, qfl):
    V.tensor_scalar(out=out[:], in0=x[:], scalar1=float(shift), scalar2=1.0 / TWO_PI,
                    op0=ALU.add, op1=ALU.mult)
    V.wait(V, V.n)
    V.tensor_copy(out=qint[:], in_=out[:])
    V.wait(V, V.n)
    V.tensor_copy(out=qfl[:], in_=qint[:])
    V.wait(V, V.n)
    V.tensor_scalar(out=qfl[:], in0=qfl[:], scalar1=-TWO_PI, scalar2=None, op0=ALU.mult)
    V.tensor_scalar(out=out[:], in0=x[:], scalar1=float(shift), scalar2=None, op0=ALU.add)
    V.wait(V, V.n)
    V.tensor_tensor(out=out[:], in0=out[:], in1=qfl[:], op=ALU.add)
    V.wait(V, V.n)
    V.tensor_scalar(out=qfl[:], in0=out[:], scalar1=PI, scalar2=-TWO_PI, op0=ALU.is_gt, op1=ALU.mult)
    V.wait(V, V.n)
    V.tensor_tensor(out=out[:], in0=out[:], in1=qfl[:], op=ALU.add)
    V.wait(V, V.n)
    V.tensor_scalar(out=qfl[:], in0=out[:], scalar1=-PI, scalar2=TWO_PI, op0=ALU.is_lt, op1=ALU.mult)
    V.wait(V, V.n)
    V.tensor_tensor(out=out[:], in0=out[:], in1=qfl[:], op=ALU.add)



def build_launch_A():
    nc = bass.Bass("TRN2", target_bir_lowering=False)
    x_sl = nc.dram_tensor("x_sl", [D, TC], F32, kind="ExternalInput").ap()
    wqkv = nc.dram_tensor("wqkv", [48, 128, D], F32, kind="ExternalInput").ap()
    tb = nc.dram_tensor("tb", [128, TC], F32, kind="ExternalInput").ap()
    jot = nc.dram_tensor("jot", [128, 1], F32, kind="ExternalInput").ap()
    onesk2 = nc.dram_tensor("onesk2", [2, 128], F32, kind="ExternalInput").ap()
    onescol = nc.dram_tensor("onescol", [128, 1], F32, kind="ExternalInput").ap()
    o_xn = nc.dram_tensor("o_xn", [D, TC], BF16, kind="ExternalOutput").ap()
    o_qkv = nc.dram_tensor("o_qkv", [6144, TC], BF16, kind="ExternalOutput").ap()

    NM = 48
    with ExitStack() as ctx:
        sbt = lambda name, shape, dt=F32: ctx.enter_context(nc.sbuf_tensor(name, shape, dt))
        xr = [sbt(f"xr{i}", [128, TC]) for i in range(2)]
        scr = [sbt(f"scr{i}", [128, TC]) for i in range(2)]
        xnbf = sbt("xnbf", [128, NKT * TC], BF16)
        murow = sbt("murow", [1, TC]); varrow = sbt("varrow", [1, TC])
        sqmu = sbt("sqmu", [1, TC]); rstdrow = sbt("rstdrow", [1, TC])
        rows2 = sbt("rows2", [2, TC])
        mub = sbt("mub", [128, TC]); rstdb = sbt("rstdb", [128, TC])
        ones2 = sbt("ones2", [2, 128]); onec = sbt("onec", [128, 1])
        tbt = sbt("tbt", [128, TC]); jt = sbt("jt", [128, 1])
        invc = sbt("invc", [128, 1])
        cost = sbt("cost", [128, TC]); sint = sbt("sint", [128, TC])
        qint = sbt("qint", [128, TC], I32)
        wst = sbt("wst", [128, D])
        wbf = [sbt(f"wbf{i}", [128, D], BF16) for i in range(2)]
        rott = sbt("rott", [128, TC]); arg = rott
        tmpa = sbt("tmpa", [128, TC]); tmpb = sbt("tmpb", [128, TC])
        outbf = [sbt(f"outbf{i}", [128, TC], BF16) for i in range(2)]

        ps_b = ctx.enter_context(nc.psum_tensor("ps_b", [128, TC], F32))
        ps_o = [ctx.enter_context(nc.psum_tensor(f"ps_o{i}", [128, TC], F32)) for i in range(2)]
        ps_mu = ps_o[0][0:1, :]
        ps_sq = ps_o[1][0:1, :]

        sem = lambda name: ctx.enter_context(nc.semaphore(name))
        ld1 = sem("ld1")
        lx = [sem("lxa"), sem("lxb")]; l2 = [sem("l2a"), sem("l2b")]
        stq = [sem("stqa"), sem("stqb")]; stdx = sem("stdx"); wld = sem("wld")
        vsem = sem("vsem"); ssem = sem("ssem"); tsem = sem("tsem")
        block = ctx.enter_context(nc.Block())
        mi = Mi()

        def emit_scalar(E):
            for kt in range(NKT):
                E.wait(lx[kt % 2], 16 * (kt // 2 + 1))
                if kt >= 2:
                    E.wait(tsem, mi[("sqmm", kt - 2)])
                E.activation(out=scr[kt % 2][:], in_=xr[kt % 2][:], func=AF.Square)
                mi[("sq", kt)] = E.n
            E.wait(vsem, mi["murow"])
            E.activation(out=sqmu[:], in_=murow[:], func=AF.Square)
            mi["sqmu"] = E.n
            E.wait(vsem, mi["varrow"])
            E.activation(out=varrow[:], in_=varrow[:], func=AF.Ln)
            E.wait(E, E.n)
            E.activation(out=rstdrow[:], in_=varrow[:], func=AF.Exp, scale=-0.5)
            mi["rstd"] = E.n
            E.wait(ld1, 64)
            E.activation(out=invc[:], in_=jt[:], func=AF.Exp,
                         scale=-2.0 * math.log(10000.0) / 128.0)
            mi["invc"] = E.n
            E.wait(vsem, mi["sarg"])
            E.activation(out=sint[:], in_=arg[:], func=AF.Sin)
            mi["sint"] = E.n
            E.wait(vsem, mi["carg"])
            E.activation(out=cost[:], in_=arg[:], func=AF.Sin)
            mi["cost"] = E.n

        def emit_tensor(E):
            E.wait(ld1, 64)
            for kt in range(NKT):
                E.wait(lx[kt % 2], 16 * (kt // 2 + 1))
                E.un("matmul")(ps_mu[:, 0:512], onec[:], xr[kt % 2][:, 0:512],
                               start=(kt == 0), stop=(kt == NKT - 1))
                E.matmul(ps_mu[:, 512:1024], onec[:], xr[kt % 2][:, 512:1024],
                         start=(kt == 0), stop=(kt == NKT - 1))
                mi[("mu", kt)] = E.n
                E.wait(ssem, mi[("sq", kt)])
                E.un("matmul")(ps_sq[:, 0:512], onec[:], scr[kt % 2][:, 0:512],
                               start=(kt == 0), stop=(kt == NKT - 1))
                E.matmul(ps_sq[:, 512:1024], onec[:], scr[kt % 2][:, 512:1024],
                         start=(kt == 0), stop=(kt == NKT - 1))
                mi[("sqmm", kt)] = E.n
            E.wait(vsem, mi["negmu"])
            E.un("matmul")(ps_b[:, 0:512], ones2[:], rows2[:, 0:512], start=True, stop=True)
            E.matmul(ps_b[:, 512:1024], ones2[:], rows2[:, 512:1024], start=True, stop=True)
            mi["bc_mu"] = E.n
            E.wait(vsem, mi["rows_rstd"])
            E.un("matmul")(ps_b[:, 0:512], ones2[:], rows2[:, 0:512], start=True, stop=True)
            E.matmul(ps_b[:, 512:1024], ones2[:], rows2[:, 512:1024], start=True, stop=True)
            mi["bc_rstd"] = E.n
            E.wait(vsem, mi[("xn", NKT - 1)])
            for m in range(NM):
                E.wait(vsem, mi[("wcast", m)])
                if m >= 2:
                    E.wait(vsem, mi[("evac", m - 2)])
                pso = ps_o[m % 2]
                for h in range(2):
                    for kt in range(NKT):
                        last = (h == 1 and kt == NKT - 1)
                        f = E.matmul if last else E.un("matmul")
                        f(pso[:, h * 512:(h + 1) * 512],
                          wbf[m % 2][:, kt * 128:(kt + 1) * 128],
                          xnbf[:, kt * TC + h * 512:kt * TC + (h + 1) * 512],
                          start=(kt == 0), stop=(kt == NKT - 1))
                mi[("mm", m)] = E.n

        def emit_vector(E):
            E.wait(tsem, mi[("mu", NKT - 1)])
            E.tensor_scalar(out=murow[:], in0=ps_mu[:], scalar1=1.0 / D, scalar2=None, op0=ALU.mult)
            mi["murow"] = E.n
            E.wait(ssem, mi["sqmu"])
            E.wait(tsem, mi[("sqmm", NKT - 1)])
            E.tensor_scalar(out=varrow[:], in0=ps_sq[:], scalar1=1.0 / D, scalar2=None, op0=ALU.mult)
            E.wait(E, E.n)
            E.tensor_tensor(out=varrow[:], in0=varrow[:], in1=sqmu[:], op=ALU.subtract)
            E.wait(E, E.n)
            E.tensor_scalar(out=varrow[:], in0=varrow[:], scalar1=1e-5, scalar2=None, op0=ALU.add)
            mi["varrow"] = E.n
            E.memset(rows2[:], 0.0)
            E.wait(E, E.n)
            E.tensor_scalar(out=rows2[0:1, :], in0=murow[:], scalar1=-1.0, scalar2=None, op0=ALU.mult)
            mi["negmu"] = E.n
            E.wait(tsem, mi["bc_mu"])
            E.tensor_copy(out=mub[:], in_=ps_b[:])
            mi["mub"] = E.n
            E.wait(ssem, mi["rstd"])
            E.wait(E, E.n)
            E.tensor_copy(out=rows2[0:1, :], in_=rstdrow[:])
            mi["rows_rstd"] = E.n
            E.wait(tsem, mi["bc_rstd"])
            E.tensor_copy(out=rstdb[:], in_=ps_b[:])
            mi["stats_done"] = E.n
            E.wait(ssem, mi["invc"])
            E.wait(ld1, 64)
            E.tensor_scalar(out=tbt[:], in0=tbt[:], scalar1=invc[:], scalar2=None, op0=ALU.mult)
            E.wait(E, E.n)
            _range_reduce(E, tbt, 0.0, arg, qint, tmpb)
            mi["sarg"] = E.n
            E.wait(ssem, mi["sint"])
            _range_reduce(E, tbt, PI / 2, arg, qint, tmpb)
            mi["carg"] = E.n
            for kt in range(NKT):
                E.wait(l2[kt % 2], 16 * (kt // 2 + 1))
                E.tensor_tensor(out=tmpa[:], in0=xr[kt % 2][:], in1=mub[:], op=ALU.add)
                E.wait(E, E.n)
                E.tensor_tensor(out=xnbf[:, kt * TC:(kt + 1) * TC], in0=tmpa[:], in1=rstdb[:],
                                op=ALU.mult)
                mi[("xn", kt)] = E.n
            E.wait(ssem, mi["cost"])

            def evac_rope(m):
                E.wait(tsem, mi[("mm", m)])
                if m >= 2:
                    E.wait(stq[m % 2], 16 * (m // 2))
                pso = ps_o[m % 2]
                E.tensor_copy(out=scr[m % 2][:], in_=pso[:])
                mi[("evac", m)] = E.n
                q = scr[m % 2]
                E.wait(E, E.n)
                if m < 40:
                    E.tensor_scalar(out=rott[0:64, :], in0=q[64:128, :], scalar1=-1.0,
                                    scalar2=None, op0=ALU.mult)
                    E.tensor_copy(out=rott[64:128, :], in_=q[0:64, :])
                    E.tensor_tensor(out=tmpa[:], in0=q[:], in1=cost[:], op=ALU.mult)
                    E.wait(E, E.n)
                    E.tensor_tensor(out=tmpb[:], in0=rott[:], in1=sint[:], op=ALU.mult)
                    E.wait(E, E.n)
                    E.tensor_tensor(out=outbf[m % 2][:], in0=tmpa[:], in1=tmpb[:], op=ALU.add)
                else:
                    E.tensor_copy(out=outbf[m % 2][:], in_=q[:])
                mi[("rope", m)] = E.n

            for m in range(NM):
                if m >= 2:
                    E.wait(tsem, mi[("mm", m - 2)])
                E.wait(wld, 16 * (m + 1))
                E.tensor_copy(out=wbf[m % 2][:], in_=wst[:])
                mi[("wcast", m)] = E.n
                if m >= 1:
                    evac_rope(m - 1)
            evac_rope(NM - 1)

        emit_scalar(FakeCEng()); emit_tensor(FakeCEng()); emit_vector(FakeCEng())
        mi.strict = True

        @block.sync
        def _(e):
            e.dma_start(out=ones2[:], in_=onesk2).then_inc(ld1, 16)
            e.dma_start(out=onec[:], in_=onescol).then_inc(ld1, 16)
            e.dma_start(out=tbt[:], in_=tb).then_inc(ld1, 16)
            e.dma_start(out=jt[:], in_=jot).then_inc(ld1, 16)
            e.dma_start(out=wst[:], in_=wqkv[0]).then_inc(wld, 16)
            for kt in range(NKT):
                if kt >= 2:
                    e.wait_ge(tsem, mi[("mu", kt - 2)])
                    e.wait_ge(ssem, mi[("sq", kt - 2)])
                e.dma_start(out=xr[kt % 2][:], in_=x_sl[kt * 128:(kt + 1) * 128, :]).then_inc(lx[kt % 2], 16)
            e.wait_ge(vsem, mi["stats_done"])
            for kt in range(NKT):
                if kt >= 2:
                    e.wait_ge(vsem, mi[("xn", kt - 2)])
                e.dma_start(out=xr[kt % 2][:], in_=x_sl[kt * 128:(kt + 1) * 128, :]).then_inc(l2[kt % 2], 16)
            for kt in range(NKT):
                e.wait_ge(vsem, mi[("xn", kt)])
                e.dma_start(out=o_xn[kt * 128:(kt + 1) * 128, :],
                            in_=xnbf[:, kt * TC:(kt + 1) * TC]).then_inc(stdx, 16)
            for m in range(1, NM + 1):
                if m < NM:
                    e.wait_ge(vsem, mi[("wcast", m - 1)])
                    e.dma_start(out=wst[:], in_=wqkv[m]).then_inc(wld, 16)
                sm = m - 1
                e.wait_ge(vsem, mi[("rope", sm)])
                e.dma_start(out=o_qkv[sm * 128:(sm + 1) * 128, :],
                            in_=outbf[sm % 2][:]).then_inc(stq[sm % 2], 16)
            e.wait_ge(stdx, 16 * NKT)
            e.wait_ge(stq[0], 16 * (NM // 2))
            e.wait_ge(stq[1], 16 * (NM // 2))

        @block.scalar
        def _(e):
            emit_scalar(CEng(e, ssem, auto=True))

        @block.tensor
        def _(e):
            emit_tensor(CEng(e, tsem))

        @block.vector
        def _(e):
            emit_vector(CEng(e, vsem, auto=True))
    return nc


def host_prep_A(inputs):
    x0 = inputs["L0_attn_hidden_states"]
    xT = np.ascontiguousarray(x0.reshape(B * S, D).T).astype(np.float32)
    Wq, Wk, Wv = inputs["L0_attn_Wq"], inputs["L0_attn_Wk"], inputs["L0_attn_Wv"]
    Wcat_T = np.concatenate([Wq.T, Wk.T, Wv.T], axis=1).astype(np.float32)
    wqkv = _w_prep_lhsT(Wcat_T)
    onesk2 = np.zeros((2, 128), np.float32); onesk2[0] = 1.0
    onescol = np.ones((128, 1), np.float32)
    jot = (np.arange(128) % 64).astype(np.float32).reshape(128, 1)
    in_maps = []
    for c in range(NCORES):
        s0 = (c % 4) * TC
        tbv = np.broadcast_to(np.arange(s0, s0 + TC, dtype=np.float32), (128, TC)).copy()
        in_maps.append({
            "x_sl": np.ascontiguousarray(xT[:, c * TC:(c + 1) * TC]),
            "wqkv": wqkv, "tb": tbv, "jot": jot, "onesk2": onesk2, "onescol": onescol,
        })
    return in_maps



SCALE_ATT = 1.0 / math.sqrt(128.0)
NIT_SEL = 6
GAMMA_SEL = [1.0, 1.0, 0.85, 0.7, 0.6, 0.55, 0.5, 0.45, 0.4, 0.35, 0.3, 0.3,
             0.3, 0.3, 0.3, 0.3, 0.3, 0.3, 0.3, 0.3]
BIGM = 1.0e37
NEGF = -1.0e38


def build_launch_B(npairs=8, nit=NIT_SEL, cut=0):
    nc = bass.Bass("TRN2", target_bir_lowering=False)
    qh = nc.dram_tensor("qh", [8, 128, S], BF16, kind="ExternalInput").ap()
    kh = nc.dram_tensor("kh", [2, 128, S], BF16, kind="ExternalInput").ap()
    ktokp = nc.dram_tensor("ktokp", [2, 128, S], BF16, kind="ExternalInput").ap()
    vtokp = nc.dram_tensor("vtokp", [2, 128, S], BF16, kind="ExternalInput").ap()
    cmaskd = nc.dram_tensor("cmaskd", [128, 4 * 512], F32, kind="ExternalInput").ap()
    cmask2d = nc.dram_tensor("cmask2d", [128, 4 * 512], F32, kind="ExternalInput").ap()
    ztgt_d = nc.dram_tensor("ztgt_d", [128, 32], F32, kind="ExternalInput").ap()
    gain_d = nc.dram_tensor("gain_d", [128, 32], F32, kind="ExternalInput").ap()
    ninv_d = nc.dram_tensor("ninv_d", [128, 32], F32, kind="ExternalInput").ap()
    kamask_d = nc.dram_tensor("kamask_d", [128, 4], F32, kind="ExternalInput").ap()
    ones128b_d = nc.dram_tensor("ones128b_d", [128, 1], BF16, kind="ExternalInput").ap()
    onesk2f_d = nc.dram_tensor("onesk2f_d", [2, 128], F32, kind="ExternalInput").ap()
    i128f_d = nc.dram_tensor("i128f_d", [128, 128], F32, kind="ExternalInput").ap()
    o_att = nc.dram_tensor("o_att", [npairs * 128, S], BF16, kind="ExternalOutput").ap()

    NG = 8
    with ExitStack() as ctx:
        sbt = lambda name, shape, dt=F32: ctx.enter_context(nc.sbuf_tensor(name, shape, dt))
        qb = [sbt(f"qb{i}", [128, S], BF16) for i in range(2)]
        kb = sbt("kb", [128, S], BF16)
        ktk = sbt("ktk", [128, S], BF16)
        vtk = sbt("vtk", [128, S], BF16)
        srow = [sbt(f"srow{i}", [128, S]) for i in range(4)]
        junk = sbt("junk", [128, S])
        gsb = sbt("gsb", [128, 128]); gbf = sbt("gbf", [128, 128], BF16)
        ksum = sbt("ksum", [128, 1]); ksumb = sbt("ksumb", [128, 1], BF16)
        kpart = sbt("kpart", [128, 1])
        qg = sbt("qg", [128, 128], BF16)
        scol = sbt("scol", [128, 8])
        thcol = sbt("thcol", [128, 4]); g0col = sbt("g0col", [128, 4])
        mucol = sbt("mucol", [128, 4]); sgcol = sbt("sgcol", [128, 4])
        vrcol = sbt("vrcol", [128, 4]); sqmu4 = sbt("sqmu4", [128, 4])
        erra = sbt("erra", [128, 4]); errb = sbt("errb", [128, 4])
        dfrz = sbt("dfrz", [128, 4]); cnt = sbt("cnt", [128, 4])
        scr4 = sbt("scr4", [128, 4]); scr4b = sbt("scr4b", [128, 4])
        rmx = sbt("rmx", [128, 4]); ka4 = sbt("ka4", [128, 4])
        zt_s = sbt("zt_s", [128, 32]); gn_s = sbt("gn_s", [128, 32]); ni_s = sbt("ni_s", [128, 32])
        cm_s = sbt("cm_s", [128, 4 * 512]); cm2_s = sbt("cm2_s", [128, 4 * 512])
        ones1b = sbt("ones1b", [128, 1], BF16); ok2f = sbt("ok2f", [2, 128])
        i128f = sbt("i128f", [128, 128])
        throw = sbt("throw", [2, 512])
        thb = sbt("thb", [128, 512])
        argt = [sbt(f"argt{i}", [128, 1024]) for i in range(2)]
        mt = [sbt(f"mt{i}", [128, 1024]) for i in range(2)]
        pt = [sbt(f"pt{i}", [128, 1024], BF16) for i in range(2)]
        zb = sbt("zb", [128, 512])
        outb = [sbt(f"outb{i}", [128, 512], BF16) for i in range(2)]

        ps_s = ctx.enter_context(nc.psum_tensor("ps_s", [128, 1024], F32))
        ps_pv = ctx.enter_context(nc.psum_tensor("ps_pv", [128, 512], F32))
        ps_g = ctx.enter_context(nc.psum_tensor("ps_g", [128, 128], F32))
        ps_st = ctx.enter_context(nc.psum_tensor("ps_st", [128, 8], F32))
        ps_row = ctx.enter_context(nc.psum_tensor("ps_row", [1, 512], F32))
        ps_z = ctx.enter_context(nc.psum_tensor("ps_z", [1, 512], F32))
        ps_zb = ctx.enter_context(nc.psum_tensor("ps_zb", [128, 512], F32))

        sem = lambda name: ctx.enter_context(nc.semaphore(name))
        ldc = sem("ldc")
        lq = [sem("lqa"), sem("lqb")]; lkv = sem("lkv")
        stq = [sem("stqa"), sem("stqb")]
        vsem = sem("vsem"); ssem = sem("ssem"); tsem = sem("tsem")
        block = ctx.enter_context(nc.Block())
        mi = Mi()

        NLDC = 9
        sbank_last = {}

        def emit_tensor(E):
            sbank_last.clear()
            E.wait(ldc, 16 * NLDC)
            for p in range(npairs):
                E.wait(lq[p % 2], 16 * (p // 2 + 1))
                if p % 4 == 0:
                    E.wait(lkv, 16 * 3 * (p // 4 + 1))
                for g in range(NG):
                    E.wait(vsem, mi[("gfree", p, g)])
                    for j in range(4 * g, 4 * g + 4):
                        E.matmul(ps_g[:], ktk[:, j * 128:(j + 1) * 128],
                                 ktk[:, j * 128:(j + 1) * 128],
                                 start=(j % 4 == 0), stop=(j % 4 == 3))
                    mi[("gram", p, g)] = E.n
                    for d in range(4):
                        qt = 4 * g + d
                        lhs = qb[p % 2][:, qt * 128:(qt + 1) * 128]
                        for ch in range(g + 1):
                            bank = ch % 2
                            if bank in sbank_last:
                                E.wait(vsem, mi[sbank_last[bank]])
                            E.matmul(ps_s[:, bank * 512:bank * 512 + 512], lhs,
                                     kb[:, ch * 512:(ch + 1) * 512], start=True, stop=True)
                            mi[("smm", p, g, d, ch)] = E.n
                            sbank_last[bank] = ("sevac", p, g, d, ch)
                    if cut == 3:
                        mi[("pairdone", p)] = E.n
                        continue
                    E.wait(vsem, mi[("gcast", p, g)])
                    for d in range(4):
                        qt = 4 * g + d
                        lhs = qb[p % 2][:, qt * 128:(qt + 1) * 128]
                        E.matmul(ps_st[:, d:d + 1], lhs, ksumb[:], start=True, stop=True)
                        mi[("r1", p, g, d)] = E.n
                        E.wait(vsem, mi[("qgfree", p, g, d)])
                        E.matmul(ps_g[:], gbf[:], lhs, start=True, stop=True)
                        mi[("gq", p, g, d)] = E.n
                        E.wait(vsem, mi[("qgm", p, g, d)])
                        E.matmul(ps_st[:, 4 + d:5 + d], qg[:], ones1b[:], start=True, stop=True)
                        mi[("r2", p, g, d)] = E.n
                    if cut in (1, 2):
                        continue
                    for d in range(4):
                        E.wait(vsem, mi[("thdone", p, g)] if d == 0 else mi[("thsg", p, g, d - 1)])
                        E.matmul(ps_row[:, 0:128], thcol[:, d:d + 1],
                                 i128f[:], start=True, stop=True)
                        mi[("thT", p, g, d)] = E.n
                    E.wait(vsem, mi[("throw_asm", p, g)])
                    E.matmul(ps_zb[:], ok2f[:], throw[:], start=True, stop=True)
                    mi[("thbc", p, g)] = E.n
                    njt = 4 * g + 4
                    if cut == 4:
                        continue
                    for j0 in range(0, njt, 2):
                        for jj in range(min(2, njt - j0)):
                            j = j0 + jj
                            bank = jj
                            if bank in sbank_last:
                                E.wait(vsem, mi[sbank_last[bank]])
                            E.matmul(ps_s[:, jj * 512:(jj + 1) * 512],
                                     kb[:, j * 128:(j + 1) * 128],
                                     qb[p % 2][:, g * 512:(g + 1) * 512],
                                     start=True, stop=True)
                            mi[("stmm", p, g, j)] = E.n
                            sbank_last[bank] = ("argev", p, g, j)
                        if j0 == 0 and g > 0:
                            E.wait(vsem, mi[("outd", p, g - 1)])
                        if j0 == 0 and g == 0 and p > 0:
                            E.wait(vsem, mi[("outd", p - 1, NG - 1)])
                        if cut == 5:
                            mi[("pvz", p, g, j0)] = E.n
                            continue
                        E.wait(ssem, mi[("exp", p, g, j0)])
                        for jj in range(min(2, njt - j0)):
                            j = j0 + jj
                            E.matmul(ps_pv[:], vtk[:, j * 128:(j + 1) * 128],
                                     pt[(j0 // 2) % 2][:, jj * 512:(jj + 1) * 512],
                                     start=(j == 0), stop=(j == njt - 1))
                            E.matmul(ps_z[:], ones1b[:],
                                     pt[(j0 // 2) % 2][:, jj * 512:(jj + 1) * 512],
                                     start=(j == 0), stop=(j == njt - 1))
                        mi[("pvz", p, g, j0)] = E.n
                    if cut not in (5, 6):
                        E.wait(vsem, mi[("zrows", p, g)])
                        E.matmul(ps_zb[:], ok2f[:], throw[:], start=True, stop=True)
                    mi[("zbc", p, g)] = E.n
                mi[("pairdone", p)] = E.n

        def emit_scalar(E):
            for p in range(npairs):
                for g in range(NG):
                    if cut == 3:
                        continue
                    E.wait(vsem, mi[("muc", p, g)])
                    E.activation(out=sqmu4[:], in_=mucol[:], func=AF.Square)
                    mi[("sqmu", p, g)] = E.n
                    E.wait(vsem, mi[("varc", p, g)])
                    E.activation(out=sgcol[:], in_=vrcol[:], func=AF.Ln)
                    E.activation(out=sgcol[:], in_=sgcol[:], func=AF.Exp, scale=0.5)
                    mi[("sig", p, g)] = E.n
                    njt = 4 * g + 4
                    if cut in (1, 2, 3, 4):
                        continue
                    for j0 in range(0, njt, 2):
                        nw = min(2, njt - j0) * 512
                        if j0 >= 4 and cut != 5:
                            E.wait(tsem, mi[("pvz", p, g, j0 - 4)])
                        E.wait(vsem, mi[("arg2", p, g, j0)])
                        E.activation(out=pt[(j0 // 2) % 2][:, 0:nw],
                                     in_=argt[(j0 // 2) % 2][:, 0:nw],
                                     func=AF.Exp, scale=SCALE_ATT)
                        mi[("exp", p, g, j0)] = E.n

        def emit_vector(E):
            E.wait(ldc, 16 * NLDC)
            for p in range(npairs):
                for g in range(NG):
                    if p % 4 == 0 and g == 0:
                        E.wait(lkv, 16 * 3 * (p // 4 + 1))
                    for d in range(4):
                        for ch in range(g + 1):
                            E.wait(tsem, mi[("smm", p, g, d, ch)])
                            dst = srow[d][:, ch * 512:(ch + 1) * 512]
                            src = ps_s[:, (ch % 2) * 512:(ch % 2) * 512 + 512]
                            if ch == g:
                                E.tensor_tensor(out=dst, in0=src,
                                                in1=cm_s[:, d * 512:(d + 1) * 512], op=ALU.add)
                            else:
                                E.tensor_copy(out=dst, in_=src)
                            mi[("sevac", p, g, d, ch)] = E.n
                    E.wait(tsem, mi[("gram", p, g)])
                    if g == 0:
                        E.tensor_copy(out=gsb[:], in_=ps_g[:])
                    else:
                        E.scalar_tensor_tensor(out=gsb[:], in0=ps_g[:], scalar=1.0,
                                               in1=gsb[:], op0=ALU.mult, op1=ALU.add)
                    mi[("gevac", p, g)] = E.n
                    E.reduce_sum(out=kpart[:], in_=kb[:, g * 512:(g + 1) * 512], axis=AX.X)
                    if g == 0:
                        E.tensor_copy(out=ksum[:], in_=kpart[:])
                    else:
                        E.tensor_tensor(out=ksum[:], in0=ksum[:], in1=kpart[:], op=ALU.add)
                    E.tensor_copy(out=ksumb[:], in_=ksum[:])
                    E.tensor_copy(out=gbf[:], in_=gsb[:])
                    mi[("gcast", p, g)] = E.n
                    if cut == 3:
                        for d in range(4):
                            mi[("qgm", p, g, d)] = E.n
                            mi[("qgfree", p, g, d + 1)] = E.n
                        mi[("qgfree", p, g, 0)] = E.n
                        E.tensor_copy(out=outb[g % 2][:], in_=srow[0][:, 0:512])
                        mi[("outd", p, g)] = E.n
                        continue
                    for d in range(4):
                        E.wait(tsem, mi[("gq", p, g, d)])
                        E.tensor_tensor(out=qg[:], in0=ps_g[:],
                                        in1=qb[p % 2][:, (4 * g + d) * 128:(4 * g + d + 1) * 128],
                                        op=ALU.mult)
                        mi[("qgm", p, g, d)] = E.n
                        mi[("qgfree", p, g, d + 1)] = E.n
                    mi[("qgfree", p, g, 0)] = mi[("gcast", p, g)]
                    E.wait(tsem, mi[("r2", p, g, 3)])
                    E.tensor_copy(out=scol[:], in_=ps_st[:])
                    gs = slice(4 * g, 4 * g + 4)
                    E.tensor_tensor(out=mucol[:], in0=scol[:, 0:4], in1=ni_s[:, gs], op=ALU.mult)
                    mi[("muc", p, g)] = E.n
                    E.tensor_tensor(out=vrcol[:], in0=scol[:, 4:8], in1=ni_s[:, gs], op=ALU.mult)
                    E.wait(ssem, mi[("sqmu", p, g)])
                    E.tensor_tensor(out=vrcol[:], in0=vrcol[:], in1=sqmu4[:], op=ALU.subtract)
                    E.tensor_scalar(out=vrcol[:], in0=vrcol[:], scalar1=1e-4, scalar2=None,
                                    op0=ALU.max)
                    mi[("varc", p, g)] = E.n
                    E.wait(ssem, mi[("sig", p, g)])
                    E.tensor_tensor(out=g0col[:], in0=sgcol[:], in1=gn_s[:, gs], op=ALU.mult)
                    E.tensor_tensor(out=thcol[:], in0=sgcol[:], in1=zt_s[:, gs], op=ALU.mult)
                    E.tensor_tensor(out=thcol[:], in0=thcol[:], in1=mucol[:], op=ALU.add)
                    E.memset(erra[:], 0.0)
                    E.memset(dfrz[:], 0.0)
                    mi[("nwinit", p, g)] = E.n
                    for it in range(nit):
                        for d in range(4):
                            E.tensor_scalar(out=junk[:, 0:(g + 1) * 512],
                                            in0=srow[d][:, 0:(g + 1) * 512],
                                            scalar1=thcol[:, d:d + 1], scalar2=0.0,
                                            op0=ALU.is_ge, op1=ALU.add,
                                            accum_out=cnt[:, d:d + 1])
                        ecur, eprv = (errb, erra) if it % 2 == 0 else (erra, errb)
                        E.tensor_scalar(out=ecur[:], in0=cnt[:], scalar1=-float(K_TOP),
                                        scalar2=None, op0=ALU.add)
                        E.tensor_scalar(out=scr4[:], in0=ecur[:], scalar1=0.0, scalar2=None,
                                        op0=ALU.is_equal)
                        E.tensor_tensor(out=dfrz[:], in0=dfrz[:], in1=scr4[:], op=ALU.max)
                        E.tensor_scalar(out=scr4[:], in0=dfrz[:], scalar1=-1.0, scalar2=1.0,
                                        op0=ALU.mult, op1=ALU.add)
                        E.tensor_tensor(out=scr4b[:], in0=ecur[:], in1=g0col[:], op=ALU.mult)
                        E.tensor_scalar(out=scr4b[:], in0=scr4b[:],
                                        scalar1=float(GAMMA_SEL[it]), scalar2=None, op0=ALU.mult)
                        E.tensor_tensor(out=scr4b[:], in0=scr4b[:], in1=scr4[:], op=ALU.mult)
                        E.tensor_tensor(out=thcol[:], in0=thcol[:], in1=scr4b[:], op=ALU.add)
                        mi[("nwit", p, g, it)] = E.n
                    if g == 0:
                        for d in range(4):
                            E.reduce_max(out=rmx[:, d:d + 1], in_=srow[d][:, 0:512], axis=AX.X)
                        E.tensor_scalar(out=rmx[:], in0=rmx[:], scalar1=-700.0, scalar2=None,
                                        op0=ALU.add)
                        E.tensor_tensor(out=rmx[:], in0=rmx[:], in1=ka4[:], op=ALU.mult)
                        E.tensor_scalar(out=scr4[:], in0=ka4[:], scalar1=-1.0, scalar2=1.0,
                                        op0=ALU.mult, op1=ALU.add)
                        E.tensor_tensor(out=thcol[:], in0=thcol[:], in1=scr4[:], op=ALU.mult)
                        E.tensor_tensor(out=thcol[:], in0=thcol[:], in1=rmx[:], op=ALU.add)
                    mi[("thdone", p, g)] = E.n
                    if cut == 2:
                        E.tensor_copy(out=outb[g % 2][:], in_=srow[0][:, 0:512])
                        mi[("outd", p, g)] = E.n
                        continue
                    if cut == 1:
                        E.tensor_copy(out=outb[g % 2][:, 0:4], in_=thcol[:])
                        E.tensor_copy(out=outb[g % 2][:, 4:512], in_=srow[1][:, 4:512])
                        mi[("outd", p, g)] = E.n
                        continue
                    E.memset(throw[:], 0.0)
                    for d in range(4):
                        E.wait(tsem, mi[("thT", p, g, d)])
                        E.tensor_scalar(out=throw[0:1, d * 128:(d + 1) * 128],
                                        in0=ps_row[0:1, 0:128],
                                        scalar1=1.0, scalar2=None, op0=ALU.mult)
                        mi[("thsg", p, g, d)] = E.n
                    mi[("throw_asm", p, g)] = E.n
                    E.wait(tsem, mi[("thbc", p, g)])
                    E.tensor_copy(out=thb[:], in_=ps_zb[:])
                    mi[("thbev", p, g)] = E.n
                    if cut == 4:
                        E.tensor_copy(out=outb[g % 2][:], in_=thb[:])
                        mi[("outd", p, g)] = E.n
                        continue
                    njt = 4 * g + 4
                    for j0 in range(0, njt, 2):
                        buf = (j0 // 2) % 2
                        if j0 >= 4:
                            E.wait(ssem, mi[("exp", p, g, j0 - 4)])
                        for jj in range(min(2, njt - j0)):
                            j = j0 + jj
                            E.wait(tsem, mi[("stmm", p, g, j)])
                            E.tensor_tensor(out=argt[buf][:, jj * 512:(jj + 1) * 512],
                                            in0=ps_s[:, jj * 512:(jj + 1) * 512],
                                            in1=thb[:], op=ALU.subtract)
                            if j >= 4 * g:
                                dd = j - 4 * g
                                E.tensor_tensor(out=argt[buf][:, jj * 512:(jj + 1) * 512],
                                                in0=argt[buf][:, jj * 512:(jj + 1) * 512],
                                                in1=cm2_s[:, dd * 512:(dd + 1) * 512], op=ALU.add)
                            mi[("argev", p, g, j)] = E.n
                            E.tensor_scalar(out=mt[buf][:, jj * 512:(jj + 1) * 512],
                                            in0=argt[buf][:, jj * 512:(jj + 1) * 512],
                                            scalar1=0.0, scalar2=-BIGM,
                                            op0=ALU.is_lt, op1=ALU.mult)
                            E.tensor_tensor(out=argt[buf][:, jj * 512:(jj + 1) * 512],
                                            in0=argt[buf][:, jj * 512:(jj + 1) * 512],
                                            in1=mt[buf][:, jj * 512:(jj + 1) * 512], op=ALU.add)
                            mi[("arg2j", p, g, j)] = E.n
                        mi[("arg2", p, g, j0)] = E.n
                    if cut == 5:
                        E.wait(ssem, mi[("exp", p, g, ((njt + 1) // 2 - 1) * 2)])
                        E.tensor_copy(out=outb[g % 2][:], in_=pt[0][:, 0:512])
                        mi[("outd", p, g)] = E.n
                        continue
                    if cut == 6:
                        E.wait(tsem, mi[("pvz", p, g, ((njt + 1) // 2 - 1) * 2)])
                        E.tensor_copy(out=outb[g % 2][:], in_=ps_pv[:])
                        mi[("outd", p, g)] = E.n
                        continue
                    E.wait(tsem, mi[("pvz", p, g, ((njt + 1) // 2 - 1) * 2)])
                    E.memset(throw[:], 0.0)
                    E.tensor_scalar(out=throw[0:1, :], in0=ps_z[0:1, :], scalar1=1.0,
                                    scalar2=None, op0=ALU.mult)
                    E.reciprocal(out=throw[0:1, :], in_=throw[0:1, :])
                    mi[("zrows", p, g)] = E.n
                    E.wait(tsem, mi[("zbc", p, g)])
                    E.tensor_copy(out=zb[:], in_=ps_zb[:])
                    nst = 4 * p + g // 2
                    if nst >= 1 and not (p == 0 and g < 2):
                        E.wait(stq[g % 2], 16 * nst)
                    if cut == 7:
                        E.tensor_copy(out=outb[g % 2][:], in_=zb[:])
                    else:
                        E.tensor_copy(out=mt[0][:, 0:512], in_=ps_pv[:])
                        E.tensor_tensor(out=outb[g % 2][:], in0=mt[0][:, 0:512], in1=zb[:],
                                        op=ALU.mult)
                    mi[("outd", p, g)] = E.n

        def set_aliases():
            for p in range(npairs):
                for g in range(NG):
                    if p == 0 and g == 0:
                        mi[("gfree", p, g)] = 0
                    elif g == 0:
                        mi[("gfree", p, g)] = mi[("qgm", p - 1, NG - 1, 3)]
                    else:
                        mi[("gfree", p, g)] = mi[("qgm", p, g - 1, 3)]

        def emit_sync(e):
            e.dma_start(out=cm_s[:], in_=cmaskd).then_inc(ldc, 16)
            e.dma_start(out=cm2_s[:], in_=cmask2d).then_inc(ldc, 16)
            e.dma_start(out=zt_s[:], in_=ztgt_d).then_inc(ldc, 16)
            e.dma_start(out=gn_s[:], in_=gain_d).then_inc(ldc, 16)
            e.dma_start(out=ni_s[:], in_=ninv_d).then_inc(ldc, 16)
            e.dma_start(out=ka4[:], in_=kamask_d).then_inc(ldc, 16)
            e.dma_start(out=ones1b[:], in_=ones128b_d).then_inc(ldc, 16)
            e.dma_start(out=ok2f[:], in_=onesk2f_d).then_inc(ldc, 16)
            e.dma_start(out=i128f[:], in_=i128f_d).then_inc(ldc, 16)
            for p in range(min(2, npairs)):
                e.dma_start(out=qb[p][:], in_=qh[p]).then_inc(lq[p], 16)
            e.dma_start(out=kb[:], in_=kh[0]).then_inc(lkv, 16)
            e.dma_start(out=ktk[:], in_=ktokp[0]).then_inc(lkv, 16)
            e.dma_start(out=vtk[:], in_=vtokp[0]).then_inc(lkv, 16)
            for p in range(npairs):
                for g in range(NG):
                    e.wait_ge(vsem, mi[("outd", p, g)])
                    e.dma_start(out=o_att[p * 128:(p + 1) * 128, g * 512:(g + 1) * 512],
                                in_=outb[g % 2][:]).then_inc(stq[g % 2], 16)
                if p + 2 < npairs:
                    e.wait_ge(tsem, mi[("pairdone", p)])
                    e.dma_start(out=qb[p % 2][:], in_=qh[p + 2]).then_inc(lq[p % 2], 16)
                if p % 4 == 3 and p + 1 < npairs:
                    e.wait_ge(tsem, mi[("pairdone", p)])
                    kv = (p + 1) // 4
                    e.dma_start(out=kb[:], in_=kh[kv]).then_inc(lkv, 16)
                    e.dma_start(out=ktk[:], in_=ktokp[kv]).then_inc(lkv, 16)
                    e.dma_start(out=vtk[:], in_=vtokp[kv]).then_inc(lkv, 16)
            for i in range(2):
                tot = sum(1 for p in range(npairs) for g in range(NG) if g % 2 == i)
                e.wait_ge(stq[i], 16 * tot)

        for _ in range(2):
            emit_tensor(FakeCEng()); emit_scalar(FakeCEng()); emit_vector(FakeCEng())
            set_aliases()
        mi.strict = True

        @block.sync
        def _(e):
            emit_sync(e)

        @block.scalar
        def _(e):
            emit_scalar(CEng(e, ssem, auto=True))

        @block.tensor
        def _(e):
            emit_tensor(CEng(e, tsem))

        @block.vector
        def _(e):
            emit_vector(CEng(e, vsem, auto=True))
    return nc


def _ndtri(p):
    from statistics import NormalDist
    return NormalDist().inv_cdf(p)


def host_consts_B():
    cm = np.zeros((128, 4, 512), np.float32)
    cm2 = np.zeros((128, 4, 512), np.float32)
    for dloc in range(4):
        r = np.arange(128)[:, None]
        j = np.arange(512)[None, :]
        cm[:, dloc, :] = np.where(j > 128 * dloc + r, NEGF, 0.0)
        cm2[:, dloc, :] = np.where(j < 128 * dloc + r, NEGF, 0.0)
    ztgt = np.zeros((128, 32), np.float32)
    gain = np.zeros((128, 32), np.float32)
    ninv = np.zeros((128, 32), np.float32)
    for qt in range(32):
        for r in range(128):
            n = qt * 128 + r + 1
            ninv[r, qt] = 1.0 / n
            if n > K_TOP:
                z = _ndtri(1.0 - (K_TOP - 0.5) / n)
                ztgt[r, qt] = z
                phi = math.exp(-0.5 * z * z) / math.sqrt(2 * math.pi)
                gain[r, qt] = 1.0 / (n * phi)
            else:
                ztgt[r, qt] = 0.0
                gain[r, qt] = 0.0
    kam = np.zeros((128, 4), np.float32)
    for d in range(4):
        q = d * 128 + np.arange(128)
        kam[:, d] = (q < K_TOP).astype(np.float32)
    ones128b = np.ones((128, 1), np.float32).astype(ml_dtypes.bfloat16)
    onesk2f = np.zeros((2, 128), np.float32); onesk2f[0] = 1.0
    i128f = np.eye(128, dtype=np.float32)
    return {
        "cmaskd": np.ascontiguousarray(cm.reshape(128, 4 * 512)),
        "cmask2d": np.ascontiguousarray(cm2.reshape(128, 4 * 512)),
        "ztgt_d": ztgt, "gain_d": gain, "ninv_d": ninv, "kamask_d": kam,
        "ones128b_d": ones128b, "onesk2f_d": onesk2f, "i128f_d": i128f,
    }


def _tok_tiles(a):
    return np.ascontiguousarray(a.reshape(32, 128, 128).transpose(1, 0, 2).reshape(128, 4096))


def host_prep_B(qkv_cores):
    consts = host_consts_B()
    qf = [np.concatenate([qkv_cores[4 * b + i][:4096] for i in range(4)], axis=1)
          for b in range(B)]
    kf = [np.concatenate([qkv_cores[4 * b + i][4096:5120] for i in range(4)], axis=1)
          for b in range(B)]
    vf = [np.concatenate([qkv_cores[4 * b + i][5120:6144] for i in range(4)], axis=1)
          for b in range(B)]
    in_maps = []
    for c in range(NCORES):
        b, hs = c // 4, (c % 4) * 8
        kvs = hs // 4
        qh = np.ascontiguousarray(qf[b].reshape(32, 128, S)[hs:hs + 8])
        kh = np.ascontiguousarray(kf[b].reshape(8, 128, S)[kvs:kvs + 2])
        ktokp = np.stack([_tok_tiles(kh[i].T) for i in range(2)])
        vh = vf[b].reshape(8, 128, S)[kvs:kvs + 2]
        vtokp = np.stack([_tok_tiles(vh[i].T) for i in range(2)])
        m = dict(consts)
        m.update({"qh": qh, "kh": kh,
                  "ktokp": np.ascontiguousarray(ktokp),
                  "vtokp": np.ascontiguousarray(vtokp)})
        in_maps.append(m)
    return in_maps



def build_launch_C1():
    nc = bass.Bass("TRN2", target_bir_lowering=False)
    hT = nc.dram_tensor("hT", [D, TC], BF16, kind="ExternalInput").ap()
    x0f = nc.dram_tensor("x0f", [D, TC], F32, kind="ExternalInput").ap()
    wwo = nc.dram_tensor("wwo", [32, 128, D], F32, kind="ExternalInput").ap()
    o_x1p = nc.dram_tensor("o_x1p", [D, TC], F32, kind="ExternalOutput").ap()
    NM = 32
    with ExitStack() as ctx:
        sbt = lambda name, shape, dt=F32: ctx.enter_context(nc.sbuf_tensor(name, shape, dt))
        hb = sbt("hb", [128, NKT * TC], BF16)
        wst = sbt("wst", [128, D])
        wbf = [sbt(f"wbf{i}", [128, D], BF16) for i in range(2)]
        x0t = [sbt(f"x0t{i}", [128, TC]) for i in range(2)]
        x1f = [sbt(f"x1f{i}", [128, TC]) for i in range(2)]
        ps_o = [ctx.enter_context(nc.psum_tensor(f"ps_o{i}", [128, TC], F32)) for i in range(2)]
        sem = lambda name: ctx.enter_context(nc.semaphore(name))
        ldc = sem("ldc"); lw = sem("lw"); lx = [sem("lxa"), sem("lxb")]
        st1 = [sem("st1a"), sem("st1b")]
        vsem = sem("vsem"); ssem = sem("ssem"); tsem = sem("tsem")
        block = ctx.enter_context(nc.Block())
        mi = Mi()

        def emit_scalar(E):
            for m in range(NM):
                E.wait(lw, 16 * (m + 1))
                if m >= 2:
                    E.wait(tsem, mi[("mm", m - 1)])
                E.activation(out=wbf[m % 2][:], in_=wst[:], func=AF.Identity)
                mi[("wcast", m)] = E.n

        def emit_tensor(E):
            E.wait(ldc, 16 * NKT)
            for m in range(NM):
                E.wait(ssem, mi[("wcast", m)])
                if m >= 2:
                    E.wait(vsem, mi[("x1", m - 2)])
                pso = ps_o[m % 2]
                for h in range(2):
                    for kt in range(NKT):
                        last = (h == 1 and kt == NKT - 1)
                        f = E.matmul if last else E.un("matmul")
                        f(pso[:, h * 512:(h + 1) * 512],
                          wbf[m % 2][:, kt * 128:(kt + 1) * 128],
                          hb[:, kt * TC + h * 512:kt * TC + (h + 1) * 512],
                          start=(kt == 0), stop=(kt == NKT - 1))
                mi[("mm", m)] = E.n

        def emit_vector(E):
            for m in range(NM):
                E.wait(tsem, mi[("mm", m)])
                E.wait(lx[m % 2], 16 * (m // 2 + 1))
                if m >= 2:
                    E.wait(st1[m % 2], 16 * (m // 2))
                E.tensor_tensor(out=x1f[m % 2][:], in0=ps_o[m % 2][:], in1=x0t[m % 2][:],
                                op=ALU.add)
                mi[("x1", m)] = E.n

        def emit_sync(e):
            for i in range(NKT):
                e.dma_start(out=hb[:, i * TC:(i + 1) * TC],
                            in_=hT[i * 128:(i + 1) * 128, :]).then_inc(ldc, 16)
            e.dma_start(out=wst[:], in_=wwo[0]).then_inc(lw, 16)
            for m in range(NM):
                if m + 1 < NM:
                    e.wait_ge(ssem, mi[("wcast", m)])
                    e.dma_start(out=wst[:], in_=wwo[m + 1]).then_inc(lw, 16)
                if m >= 2:
                    e.wait_ge(vsem, mi[("x1", m - 2)])
                e.dma_start(out=x0t[m % 2][:], in_=x0f[m * 128:(m + 1) * 128, :]).then_inc(lx[m % 2], 16)
                e.wait_ge(vsem, mi[("x1", m)])
                e.dma_start(out=o_x1p[m * 128:(m + 1) * 128, :],
                            in_=x1f[m % 2][:]).then_inc(st1[m % 2], 16)
            e.wait_ge(st1[0], 16 * 16); e.wait_ge(st1[1], 16 * 16)

        for _ in range(2):
            emit_scalar(FakeCEng()); emit_tensor(FakeCEng()); emit_vector(FakeCEng())
        mi.strict = True

        @block.sync
        def _(e):
            emit_sync(e)

        @block.scalar
        def _(e):
            emit_scalar(CEng(e, ssem, auto=True))

        @block.tensor
        def _(e):
            emit_tensor(CEng(e, tsem))

        @block.vector
        def _(e):
            emit_vector(CEng(e, vsem, auto=True))
    return nc


def host_prep_C1(inputs, att_cores):
    x0 = inputs["L0_attn_hidden_states"]
    xT = np.ascontiguousarray(x0.reshape(B * S, D).T).astype(np.float32)
    wwo = _w_prep_lhsT(np.ascontiguousarray(inputs["L0_attn_Wo"].T).astype(np.float32))
    in_maps = []
    for c in range(NCORES):
        b, sl = c // 4, slice((c % 4) * TC, (c % 4) * TC + TC)
        hT = np.concatenate([att_cores[4 * b + i][:, sl] for i in range(4)], axis=0)
        in_maps.append({
            "hT": np.ascontiguousarray(hT),
            "x0f": np.ascontiguousarray(xT[:, c * TC:(c + 1) * TC]),
            "wwo": wwo,
        })
    return in_maps



def build_launch_C2():
    nc = bass.Bass("TRN2", target_bir_lowering=False)
    xn0 = nc.dram_tensor("xn0", [D, TC], BF16, kind="ExternalInput").ap()
    x1p = nc.dram_tensor("x1p", [D, TC], F32, kind="ExternalInput").ap()
    wkr = nc.dram_tensor("wkr", [32, 128, D], F32, kind="ExternalInput").ap()
    aeff_d = nc.dram_tensor("aeff_d", [16, 128, 2048], F32, kind="ExternalInput").ap()
    ashare_d = nc.dram_tensor("ashare_d", [128, 2048], F32, kind="ExternalInput").ap()
    beff_d = nc.dram_tensor("beff_d", [16, 32, D], F32, kind="ExternalInput").ap()
    bshare_d = nc.dram_tensor("bshare_d", [32, D], F32, kind="ExternalInput").ap()
    gate_d = nc.dram_tensor("gate_d", [128, 16], F32, kind="ExternalInput").ap()
    bias_d = nc.dram_tensor("bias_d", [128, 32], F32, kind="ExternalInput").ap()
    onescol = nc.dram_tensor("onescol", [128, 1], F32, kind="ExternalInput").ap()
    onesk2 = nc.dram_tensor("onesk2", [2, 128], F32, kind="ExternalInput").ap()
    o_x1 = nc.dram_tensor("o_x1", [D, TC], F32, kind="ExternalOutput").ap()
    o_xn1 = nc.dram_tensor("o_xn1", [D, TC], BF16, kind="ExternalOutput").ap()
    NM = 32
    with ExitStack() as ctx:
        sbt = lambda name, shape, dt=F32: ctx.enter_context(nc.sbuf_tensor(name, shape, dt))
        xnb = sbt("xnb", [128, NKT * TC], BF16)
        wst = sbt("wst", [128, D])
        wbf = [sbt(f"wbf{i}", [128, D], BF16) for i in range(2)]
        aeff = sbt("aeff", [128, 2048]); aeb = sbt("aeb", [128, 2048], BF16)
        xnout_t = aeb
        astg = sbt("astg", [128, 2048])
        beff = sbt("beff", [64, D]); beb = sbt("beb", [64, D], BF16)
        bstg = sbt("bstg", [32, D])
        gtv = sbt("gtv", [128, 16]); biasl = sbt("biasl", [128, 32])
        axb = sbt("axb", [64, TC], BF16)
        x1pt = [sbt(f"x1pt{i}", [128, TC]) for i in range(2)]
        x1f = [sbt(f"x1f{i}", [128, TC]) for i in range(2)]
        sq = sbt("sq", [128, TC])
        murow = sbt("murow", [1, TC]); varrow = sbt("varrow", [1, TC])
        sqmu = sbt("sqmu", [1, TC]); rstdrow = sbt("rstdrow", [1, TC])
        rows2 = sbt("rows2", [2, TC])
        mub = sbt("mub", [128, TC]); rstdb = sbt("rstdb", [128, TC])
        onec = sbt("onec", [128, 1]); ones2 = sbt("ones2", [2, 128])
        tmpa = sq
        xnout = [xnout_t[:, 0:TC], xnout_t[:, TC:2 * TC]]
        ps_o = [ctx.enter_context(nc.psum_tensor(f"ps_o{i}", [128, TC], F32)) for i in range(2)]
        ps_ax = ctx.enter_context(nc.psum_tensor("ps_ax", [64, TC], F32))
        ps_b = ctx.enter_context(nc.psum_tensor("ps_b", [128, TC], F32))
        ps_mu = ps_b[0:1, :]
        ps_sq = ps_ax[0:1, :]
        sem = lambda name: ctx.enter_context(nc.semaphore(name))
        ldc = sem("ldc"); lw = sem("lw"); lxn = sem("lxn"); la = sem("la"); lb2 = sem("lb2")
        lx = [sem("lxa"), sem("lxb")]; l2 = [sem("l2a"), sem("l2b")]
        st1 = [sem("st1a"), sem("st1b")]; st2 = [sem("st2a"), sem("st2b")]
        vsem = sem("vsem"); ssem = sem("ssem"); tsem = sem("tsem")
        block = ctx.enter_context(nc.Block())
        mi = Mi()

        def emit_scalar(E):
            def do_sq(m):
                E.wait(vsem, mi[("x1", m)])
                E.wait(tsem, mi[("stm", m - 1)] if m >= 1 else 0)
                E.activation(out=sq[:], in_=x1f[m % 2][:], func=AF.Square)
                mi[("sq", m)] = E.n
            for m in range(NM):
                E.wait(lw, 16 * (m + 1))
                if m >= 2:
                    E.wait(tsem, mi[("mm", m - 1)])
                E.activation(out=wbf[m % 2][:], in_=wst[:], func=AF.Identity)
                mi[("wcast", m)] = E.n
                if m >= 2:
                    do_sq(m - 2)
            do_sq(NM - 2); do_sq(NM - 1)
            E.wait(vsem, mi["murow"])
            E.activation(out=sqmu[:], in_=murow[:], func=AF.Square)
            mi["sqmu"] = E.n
            E.wait(vsem, mi["varrow"])
            E.activation(out=varrow[:], in_=varrow[:], func=AF.Ln)
            E.activation(out=rstdrow[:], in_=varrow[:], func=AF.Exp, scale=-0.5)
            mi["rstd"] = E.n

        def emit_tensor(E):
            E.wait(ldc, 16 * 4)
            E.wait(lxn, 16 * NKT)
            E.wait(vsem, mi["aeff"])
            for kt in range(NKT):
                for h in range(2):
                    last = (kt == NKT - 1 and h == 1)
                    f = E.matmul if last else E.un("matmul")
                    f(ps_ax[:, h * 512:(h + 1) * 512], aeb[:, kt * 64:(kt + 1) * 64],
                      xnb[:, kt * TC + h * 512:kt * TC + (h + 1) * 512],
                      start=(kt == 0), stop=(kt == NKT - 1))
            mi["axmm"] = E.n
            E.wait(vsem, mi["beff"])
            E.wait(vsem, mi["axb"])
            def do_stats(m):
                E.wait(vsem, mi[("x1", m)])
                E.un("matmul")(ps_mu[:, 0:512], onec[:], x1f[m % 2][:, 0:512],
                               start=(m == 0), stop=(m == NM - 1))
                E.matmul(ps_mu[:, 512:1024], onec[:], x1f[m % 2][:, 512:1024],
                         start=(m == 0), stop=(m == NM - 1))
                if m == NM - 1:
                    mi["mumm"] = E.n
                E.wait(ssem, mi[("sq", m)])
                E.un("matmul")(ps_sq[:, 0:512], onec[:], sq[:, 0:512],
                               start=(m == 0), stop=(m == NM - 1))
                E.matmul(ps_sq[:, 512:1024], onec[:], sq[:, 512:1024],
                         start=(m == 0), stop=(m == NM - 1))
                mi[("stm", m)] = E.n
                if m == NM - 1:
                    mi["sqmm"] = E.n
            for m in range(NM):
                E.wait(ssem, mi[("wcast", m)])
                if m >= 2:
                    E.wait(vsem, mi[("x1", m - 2)])
                pso = ps_o[m % 2]
                for h in range(2):
                    for kt in range(NKT):
                        E.un("matmul")(pso[:, h * 512:(h + 1) * 512],
                                       wbf[m % 2][:, kt * 128:(kt + 1) * 128],
                                       xnb[:, kt * TC + h * 512:kt * TC + (h + 1) * 512],
                                       start=(kt == 0), stop=False)
                    last = (h == 1)
                    f = E.matmul if last else E.un("matmul")
                    f(pso[:, h * 512:(h + 1) * 512], beb[:, m * 128:(m + 1) * 128],
                      axb[:, h * 512:(h + 1) * 512], start=False, stop=True)
                mi[("mm", m)] = E.n
                if m >= 1:
                    do_stats(m - 1)
            do_stats(NM - 1)
            E.wait(vsem, mi["negmu"])
            E.un("matmul")(ps_b[:, 0:512], ones2[:], rows2[:, 0:512], start=True, stop=True)
            E.matmul(ps_b[:, 512:1024], ones2[:], rows2[:, 512:1024], start=True, stop=True)
            mi["bc_mu"] = E.n
            E.wait(vsem, mi["rows_rstd"])
            E.un("matmul")(ps_b[:, 0:512], ones2[:], rows2[:, 0:512], start=True, stop=True)
            E.matmul(ps_b[:, 512:1024], ones2[:], rows2[:, 512:1024], start=True, stop=True)
            mi["bc_rstd"] = E.n

        def emit_vector(E):
            E.wait(ldc, 16 * 4)
            E.memset(aeff[:], 0.0)
            E.memset(beff[:], 0.0)
            for n in range(16):
                E.wait(la, 16 * (n + 1))
                op1 = ALU.bypass if n == 0 else ALU.add
                E.scalar_tensor_tensor(out=aeff[:], in0=astg[:], scalar=gtv[:, n:n + 1],
                                       in1=aeff[:], op0=ALU.mult, op1=op1)
                mi[("an", n)] = E.n
            E.wait(la, 16 * 17)
            E.tensor_tensor(out=aeff[:], in0=aeff[:], in1=astg[:], op=ALU.add)
            E.tensor_copy(out=aeb[:], in_=aeff[:])
            mi["aeff"] = E.n
            for n in range(16):
                E.wait(lb2, 16 * (n + 1))
                op1 = ALU.bypass if n == 0 else ALU.add
                E.scalar_tensor_tensor(out=beff[0:32, :], in0=bstg[:], scalar=gtv[0:32, n:n + 1],
                                       in1=beff[0:32, :], op0=ALU.mult, op1=op1)
                mi[("bn", n)] = E.n
            E.wait(lb2, 16 * 17)
            E.tensor_copy(out=beff[32:64, :], in_=bstg[:])
            E.tensor_copy(out=beb[:], in_=beff[:])
            mi["beff"] = E.n
            E.wait(tsem, mi["axmm"])
            E.tensor_copy(out=axb[:], in_=ps_ax[:])
            mi["axb"] = E.n
            for m in range(NM):
                E.wait(tsem, mi[("mm", m)])
                E.wait(lx[m % 2], 16 * (m // 2 + 1))
                if m >= 2:
                    E.wait(st1[m % 2], 16 * (m // 2))
                    E.wait(ssem, mi[("sq", m - 2)])
                    E.wait(tsem, mi[("stm", m - 2)])
                E.tensor_scalar(out=x1f[m % 2][:], in0=ps_o[m % 2][:], scalar1=biasl[:, m:m + 1],
                                scalar2=None, op0=ALU.add)
                E.tensor_tensor(out=x1f[m % 2][:], in0=x1f[m % 2][:], in1=x1pt[m % 2][:],
                                op=ALU.add)
                mi[("x1", m)] = E.n
            E.wait(tsem, mi["mumm"])
            E.tensor_scalar(out=murow[:], in0=ps_mu[:], scalar1=1.0 / D, scalar2=None, op0=ALU.mult)
            mi["murow"] = E.n
            E.wait(ssem, mi["sqmu"])
            E.wait(tsem, mi["sqmm"])
            E.tensor_scalar(out=varrow[:], in0=ps_sq[:], scalar1=1.0 / D, scalar2=None, op0=ALU.mult)
            E.tensor_tensor(out=varrow[:], in0=varrow[:], in1=sqmu[:], op=ALU.subtract)
            E.tensor_scalar(out=varrow[:], in0=varrow[:], scalar1=1e-5, scalar2=None, op0=ALU.add)
            mi["varrow"] = E.n
            E.memset(rows2[:], 0.0)
            E.tensor_scalar(out=rows2[0:1, :], in0=murow[:], scalar1=-1.0, scalar2=None, op0=ALU.mult)
            mi["negmu"] = E.n
            E.wait(tsem, mi["bc_mu"])
            E.tensor_copy(out=mub[:], in_=ps_b[:])
            mi["mub"] = E.n
            E.wait(ssem, mi["rstd"])
            E.tensor_copy(out=rows2[0:1, :], in_=rstdrow[:])
            mi["rows_rstd"] = E.n
            E.wait(tsem, mi["bc_rstd"])
            E.tensor_copy(out=rstdb[:], in_=ps_b[:])
            mi["stats_done"] = E.n
            for m in range(NM):
                E.wait(l2[m % 2], 16 * (m // 2 + 1))
                if m >= 2:
                    E.wait(st2[m % 2], 16 * (m // 2))
                E.tensor_tensor(out=tmpa[:], in0=x1pt[m % 2][:], in1=mub[:], op=ALU.add)
                E.tensor_tensor(out=xnout[m % 2], in0=tmpa[:], in1=rstdb[:], op=ALU.mult)
                mi[("xn1", m)] = E.n

        def emit_sync(e):
            e.dma_start(out=onec[:], in_=onescol).then_inc(ldc, 16)
            e.dma_start(out=ones2[:], in_=onesk2).then_inc(ldc, 16)
            e.dma_start(out=gtv[:], in_=gate_d).then_inc(ldc, 16)
            e.dma_start(out=biasl[:], in_=bias_d).then_inc(ldc, 16)
            for i in range(NKT):
                e.dma_start(out=xnb[:, i * TC:(i + 1) * TC],
                            in_=xn0[i * 128:(i + 1) * 128, :]).then_inc(lxn, 16)
            for n in range(16):
                if n >= 1:
                    e.wait_ge(vsem, mi[("an", n - 1)])
                e.dma_start(out=astg[:], in_=aeff_d[n]).then_inc(la, 16)
            e.wait_ge(vsem, mi[("an", 15)])
            e.dma_start(out=astg[:], in_=ashare_d).then_inc(la, 16)
            for n in range(16):
                e.wait_ge(vsem, mi[("bn", n - 1)] if n >= 1 else mi["aeff"])
                e.dma_start(out=bstg[:], in_=beff_d[n]).then_inc(lb2, 16)
            e.wait_ge(vsem, mi[("bn", 15)])
            e.dma_start(out=bstg[:], in_=bshare_d).then_inc(lb2, 16)
            e.dma_start(out=wst[:], in_=wkr[0]).then_inc(lw, 16)
            for m in range(NM):
                if m + 1 < NM:
                    e.wait_ge(ssem, mi[("wcast", m)])
                    e.dma_start(out=wst[:], in_=wkr[m + 1]).then_inc(lw, 16)
                e.wait_ge(vsem, mi[("x1", m - 1)] if m >= 1 else 0)
                e.dma_start(out=x1pt[m % 2][:], in_=x1p[m * 128:(m + 1) * 128, :]).then_inc(lx[m % 2], 16)
                e.wait_ge(vsem, mi[("x1", m)])
                e.dma_start(out=o_x1[m * 128:(m + 1) * 128, :],
                            in_=x1f[m % 2][:]).then_inc(st1[m % 2], 16)
            e.wait_ge(vsem, mi["stats_done"])
            e.wait_ge(st1[0], 16 * 16); e.wait_ge(st1[1], 16 * 16)
            for m in range(NM + 2):
                if m < NM:
                    if m >= 2:
                        e.wait_ge(vsem, mi[("xn1", m - 2)])
                    e.dma_start(out=x1pt[m % 2][:], in_=o_x1[m * 128:(m + 1) * 128, :]).then_inc(l2[m % 2], 16)
                if m >= 2:
                    sm = m - 2
                    e.wait_ge(vsem, mi[("xn1", sm)])
                    e.dma_start(out=o_xn1[sm * 128:(sm + 1) * 128, :],
                                in_=xnout[sm % 2]).then_inc(st2[sm % 2], 16)
            e.wait_ge(st1[0], 16 * 16); e.wait_ge(st1[1], 16 * 16)
            e.wait_ge(st2[0], 16 * 16); e.wait_ge(st2[1], 16 * 16)

        for _ in range(2):
            emit_scalar(FakeCEng()); emit_tensor(FakeCEng()); emit_vector(FakeCEng())
        mi.strict = True

        @block.sync
        def _(e):
            emit_sync(e)

        @block.scalar
        def _(e):
            emit_scalar(CEng(e, ssem, auto=True))

        @block.tensor
        def _(e):
            emit_tensor(CEng(e, tsem))

        @block.vector
        def _(e):
            emit_vector(CEng(e, vsem, auto=True))
    return nc


def host_prep_C2(inputs, x1p_cores, xn_cores):
    kern = inputs["L0_ffn_kernel"].astype(np.float32)
    wkr = _w_prep_lhsT(kern)
    lora_A = inputs["L0_ffn_lora_A"].astype(np.float32)
    lora_B = inputs["L0_ffn_lora_B"].astype(np.float32)
    ash = inputs["L0_ffn_lora_A_shared"].astype(np.float32)
    bsh = inputs["L0_ffn_lora_B_shared"].astype(np.float32)
    ae = np.zeros((16, 128, NKT, 64), np.float32)
    la = lora_A.transpose(0, 2, 1).reshape(16, NKT, 128, 32)
    ae[:, :, :, 0:32] = la.transpose(0, 2, 1, 3)
    aeff_d = np.ascontiguousarray(ae.reshape(16, 128, NKT * 64))
    ash_t = np.zeros((128, NKT, 64), np.float32)
    ash_t[:, :, 32:64] = ash.T.reshape(NKT, 128, 32).transpose(1, 0, 2)
    ashare_d = np.ascontiguousarray(ash_t.reshape(128, NKT * 64))
    beff_d = np.ascontiguousarray(lora_B.transpose(0, 2, 1))
    bshare_d = np.ascontiguousarray(bsh.T)
    gate = inputs["L0_ffn_gate"].astype(np.float32)
    bias = inputs["L0_ffn_bias"].astype(np.float32)
    bias_d = np.ascontiguousarray(bias.reshape(32, 128).T)
    onescol = np.ones((128, 1), np.float32)
    onesk2 = np.zeros((2, 128), np.float32); onesk2[0] = 1.0
    in_maps = []
    for c in range(NCORES):
        in_maps.append({
            "xn0": xn_cores[c], "x1p": x1p_cores[c],
            "wkr": wkr, "aeff_d": aeff_d, "ashare_d": ashare_d,
            "beff_d": beff_d, "bshare_d": bshare_d,
            "gate_d": np.ascontiguousarray(np.broadcast_to(gate[c], (128, 16)).copy()),
            "bias_d": bias_d, "onescol": onescol, "onesk2": onesk2,
        })
    return in_maps



def build_launch_D():
    nc = bass.Bass("TRN2", target_bir_lowering=False)
    xn1 = nc.dram_tensor("xn1", [D, TC], BF16, kind="ExternalInput").ap()
    xprev = nc.dram_tensor("xprev", [128, NKT], BF16, kind="ExternalInput").ap()
    wkvrg = nc.dram_tensor("wkvrg", [128, 128, D], F32, kind="ExternalInput").ap()
    mixd = nc.dram_tensor("mixd", [128, 128], F32, kind="ExternalInput").ap()
    o_kvrg = nc.dram_tensor("o_kvrg", [4 * D, TC], BF16, kind="ExternalOutput").ap()
    NM = 32
    TH = 512
    with ExitStack() as ctx:
        sbt = lambda name, shape, dt=F32: ctx.enter_context(nc.sbuf_tensor(name, shape, dt))
        xnb = sbt("xnb", [128, NKT * TH], BF16)
        db = sbt("db", [128, NKT * TH], BF16)
        mixw = sbt("mixw", [128, NKT * TH], BF16)
        xpv = sbt("xpv", [128, NKT], BF16)
        mixc = sbt("mixc", [128, 128])
        xstg = [sbt(f"xstg{i}", [128, TH], BF16) for i in range(2)]
        wst = sbt("wst", [128, D])
        wbf = [sbt(f"wbf{i}", [128, D], BF16) for i in range(2)]
        tmp = sbt("tmp", [128, TH])
        outb = [sbt(f"outb{i}", [128, TH], BF16) for i in range(2)]
        ps_o = [ctx.enter_context(nc.psum_tensor(f"ps_o{i}", [128, TH], F32)) for i in range(2)]
        sem = lambda name: ctx.enter_context(nc.semaphore(name))
        ldc = sem("ldc"); lw = sem("lw"); lx = [sem("lxa"), sem("lxb")]
        stq = [sem("stqa"), sem("stqb")]
        vsem = sem("vsem"); ssem = sem("ssem"); tsem = sem("tsem")
        block = ctx.enter_context(nc.Block())
        mi = Mi()

        def emit_vector(E):
            E.wait(ldc, 16 * 2)
            for hf in range(2):
                for kt in range(NKT):
                    i = hf * NKT + kt
                    E.wait(lx[i % 2], 16 * (i // 2 + 1))
                    st = xstg[i % 2]
                    if hf == 0:
                        E.tensor_tensor(out=db[:, kt * TH:kt * TH + 1],
                                        in0=st[:, 0:1], in1=xpv[:, kt:kt + 1], op=ALU.subtract)
                    else:
                        E.tensor_tensor(out=db[:, kt * TH:kt * TH + 1],
                                        in0=st[:, 0:1],
                                        in1=xnb[:, kt * TH + TH - 1:kt * TH + TH],
                                        op=ALU.subtract)
                    E.tensor_tensor(out=db[:, kt * TH + 1:(kt + 1) * TH],
                                    in0=st[:, 1:TH], in1=st[:, 0:TH - 1], op=ALU.subtract)
                    E.tensor_copy(out=xnb[:, kt * TH:(kt + 1) * TH], in_=st[:, 0:TH])
                    mi[("d", hf, kt)] = E.n
                if hf == 1:
                    E.wait(tsem, mi[("mm", 127)])
                for w in range(4):
                    for kt in range(NKT):
                        E.scalar_tensor_tensor(out=mixw[:, kt * TH:(kt + 1) * TH],
                                               in0=db[:, kt * TH:(kt + 1) * TH],
                                               scalar=mixc[:, kt * 4 + w:kt * 4 + w + 1],
                                               in1=xnb[:, kt * TH:(kt + 1) * TH],
                                               op0=ALU.mult, op1=ALU.add)
                    mi[("mix", hf, w)] = E.n
                    if w < 3:
                        for m in range(NM):
                            gm = hf * 128 + w * NM + m
                            E.wait(tsem, mi[("mm", gm)])
                            if gm >= 2:
                                E.wait(stq[gm % 2], 16 * (gm // 2))
                            E.tensor_copy(out=outb[gm % 2][:], in_=ps_o[gm % 2][:])
                            mi[("out", gm)] = E.n

        def emit_scalar(E):
            def outg(gm):
                E.wait(tsem, mi[("mm", gm)])
                if gm >= 2:
                    E.wait(stq[gm % 2], 16 * (gm // 2))
                E.activation(out=outb[gm % 2][:], in_=ps_o[gm % 2][:], func=AF.Silu)
                mi[("outg", gm)] = E.n
            for hf in range(2):
                for w in range(4):
                    for m in range(NM):
                        gm = hf * 128 + w * NM + m
                        E.wait(lw, 16 * (gm + 1))
                        if gm >= 2:
                            E.wait(tsem, mi[("mm", gm - 1)])
                        E.activation(out=wbf[gm % 2][:], in_=wst[:], func=AF.Identity)
                        mi[("wcast", gm)] = E.n
                        if w == 3 and m >= 1:
                            outg(gm - 1)
                outg(hf * 128 + 127)

        def emit_tensor(E):
            E.wait(ldc, 16 * 2)
            for hf in range(2):
                for w in range(4):
                    E.wait(vsem, mi[("mix", hf, w)])
                    for m in range(NM):
                        gm = hf * 128 + w * NM + m
                        E.wait(ssem, mi[("wcast", gm)])
                        if gm >= 2:
                            if (gm - 2) % 128 < 3 * NM:
                                E.wait(vsem, mi[("out", gm - 2)])
                            else:
                                E.wait(ssem, mi[("outg", gm - 2)])
                        pso = ps_o[gm % 2]
                        for kt in range(NKT):
                            last = (kt == NKT - 1)
                            f = E.matmul if last else E.un("matmul")
                            f(pso[:], wbf[gm % 2][:, kt * 128:(kt + 1) * 128],
                              mixw[:, kt * TH:(kt + 1) * TH],
                              start=(kt == 0), stop=(kt == NKT - 1))
                        mi[("mm", gm)] = E.n

        def emit_sync(e):
            e.dma_start(out=mixc[:], in_=mixd).then_inc(ldc, 16)
            e.dma_start(out=xpv[:], in_=xprev).then_inc(ldc, 16)
            e.dma_start(out=wst[:], in_=wkvrg[0]).then_inc(lw, 16)
            def xload(i):
                hf, kt = i // NKT, i % NKT
                if i >= 2:
                    e.wait_ge(vsem, mi[("d", (i - 2) // NKT, (i - 2) % NKT)])
                e.dma_start(out=xstg[i % 2][:, 0:TH],
                            in_=xn1[kt * 128:(kt + 1) * 128, hf * TH:(hf + 1) * TH]).then_inc(lx[i % 2], 16)
            for i in range(NKT + 2):
                xload(i)
            for gm2 in range(256):
                if 96 <= gm2 < 96 + NKT - 2:
                    xload(NKT + 2 + (gm2 - 96))
                if gm2 + 1 < 256:
                    w_idx = (gm2 + 1) % 128
                    e.wait_ge(ssem, mi[("wcast", gm2)])
                    e.dma_start(out=wst[:], in_=wkvrg[w_idx]).then_inc(lw, 16)
                hf, gl = gm2 // 128, gm2 % 128
                if gl < 3 * NM:
                    e.wait_ge(vsem, mi[("out", gm2)])
                else:
                    e.wait_ge(ssem, mi[("outg", gm2)])
                e.dma_start(out=o_kvrg[gl * 128:(gl + 1) * 128, hf * TH:(hf + 1) * TH],
                            in_=outb[gm2 % 2][:]).then_inc(stq[gm2 % 2], 16)
            e.wait_ge(stq[0], 16 * 128); e.wait_ge(stq[1], 16 * 128)

        for _ in range(2):
            emit_vector(FakeCEng()); emit_scalar(FakeCEng()); emit_tensor(FakeCEng())
        mi.strict = True

        @block.sync
        def _(e):
            emit_sync(e)

        @block.scalar
        def _(e):
            emit_scalar(CEng(e, ssem, auto=True))

        @block.tensor
        def _(e):
            emit_tensor(CEng(e, tsem))

        @block.vector
        def _(e):
            emit_vector(CEng(e, vsem, auto=True))
    return nc


def host_prep_D(inputs, xn1_cores):
    Wk = inputs["L1_attn_w_key"].astype(np.float32)
    Wv = inputs["L1_attn_w_value"].astype(np.float32)
    Wr = inputs["L1_attn_w_receptance"].astype(np.float32)
    Wg = inputs["L1_attn_w_gate"].astype(np.float32)
    wk = np.concatenate([_w_prep_lhsT(W.T) for W in (Wk, Wv, Wr, Wg)], axis=0)
    mixs = [inputs[f"L1_attn_time_mix_{n}"].astype(np.float32)
            for n in ("key", "value", "receptance", "gate")]
    mixd = np.zeros((128, 128), np.float32)
    for kt in range(NKT):
        for w in range(4):
            mixd[:, kt * 4 + w] = mixs[w][kt * 128:(kt + 1) * 128] - 1.0
    in_maps = []
    for c in range(NCORES):
        if c % 4 == 0:
            xprev = np.zeros((128, NKT), np.float32).astype(ml_dtypes.bfloat16)
        else:
            xprev = np.ascontiguousarray(xn1_cores[c - 1][:, -1].reshape(NKT, 128).T)
        in_maps.append({"xn1": xn1_cores[c], "xprev": xprev, "wkvrg": wk, "mixd": mixd})
    return in_maps


CHUNK = 128
NCH = S // CHUNK


def build_launch_E():
    nc = bass.Bass("TRN2", target_bir_lowering=False)
    kvr = nc.dram_tensor("kvr", [16, 3, 64, S], BF16, kind="ExternalInput").ap()
    tdc = nc.dram_tensor("tdc", [64, 16], F32, kind="ExternalInput").ap()
    tfc = nc.dram_tensor("tfc", [64, 16], F32, kind="ExternalInput").ap()
    idxp = nc.dram_tensor("idxp", [64, 14 * CHUNK], F32, kind="ExternalInput").ap()
    masksd = nc.dram_tensor("masksd", [128, 6 * CHUNK], F32, kind="ExternalInput").ap()
    idb_d = nc.dram_tensor("idb_d", [64, 64], BF16, kind="ExternalInput").ap()
    ones64_d = nc.dram_tensor("ones64_d", [64, 1], BF16, kind="ExternalInput").ap()
    gnw_d = nc.dram_tensor("gnw_d", [16, 128, 64], BF16, kind="ExternalInput").ap()
    gnb_d = nc.dram_tensor("gnb_d", [16, 128, 64], BF16, kind="ExternalInput").ap()
    o_gn = nc.dram_tensor("o_gn", [16, 128, NCH * 64], BF16, kind="ExternalOutput").ap()

    (IX_INTER, IX_ST, IX_D4R, IX_D4K, IX_K4, IX_R4, IX_K8, IX_R8,
     IX_K16, IX_R16, IX_K32, IX_R32, IX_K64, IX_R64) = range(14)
    with ExitStack() as ctx:
        sbt = lambda name, shape, dt=F32: ctx.enter_context(nc.sbuf_tensor(name, shape, dt))
        kf = [sbt(f"kf{i}", [64, S], BF16) for i in range(2)]
        vf = [sbt(f"vf{i}", [64, S], BF16) for i in range(2)]
        rf = [sbt(f"rf{i}", [64, S], BF16) for i in range(2)]
        scl = {q: sbt(f"scl{q}", [64, S], BF16)
               for q in (IX_INTER, IX_D4R, IX_D4K, IX_K4, IX_R4, IX_K8, IX_R8,
                         IX_K16, IX_R16, IX_K32, IX_R32, IX_K64, IX_R64)}
        tdl = sbt("tdl", [64, 16]); tfl = sbt("tfl", [64, 16])
        ecol = sbt("ecol", [64, 1]); dec128 = sbt("dec128", [64, 1])
        idxl = sbt("idxl", [64, 14 * CHUNK])
        maskl = sbt("maskl", [128, 6 * CHUNK])
        idb = sbt("idb", [64, 64], BF16); ones64 = sbt("ones64", [64, 1], BF16)
        ptmp = sbt("ptmp", [64, CHUNK])
        pst = sbt("pst", [64, CHUNK], BF16)
        pstt = sbt("pstt", [128, 64], BF16)
        vtk = sbt("vtk", [128, 64], BF16); ktk = sbt("ktk", [128, 64], BF16)
        ktt = sbt("ktt", [128, 64], BF16)
        rk = sbt("rk", [64, CHUNK], BF16)
        wcol = sbt("wcol", [128, 1])
        acc = sbt("acc", [128, CHUNK]); tmm = sbt("tmm", [128, CHUNK])
        accb = sbt("accb", [128, CHUNK], BF16)
        sfp = sbt("sfp", [64, 64]); sbf = sbt("sbf", [64, 64], BF16)
        opair = sbt("opair", [128, NCH * 64])
        osq = sbt("osq", [128, NCH * 64])
        gsum = sbt("gsum", [128, NCH]); gsq = sbt("gsq", [128, NCH])
        gmu = sbt("gmu", [128, NCH]); gms = sbt("gms", [128, NCH]); grs = sbt("grs", [128, NCH])
        gnwl = sbt("gnwl", [128, 64], BF16); gnbl = sbt("gnbl", [128, 64], BF16)
        ogn = sbt("ogn", [128, NCH * 64], BF16)

        ps_a = [ctx.enter_context(nc.psum_tensor(f"ps_a{i}", [128, CHUNK], F32)) for i in range(4)]
        ps_o = ctx.enter_context(nc.psum_tensor("ps_o", [128, 64], F32))
        ps_tt = ctx.enter_context(nc.psum_tensor("ps_tt", [64, 64], BF16))
        ps_t2 = ctx.enter_context(nc.psum_tensor("ps_t2", [128, 128], F32))
        ps_s = ctx.enter_context(nc.psum_tensor("ps_s", [64, 64], F32))

        sem = lambda name: ctx.enter_context(nc.semaphore(name))
        ldc = sem("ldc"); lp = [sem("lpa"), sem("lpb")]; lg = sem("lg")
        stq = sem("stq")
        vsem = sem("vsem"); ssem = sem("ssem"); tsem = sem("tsem")
        block = ctx.enter_context(nc.Block())
        mi = Mi()

        SC32 = float(2.0 ** -32)

        def rep_ch(ap128):
            return ap128.unsqueeze(1).broadcast_to((64, NCH, CHUNK))

        def v3(t):
            return t[:].rearrange("p (c k) -> p c k", k=CHUNK)

        def emit_scalar(E):
            for p in range(16):
                E.wait(ldc, 16 * 6)
                if p > 0:
                    E.wait(vsem, mi[("scl_done", p - 1)])
                E.activation(out=ecol[:], in_=tdl[:, p:p + 1], func=AF.Exp)
                E.wait(E, E.n)
                E.activation(out=dec128[:], in_=ecol[:], func=AF.Exp, scale=-float(CHUNK))
                mi[("ecol", p)] = E.n
                for q in range(14):
                    E.wait(vsem, mi[("pt_t", p, q)])
                    if q == IX_ST:
                        E.activation(out=pst[:], in_=ptmp[:], func=AF.Exp, scale=-1.0)
                    else:
                        E.activation(out=ptmp[:], in_=ptmp[:], func=AF.Exp, scale=-1.0)
                    mi[("pt_e", p, q)] = E.n
                E.wait(vsem, mi[("oev_all", p)])
                E.activation(out=osq[:], in_=opair[:], func=AF.Square)
                mi[("osqd", p)] = E.n
                E.wait(vsem, mi[("gmu", p)])
                E.activation(out=gms[:], in_=gmu[:], func=AF.Square)
                mi[("gmusq", p)] = E.n
                E.wait(vsem, mi[("gvar", p)])
                E.activation(out=gsq[:], in_=gsq[:], func=AF.Ln)
                E.wait(E, E.n)
                E.activation(out=grs[:], in_=gsq[:], func=AF.Exp, scale=-0.5)
                mi[("grstd", p)] = E.n

        def emit_tensor(E):
            E.wait(ldc, 16 * 6)
            for p in range(16):
                E.wait(lp[p % 2], 16 * 3 * (p // 2 + 1))
                E.wait(ssem, mi[("pt_e", p, IX_ST)])
                E.transpose(ps_tt[:], pst[:, 0:64], idb[:])
                mi[("ptT1", p)] = E.n
                E.wait(vsem, mi[("ptTev1", p)])
                E.transpose(ps_tt[:], pst[:, 64:128], idb[:])
                mi[("ptT2", p)] = E.n
                for c in range(NCH):
                    sl = slice(c * CHUNK, (c + 1) * CHUNK)
                    E.wait(vsem, mi[("ptTev2", p)] if c == 0 else mi[("kT2e", p, c - 1)])
                    E.transpose(ps_tt[:], vf[p % 2][:, c * CHUNK:c * CHUNK + 64], idb[:])
                    mi[("vT1", p, c)] = E.n
                    E.wait(vsem, mi[("vT1e", p, c)])
                    E.transpose(ps_tt[:], vf[p % 2][:, c * CHUNK + 64:(c + 1) * CHUNK], idb[:])
                    mi[("vT2", p, c)] = E.n
                    E.wait(vsem, mi[("vT2e", p, c)])
                    E.transpose(ps_tt[:], kf[p % 2][:, c * CHUNK:c * CHUNK + 64], idb[:])
                    mi[("kT1", p, c)] = E.n
                    E.wait(vsem, mi[("kT1e", p, c)])
                    E.transpose(ps_tt[:], kf[p % 2][:, c * CHUNK + 64:(c + 1) * CHUNK], idb[:])
                    mi[("kT2", p, c)] = E.n
                    E.wait(vsem, mi[("accprev", p, c)])
                    E.matmul(ps_a[0][:], scl[IX_D4K][:, sl], scl[IX_D4R][:, sl], start=True, stop=True)
                    E.matmul(ps_a[1][:], scl[IX_K4][:, sl], scl[IX_R4][:, sl], start=True, stop=True)
                    E.matmul(ps_a[2][:], scl[IX_K8][:, sl], scl[IX_R8][:, sl], start=True, stop=True)
                    E.matmul(ps_a[3][:], scl[IX_K16][:, sl], scl[IX_R16][:, sl], start=True, stop=True)
                    mi[("amm", p, c)] = E.n
                    E.wait(vsem, mi[("mrg01", p, c)])
                    E.matmul(ps_a[0][:], scl[IX_K32][:, sl], scl[IX_R32][:, sl], start=True, stop=True)
                    E.matmul(ps_a[1][:], scl[IX_K64][:, sl], scl[IX_R64][:, sl], start=True, stop=True)
                    mi[("amm2", p, c)] = E.n
                    E.wait(vsem, mi[("rktf", p, c)])
                    E.matmul(ps_o[:, 0:1], rk[:], ones64[:], start=True, stop=True)
                    mi[("wmm", p, c)] = E.n
                    E.wait(vsem, mi[("accb", p, c)])
                    E.wait(vsem, mi[("wcole", p, c)])
                    E.matmul(ps_o[:], accb[:], vtk[:], start=True, stop=True)
                    mi[("ointra", p, c)] = E.n
                    E.wait(vsem, mi[("sbf", p, c)])
                    E.wait(vsem, mi[("oev1", p, c)])
                    E.matmul(ps_t2[:, 0:64], scl[IX_INTER][:, sl], sbf[:], start=True, stop=True)
                    mi[("ointer", p, c)] = E.n
                    E.wait(vsem, mi[("ktt", p, c)])
                    E.wait(vsem, mi[("sdecay", p, c)])
                    E.matmul(ps_s[:], ktt[:], vtk[:], start=True, stop=True)
                    mi[("smm", p, c)] = E.n
                mi[("pairdone_t", p)] = E.n

        def emit_vector(E):
            E.wait(ldc, 16 * 6)
            for p in range(16):
                E.wait(lp[p % 2], 16 * 3 * (p // 2 + 1))
                E.wait(lg, 16 * 2 * (p + 1))
                for q in range(14):
                    E.wait(ssem, mi[("ecol", p)])
                    if q > 0:
                        E.wait(ssem, mi[("pt_e", p, q - 1)])
                    E.tensor_scalar(out=ptmp[:], in0=idxl[:, q * CHUNK:(q + 1) * CHUNK],
                                    scalar1=ecol[:], scalar2=None, op0=ALU.mult)
                    mi[("pt_t", p, q)] = E.n
                    E.wait(ssem, mi[("pt_e", p, q)])
                    if q == IX_ST:
                        pass
                    else:
                        if q in (IX_D4R, IX_D4K):
                            E.tensor_scalar(out=ptmp[:], in0=ptmp[:], scalar1=SC32,
                                            scalar2=None, op0=ALU.mult)
                        src = rf if q in (IX_INTER, IX_D4R, IX_R4, IX_R8,
                                          IX_R16, IX_R32, IX_R64) else kf
                        E.tensor_tensor(out=v3(scl[q]), in0=v3(src[p % 2]),
                                        in1=rep_ch(ptmp[:]), op=ALU.mult)
                    mi[("pt_full", p, q)] = E.n
                mi[("scl_done", p)] = E.n
                E.wait(tsem, mi[("ptT1", p)])
                E.tensor_copy(out=pstt[0:64, :], in_=ps_tt[:])
                mi[("ptTev1", p)] = E.n
                E.wait(tsem, mi[("ptT2", p)])
                E.tensor_copy(out=pstt[64:128, :], in_=ps_tt[:])
                mi[("ptTev2", p)] = E.n
                E.memset(sfp[:], 0.0)
                E.memset(sbf[:], 0.0)
                mi[("sbf", p, 0)] = E.n
                mi[("sdecay", p, 0)] = E.n
                for c in range(NCH):
                    sl = slice(c * CHUNK, (c + 1) * CHUNK)
                    mi[("accprev", p, c)] = mi[("accb", p, c - 1)] if c > 0 else E.n
                    E.wait(tsem, mi[("vT1", p, c)])
                    E.tensor_copy(out=vtk[0:64, :], in_=ps_tt[:])
                    mi[("vT1e", p, c)] = E.n
                    E.wait(tsem, mi[("vT2", p, c)])
                    E.tensor_copy(out=vtk[64:128, :], in_=ps_tt[:])
                    mi[("vT2e", p, c)] = E.n
                    E.wait(tsem, mi[("kT1", p, c)])
                    E.tensor_copy(out=ktk[0:64, :], in_=ps_tt[:])
                    mi[("kT1e", p, c)] = E.n
                    E.wait(tsem, mi[("kT2", p, c)])
                    E.tensor_copy(out=ktk[64:128, :], in_=ps_tt[:])
                    mi[("kT2e", p, c)] = E.n
                    E.tensor_tensor(out=ktt[:], in0=ktk[:], in1=pstt[:], op=ALU.mult)
                    mi[("ktt", p, c)] = E.n
                    E.tensor_tensor(out=rk[:], in0=rf[p % 2][:, sl], in1=kf[p % 2][:, sl], op=ALU.mult)
                    E.tensor_scalar(out=rk[:], in0=rk[:], scalar1=tfl[:, p:p + 1], scalar2=None,
                                    op0=ALU.mult)
                    mi[("rktf", p, c)] = E.n
                    E.wait(tsem, mi[("amm", p, c)])
                    E.tensor_tensor(out=acc[:], in0=ps_a[0][:], in1=maskl[:, 0:CHUNK], op=ALU.mult)
                    E.tensor_tensor(out=tmm[:], in0=ps_a[1][:],
                                    in1=maskl[:, 1 * CHUNK:2 * CHUNK], op=ALU.mult)
                    E.tensor_tensor(out=acc[:], in0=acc[:], in1=tmm[:], op=ALU.add)
                    mi[("mrg01", p, c)] = E.n
                    for lv in range(2):
                        E.tensor_tensor(out=tmm[:], in0=ps_a[lv + 2][:],
                                        in1=maskl[:, (lv + 2) * CHUNK:(lv + 3) * CHUNK], op=ALU.mult)
                        E.tensor_tensor(out=acc[:], in0=acc[:], in1=tmm[:], op=ALU.add)
                    E.wait(tsem, mi[("amm2", p, c)])
                    for lv in range(2):
                        E.tensor_tensor(out=tmm[:], in0=ps_a[lv][:],
                                        in1=maskl[:, (lv + 4) * CHUNK:(lv + 5) * CHUNK], op=ALU.mult)
                        E.tensor_tensor(out=acc[:], in0=acc[:], in1=tmm[:], op=ALU.add)
                    E.tensor_copy(out=accb[:], in_=acc[:])
                    mi[("accb", p, c)] = E.n
                    E.wait(tsem, mi[("wmm", p, c)])
                    E.tensor_copy(out=wcol[:], in_=ps_o[:, 0:1])
                    mi[("wcole", p, c)] = E.n
                    E.wait(tsem, mi[("ointra", p, c)])
                    E.tensor_copy(out=opair[:, c * 64:(c + 1) * 64], in_=ps_o[:])
                    mi[("oev1", p, c)] = E.n
                    E.wait(tsem, mi[("ointer", p, c)])
                    E.scalar_tensor_tensor(out=opair[0:128, c * 64:(c + 1) * 64], in0=ps_t2[:, 0:64],
                                           scalar=1.0, in1=opair[:, c * 64:(c + 1) * 64],
                                           op0=ALU.mult, op1=ALU.add)
                    E.scalar_tensor_tensor(out=opair[:, c * 64:(c + 1) * 64], in0=vtk[:],
                                           scalar=wcol[:], in1=opair[:, c * 64:(c + 1) * 64],
                                           op0=ALU.mult, op1=ALU.add)
                    mi[("oev", p, c)] = E.n
                    E.tensor_scalar(out=sfp[:], in0=sfp[:], scalar1=dec128[:], scalar2=None,
                                    op0=ALU.mult)
                    mi[("sdecay", p, c + 1)] = E.n
                    E.wait(tsem, mi[("smm", p, c)])
                    E.scalar_tensor_tensor(out=sfp[:], in0=ps_s[:], scalar=1.0, in1=sfp[:],
                                           op0=ALU.mult, op1=ALU.add)
                    E.tensor_copy(out=sbf[:], in_=sfp[:])
                    mi[("sbf", p, c + 1)] = E.n
                mi[("oev_all", p)] = E.n
                E.tensor_reduce(out=gsum[:], in_=opair[:].rearrange("p (c e) -> p c e", e=64),
                                axis=AX.X, op=ALU.add)
                E.tensor_scalar(out=gmu[:], in0=gsum[:], scalar1=1.0 / 64, scalar2=None, op0=ALU.mult)
                mi[("gmu", p)] = E.n
                E.wait(ssem, mi[("osqd", p)])
                E.tensor_reduce(out=gsq[:], in_=osq[:].rearrange("p (c e) -> p c e", e=64),
                                axis=AX.X, op=ALU.add)
                E.tensor_scalar(out=gsq[:], in0=gsq[:], scalar1=1.0 / 64, scalar2=None, op0=ALU.mult)
                E.wait(ssem, mi[("gmusq", p)])
                E.tensor_tensor(out=gsq[:], in0=gsq[:], in1=gms[:], op=ALU.subtract)
                E.tensor_scalar(out=gsq[:], in0=gsq[:], scalar1=1e-5, scalar2=None, op0=ALU.add)
                mi[("gvar", p)] = E.n
                E.wait(ssem, mi[("grstd", p)])
                if p >= 1:
                    E.wait(stq, 16 * p)
                op3 = opair[:].rearrange("p (c e) -> p c e", e=64)
                E.tensor_tensor(out=op3, in0=op3,
                                in1=gmu[:].unsqueeze(2).broadcast_to((128, NCH, 64)), op=ALU.subtract)
                E.tensor_tensor(out=op3, in0=op3,
                                in1=grs[:].unsqueeze(2).broadcast_to((128, NCH, 64)), op=ALU.mult)
                E.tensor_tensor(out=op3, in0=op3,
                                in1=gnwl[:].unsqueeze(1).broadcast_to((128, NCH, 64)), op=ALU.mult)
                E.tensor_tensor(out=ogn[:].rearrange("p (c e) -> p c e", e=64), in0=op3,
                                in1=gnbl[:].unsqueeze(1).broadcast_to((128, NCH, 64)), op=ALU.add)
                mi[("ogn", p)] = E.n

        def emit_sync(e):
            e.dma_start(out=tdl[:], in_=tdc).then_inc(ldc, 16)
            e.dma_start(out=tfl[:], in_=tfc).then_inc(ldc, 16)
            e.dma_start(out=idxl[:], in_=idxp).then_inc(ldc, 16)
            e.dma_start(out=maskl[:], in_=masksd).then_inc(ldc, 16)
            e.dma_start(out=idb[:], in_=idb_d).then_inc(ldc, 16)
            e.dma_start(out=ones64[:], in_=ones64_d).then_inc(ldc, 16)
            for p in range(2):
                e.dma_start(out=kf[p][:], in_=kvr[p, 0]).then_inc(lp[p], 16)
                e.dma_start(out=vf[p][:], in_=kvr[p, 1]).then_inc(lp[p], 16)
                e.dma_start(out=rf[p][:], in_=kvr[p, 2]).then_inc(lp[p], 16)
            e.dma_start(out=gnwl[:], in_=gnw_d[0]).then_inc(lg, 16)
            e.dma_start(out=gnbl[:], in_=gnb_d[0]).then_inc(lg, 16)
            for p in range(16):
                e.wait_ge(vsem, mi[("ogn", p)])
                e.dma_start(out=o_gn[p], in_=ogn[:]).then_inc(stq, 16)
                if p + 2 < 16:
                    e.wait_ge(tsem, mi[("pairdone_t", p)])
                    e.dma_start(out=kf[p % 2][:], in_=kvr[p + 2, 0]).then_inc(lp[p % 2], 16)
                    e.dma_start(out=vf[p % 2][:], in_=kvr[p + 2, 1]).then_inc(lp[p % 2], 16)
                    e.dma_start(out=rf[p % 2][:], in_=kvr[p + 2, 2]).then_inc(lp[p % 2], 16)
                if p + 1 < 16:
                    e.wait_ge(vsem, mi[("ogn", p)])
                    e.dma_start(out=gnwl[:], in_=gnw_d[p + 1]).then_inc(lg, 16)
                    e.dma_start(out=gnbl[:], in_=gnb_d[p + 1]).then_inc(lg, 16)
            e.wait_ge(stq, 16 * 16)

        for _ in range(2):
            emit_scalar(FakeCEng()); emit_tensor(FakeCEng()); emit_vector(FakeCEng())
        mi.strict = True

        @block.sync
        def _(e):
            emit_sync(e)

        @block.scalar
        def _(e):
            emit_scalar(CEng(e, ssem, auto=True))

        @block.tensor
        def _(e):
            emit_tensor(CEng(e, tsem))

        @block.vector
        def _(e):
            emit_vector(CEng(e, vsem, auto=True))
    return nc


def host_consts_E():
    i = np.arange(CHUNK, dtype=np.float32)
    pats = [i, (CHUNK - 1) - i, (i % 4) - 2, 1 - (i % 4),
            3 - (i % 4), i % 4, 7 - (i % 8), i % 8,
            15 - (i % 16), i % 16, 31 - (i % 32), i % 32, 63 - (i % 64), i % 64]
    idxp = np.broadcast_to(np.concatenate(pats).reshape(1, 14 * CHUNK), (64, 14 * CHUNK)).copy()
    jj = np.arange(CHUNK)[:, None]; ii = np.arange(CHUNK)[None, :]
    same4 = (jj // 4) == (ii // 4)
    mD = np.where(same4 & (ii > jj), float(2.0 ** 64), 0.0)
    mlv = [np.where(((jj // L) % 2 == 0) & ((ii // L) == (jj // L) + 1), 1.0, 0.0)
           for L in (4, 8, 16, 32, 64)]
    masksd = np.concatenate([mD] + mlv, axis=1).astype(np.float32)
    idb = np.eye(64, dtype=np.float32).astype(ml_dtypes.bfloat16)
    ones64 = np.ones((64, 1), np.float32).astype(ml_dtypes.bfloat16)
    return {"idxp": idxp.astype(np.float32), "masksd": masksd, "idb_d": idb, "ones64_d": ones64}


def host_prep_E(inputs, kvrg_cores):
    consts = host_consts_E()
    td = inputs["L1_attn_time_decay"].astype(np.float32)
    tf = inputs["L1_attn_time_faaaa"].astype(np.float32)
    gnw = inputs["L1_attn_gn_weight"].astype(np.float32)
    gnb = inputs["L1_attn_gn_bias"].astype(np.float32)
    kA = [np.concatenate([kvrg_cores[4 * b + i][0 * D:1 * D] for i in range(4)], axis=1) for b in range(B)]
    vA = [np.concatenate([kvrg_cores[4 * b + i][1 * D:2 * D] for i in range(4)], axis=1) for b in range(B)]
    rA = [np.concatenate([kvrg_cores[4 * b + i][2 * D:3 * D] for i in range(4)], axis=1) for b in range(B)]
    in_maps = []
    for c in range(NCORES):
        b, h0 = c // 4, (c % 4) * 16
        kvrl = np.zeros((16, 3, 64, S), np.float32).astype(ml_dtypes.bfloat16)
        for p in range(16):
            h = h0 + p
            kvrl[p, 0] = kA[b][h * 64:(h + 1) * 64]
            kvrl[p, 1] = vA[b][h * 64:(h + 1) * 64]
            kvrl[p, 2] = rA[b][h * 64:(h + 1) * 64]
        tdcl = np.ascontiguousarray(td[h0:h0 + 16].T)
        tfcl = np.ascontiguousarray(tf[h0:h0 + 16].T)
        gnwl = np.zeros((16, 128, 64), np.float32)
        gnbl = np.zeros((16, 128, 64), np.float32)
        for p in range(16):
            h = h0 + p
            gnwl[p] = np.broadcast_to(gnw[h * 64:(h + 1) * 64], (128, 64))
            gnbl[p] = np.broadcast_to(gnb[h * 64:(h + 1) * 64], (128, 64))
        m = dict(consts)
        m.update({"kvr": kvrl, "tdc": tdcl, "tfc": tfcl,
                  "gnw_d": gnwl.astype(ml_dtypes.bfloat16),
                  "gnb_d": gnbl.astype(ml_dtypes.bfloat16)})
        in_maps.append(m)
    return in_maps



def build_launch_F(dbg=0):
    nc = bass.Bass("TRN2", target_bir_lowering=False)
    gsl = nc.dram_tensor("gsl", [D, TC], BF16, kind="ExternalInput").ap()
    gnsl = nc.dram_tensor("gnsl", [D, TC], BF16, kind="ExternalInput").ap()
    xn1 = nc.dram_tensor("xn1", [D, TC], BF16, kind="ExternalInput").ap()
    x1f = nc.dram_tensor("x1f", [D, TC], F32, kind="ExternalInput").ap()
    wout_d = nc.dram_tensor("wout_d", [32, 128, D], F32, kind="ExternalInput").ap()
    wbase_d = nc.dram_tensor("wbase_d", [32, 128, D], F32, kind="ExternalInput").ap()
    aflat_d = nc.dram_tensor("aflat_d", [128, D], F32, kind="ExternalInput").ap()
    bflat_d = nc.dram_tensor("bflat_d", [128, D], F32, kind="ExternalInput").ap()
    wr8_d = nc.dram_tensor("wr8_d", [128, 256], F32, kind="ExternalInput").ap()
    bb_d = nc.dram_tensor("bb_d", [128, 32], F32, kind="ExternalInput").ap()
    rep_d = nc.dram_tensor("rep_d", [8, 128], BF16, kind="ExternalInput").ap()
    i8_d = nc.dram_tensor("i8_d", [8, 8], F32, kind="ExternalInput").ap()
    i128_d = nc.dram_tensor("i128_d", [128, 128], F32, kind="ExternalInput").ap()
    o_out = nc.dram_tensor("o_out", [D, TC], F32, kind="ExternalOutput").ap()
    if dbg:
        o_dbg = nc.dram_tensor("o_dbg", [384, TC], BF16, kind="ExternalOutput").ap()
        o_dbg2 = nc.dram_tensor("o_dbg2", [8, TC], F32, kind="ExternalOutput").ap()
        o_dbg3 = nc.dram_tensor("o_dbg3", [256, D], BF16, kind="ExternalOutput").ap()
        o_dbg4 = nc.dram_tensor("o_dbg4", [128, TC], BF16, kind="ExternalOutput").ap()
    NM = 32
    with ExitStack() as ctx:
        sbt = lambda name, shape, dt=F32: ctx.enter_context(nc.sbuf_tensor(name, shape, dt))
        xnb = sbt("xnb", [128, NKT * TC], BF16)
        h1 = sbt("h1", [128, NKT * TC], BF16)
        gstg = [sbt(f"gstg{i}", [128, TC], BF16) for i in range(2)]
        gnstg = [sbt(f"gnstg{i}", [128, TC], BF16) for i in range(2)]
        wst = sbt("wst", [128, D]); wst2 = wst
        _wbf = sbt("wbf", [128, D], BF16); wbf = [_wbf, _wbf]
        _wbf2 = sbt("wbf2", [128, D], BF16); wbf2 = [_wbf2, _wbf2]
        afb = sbt("afb", [128, D], BF16); bfb = sbt("bfb", [128, D], BF16)
        wr8 = sbt("wr8", [128, 256]); wr8b = sbt("wr8b", [128, 256], BF16)
        bbl = sbt("bbl", [128, 32])
        repm = sbt("repm", [8, 128], BF16); i8 = sbt("i8", [8, 8]); i128f = sbt("i128f", [128, 128])
        lgrow = sbt("lgrow", [8, TC])
        wet = lgrow[:].bitcast(BF16)[:, 0:TC]
        lgt = sbt("lgt", [128, 8])
        l1c = sbt("l1c", [128, 1]); l2c = sbt("l2c", [128, 1])
        eq1 = sbt("eq1", [128, 8]); eq2 = sbt("eq2", [128, 8]); tmp8 = sbt("tmp8", [128, 8])
        w1c = sbt("w1c", [128, 1]); w2c = sbt("w2c", [128, 1])
        web = gstg[0]
        hb = gnstg[0]
        hw = gstg[1]
        x1t = [sbt(f"x1t{i}", [128, TC]) for i in range(2)]
        outf = [sbt(f"outf{i}", [128, TC]) for i in range(2)]
        ps_o = [ctx.enter_context(nc.psum_tensor(f"ps_o{i}", [128, TC], F32)) for i in range(2)]
        ps_b = ctx.enter_context(nc.psum_tensor("ps_b", [128, TC], F32))
        sem = lambda name: ctx.enter_context(nc.semaphore(name))
        ldc = sem("ldc"); lgn = [sem("lgna"), sem("lgnb")]; lxn = sem("lxn")
        laf = [sem("lafa"), sem("lafb")]
        lw = sem("lw"); lw2 = sem("lw2"); lx = [sem("lxa"), sem("lxb")]
        st1 = [sem("st1a"), sem("st1b")]
        vsem = sem("vsem"); ssem = sem("ssem"); tsem = sem("tsem")
        block = ctx.enter_context(nc.Block())
        mi = Mi()
        SCALING = 2.0

        def emit_vector(E):
            E.wait(ldc, 16 * 5)
            E.tensor_copy(out=wr8b[:], in_=wr8[:])
            mi["wr8b"] = E.n
            for q in range(4):
                E.wait(laf[q % 2], 16 * (q // 2 + 1))
                E.tensor_copy(out=afb[:, q * 1024:(q + 1) * 1024], in_=outf[q % 2][:])
                mi[("afq", q)] = E.n
            mi["afb"] = E.n
            for q in range(4):
                g = 4 + q
                E.wait(laf[g % 2], 16 * (g // 2 + 1))
                E.tensor_copy(out=bfb[:, q * 1024:(q + 1) * 1024], in_=outf[g % 2][:])
                mi[("bfq", q)] = E.n
            mi["bfb"] = E.n
            for kt in range(NKT):
                E.wait(lgn[kt % 2], 16 * 2 * (kt // 2 + 1))
                E.tensor_tensor(out=h1[:, kt * TC:(kt + 1) * TC], in0=gstg[kt % 2][:],
                                in1=gnstg[kt % 2][:], op=ALU.mult)
                mi[("h1", kt)] = E.n
            E.wait(tsem, mi["rmm"])
            E.tensor_copy(out=lgrow[:], in_=ps_o[0][0:8, :])
            mi["lgrow"] = E.n
            for t in range(8):
                E.wait(tsem, mi[("lgT", t)])
                E.tensor_copy(out=lgt[:], in_=ps_b[:, 0:8])
                E.tensor_reduce(out=l1c[:], in_=lgt[:], axis=AX.X, op=ALU.max)
                E.tensor_scalar(out=eq1[:], in0=lgt[:], scalar1=l1c[:], scalar2=None, op0=ALU.is_equal)
                E.tensor_scalar(out=tmp8[:], in0=eq1[:], scalar1=-1e30, scalar2=None, op0=ALU.mult)
                E.tensor_tensor(out=tmp8[:], in0=lgt[:], in1=tmp8[:], op=ALU.add)
                E.tensor_reduce(out=l2c[:], in_=tmp8[:], axis=AX.X, op=ALU.max)
                E.tensor_scalar(out=eq2[:], in0=lgt[:], scalar1=l2c[:], scalar2=None, op0=ALU.is_equal)
                E.tensor_tensor(out=w1c[:], in0=l2c[:], in1=l1c[:], op=ALU.subtract)
                mi[("d21", t)] = E.n
                E.wait(ssem, mi[("expw", t)])
                E.tensor_scalar(out=w1c[:], in0=w1c[:], scalar1=1.0, scalar2=None, op0=ALU.add)
                E.reciprocal(out=w1c[:], in_=w1c[:])
                E.tensor_scalar(out=w1c[:], in0=w1c[:], scalar1=SCALING, scalar2=None, op0=ALU.mult)
                E.tensor_scalar(out=w2c[:], in0=w1c[:], scalar1=-1.0, scalar2=SCALING,
                                op0=ALU.mult, op1=ALU.add)
                E.tensor_scalar(out=eq1[:], in0=eq1[:], scalar1=w1c[:], scalar2=None, op0=ALU.mult)
                E.tensor_scalar(out=eq2[:], in0=eq2[:], scalar1=w2c[:], scalar2=None, op0=ALU.mult)
                E.tensor_tensor(out=eq1[:], in0=eq1[:], in1=eq2[:], op=ALU.add)
                mi[("we", t)] = E.n
                E.wait(tsem, mi[("wetT", t)])
                E.tensor_copy(out=wet[:, t * 128:(t + 1) * 128], in_=ps_b[0:8, 0:128])
                mi[("wetev", t)] = E.n
            E.wait(tsem, mi["webmm"])
            E.tensor_copy(out=web[:], in_=ps_b[:])
            mi["webev"] = E.n
            E.wait(tsem, mi["hmm"])
            E.tensor_copy(out=hb[:], in_=ps_o[1][:])
            E.tensor_tensor(out=hw[:], in0=hb[:], in1=web[:], op=ALU.mult)
            mi["hw"] = E.n
            for m in range(NM):
                E.wait(tsem, mi[("mm", m)])
                E.wait(lx[m % 2], 16 * (m // 2 + 1))
                if m >= 2:
                    E.wait(st1[m % 2], 16 * (m // 2))
                E.tensor_scalar(out=outf[m % 2][:], in0=ps_o[m % 2][:], scalar1=bbl[:, m:m + 1],
                                scalar2=None, op0=ALU.add)
                E.tensor_tensor(out=outf[m % 2][:], in0=outf[m % 2][:], in1=x1t[m % 2][:], op=ALU.add)
                mi[("out", m)] = E.n

        def emit_scalar(E):
            for t in range(8):
                E.wait(vsem, mi[("d21", t)])
                E.activation(out=w1c[:], in_=w1c[:], func=AF.Exp)
                mi[("expw", t)] = E.n
            for m in range(NM):
                E.wait(lw, 16 * (2 * m + 1))
                if m >= 1:
                    E.wait(tsem, mi[("mm", m - 1)])
                E.activation(out=wbf[m % 2][:], in_=wst[:], func=AF.Identity)
                mi[("wcast", m)] = E.n
                E.wait(lw, 16 * (2 * m + 2))
                E.activation(out=wbf2[m % 2][:], in_=wst2[:], func=AF.Identity)
                mi[("wcast2", m)] = E.n

        def emit_tensor(E):
            E.wait(ldc, 16 * 5)
            E.wait(lxn, 16 * NKT)
            E.wait(vsem, mi["wr8b"])
            for kt in range(NKT):
                for h in range(2):
                    last = (kt == NKT - 1 and h == 1)
                    f = E.matmul if last else E.un("matmul")
                    f(ps_o[0][0:8, h * 512:(h + 1) * 512], wr8b[:, kt * 8:(kt + 1) * 8],
                      xnb[:, kt * TC + h * 512:kt * TC + (h + 1) * 512],
                      start=(kt == 0), stop=(kt == NKT - 1))
            mi["rmm"] = E.n
            E.wait(vsem, mi["afb"])
            for kt in range(NKT):
                for h in range(2):
                    last = (kt == NKT - 1 and h == 1)
                    f = E.matmul if last else E.un("matmul")
                    f(ps_o[1][:, h * 512:(h + 1) * 512], afb[:, kt * 128:(kt + 1) * 128],
                      xnb[:, kt * TC + h * 512:kt * TC + (h + 1) * 512],
                      start=(kt == 0), stop=(kt == NKT - 1))
            mi["hmm"] = E.n
            for t in range(8):
                E.wait(vsem, mi["lgrow"] if t == 0 else mi[("wetev", t - 1)])
                E.matmul(ps_b[:, 0:8], lgrow[:, t * 128:(t + 1) * 128], i8[:], start=True, stop=True)
                mi[("lgT", t)] = E.n
                E.wait(vsem, mi[("we", t)])
                E.matmul(ps_b[0:8, 0:128], eq1[:], i128f[:], start=True, stop=True)
                mi[("wetT", t)] = E.n
            E.wait(vsem, mi[("wetev", 7)])
            E.un("matmul")(ps_b[:, 0:512], repm[:], wet[:, 0:512], start=True, stop=True)
            E.matmul(ps_b[:, 512:1024], repm[:], wet[:, 512:1024], start=True, stop=True)
            mi["webmm"] = E.n
            E.wait(vsem, mi["hw"])
            E.wait(vsem, mi["bfb"])
            for m in range(NM):
                E.wait(ssem, mi[("wcast2", m)])
                if m >= 2:
                    E.wait(vsem, mi[("out", m - 2)])
                pso = ps_o[m % 2]
                for h in range(2):
                    E.un("matmul")(pso[:, h * 512:(h + 1) * 512],
                                   bfb[:, m * 128:(m + 1) * 128],
                                   hw[:, h * 512:(h + 1) * 512], start=True, stop=False)
                    for kt in range(NKT):
                        E.un("matmul")(pso[:, h * 512:(h + 1) * 512],
                                       wbf[m % 2][:, kt * 128:(kt + 1) * 128],
                                       h1[:, kt * TC + h * 512:kt * TC + (h + 1) * 512],
                                       start=False, stop=False)
                    for kt in range(NKT):
                        last = (h == 1 and kt == NKT - 1)
                        f = E.matmul if last else E.un("matmul")
                        f(pso[:, h * 512:(h + 1) * 512],
                          wbf2[m % 2][:, kt * 128:(kt + 1) * 128],
                          xnb[:, kt * TC + h * 512:kt * TC + (h + 1) * 512],
                          start=False, stop=(kt == NKT - 1))
                mi[("mm", m)] = E.n

        def emit_vector_post(E):
            pass

        def emit_sync(e):
            e.dma_start(out=wr8[:], in_=wr8_d).then_inc(ldc, 16)
            e.dma_start(out=bbl[:], in_=bb_d).then_inc(ldc, 16)
            e.dma_start(out=repm[:], in_=rep_d).then_inc(ldc, 16)
            e.dma_start(out=i8[:], in_=i8_d).then_inc(ldc, 16)
            e.dma_start(out=i128f[:], in_=i128_d).then_inc(ldc, 16)
            for q in range(4):
                if q >= 2:
                    e.wait_ge(vsem, mi[("afq", q - 2)])
                e.dma_start(out=outf[q % 2][:], in_=aflat_d[:, q * 1024:(q + 1) * 1024]).then_inc(laf[q % 2], 16)
            for q in range(4):
                g = 4 + q
                if q >= 2:
                    e.wait_ge(vsem, mi[("bfq", q - 2)])
                else:
                    e.wait_ge(vsem, mi[("afq", q + 2)])
                e.dma_start(out=outf[g % 2][:], in_=bflat_d[:, q * 1024:(q + 1) * 1024]).then_inc(laf[g % 2], 16)
            for i in range(NKT):
                e.dma_start(out=xnb[:, i * TC:(i + 1) * TC],
                            in_=xn1[i * 128:(i + 1) * 128, :]).then_inc(lxn, 16)
            for kt in range(NKT):
                if kt >= 2:
                    e.wait_ge(vsem, mi[("h1", kt - 2)])
                e.dma_start(out=gstg[kt % 2][:],
                            in_=gsl[kt * 128:(kt + 1) * 128, :]).then_inc(lgn[kt % 2], 16)
                e.dma_start(out=gnstg[kt % 2][:],
                            in_=gnsl[kt * 128:(kt + 1) * 128, :]).then_inc(lgn[kt % 2], 16)
            e.dma_start(out=wst[:], in_=wout_d[0]).then_inc(lw, 16)
            e.wait_ge(ssem, mi[("wcast", 0)])
            e.dma_start(out=wst[:], in_=wbase_d[0]).then_inc(lw, 16)
            for m in range(NM):
                if m + 1 < NM:
                    e.wait_ge(ssem, mi[("wcast2", m)])
                    e.dma_start(out=wst[:], in_=wout_d[m + 1]).then_inc(lw, 16)
                    e.wait_ge(ssem, mi[("wcast", m + 1)])
                    e.dma_start(out=wst[:], in_=wbase_d[m + 1]).then_inc(lw, 16)
                e.wait_ge(vsem, mi[("out", m - 1)] if m >= 1 else 0)
                e.dma_start(out=x1t[m % 2][:], in_=x1f[m * 128:(m + 1) * 128, :]).then_inc(lx[m % 2], 16)
                e.wait_ge(vsem, mi[("out", m)])
                e.dma_start(out=o_out[m * 128:(m + 1) * 128, :],
                            in_=outf[m % 2][:]).then_inc(st1[m % 2], 16)
            if dbg:
                e.wait_ge(vsem, mi["hw"])
                e.dma_start(out=o_dbg[0:128], in_=hb[:]).then_inc(ldc, 16)
                e.dma_start(out=o_dbg[128:256], in_=web[:]).then_inc(ldc, 16)
                e.dma_start(out=o_dbg[256:384], in_=hw[:]).then_inc(ldc, 16)
                e.dma_start(out=o_dbg2, in_=lgrow[:]).then_inc(ldc, 16)
                e.dma_start(out=o_dbg3[0:128], in_=afb[:]).then_inc(ldc, 16)
                e.dma_start(out=o_dbg3[128:256], in_=bfb[:]).then_inc(ldc, 16)
                e.dma_start(out=o_dbg4, in_=h1[:, 0:TC]).then_inc(ldc, 16)
                e.wait_ge(ldc, 16 * 12)
            e.wait_ge(st1[0], 16 * 16); e.wait_ge(st1[1], 16 * 16)

        def emit_vector_full(E):
            emit_vector(E)

        _orig_emit_vector = emit_vector

        def emit_vector2(E):
            _orig_emit_vector(E)


        for _ in range(2):
            emit_vector(FakeCEng()); emit_scalar(FakeCEng()); emit_tensor(FakeCEng())
        mi.strict = True

        @block.sync
        def _(e):
            emit_sync(e)

        @block.scalar
        def _(e):
            emit_scalar(CEng(e, ssem, auto=True))

        @block.tensor
        def _(e):
            emit_tensor(CEng(e, tsem))

        @block.vector
        def _(e):
            emit_vector(CEng(e, vsem, auto=True))
    return nc


def host_prep_F(inputs, x1_cores, xn1_cores, kvrg_cores, gn_cores):
    Wout = inputs["L1_attn_w_output"].astype(np.float32)
    wout_d = _w_prep_lhsT(Wout.T)
    Wb = inputs["L1_ffn_W_base"].astype(np.float32)
    wbase_d = _w_prep_lhsT(Wb.T)
    A = inputs["L1_ffn_A"].astype(np.float32)
    aflat_d = np.ascontiguousarray(A.reshape(128, D).T)
    aflat_d = _w_prep_lhsT(A.reshape(128, D).T)[0]
    Bm = inputs["L1_ffn_B"].astype(np.float32)
    bflat_d = np.ascontiguousarray(Bm.transpose(0, 2, 1).reshape(128, D))
    Wr = inputs["L1_ffn_W_router"].astype(np.float32)
    wr8 = np.ascontiguousarray(Wr.T.reshape(NKT, 128, 8).transpose(1, 0, 2).reshape(128, 256))
    bb = inputs["L1_ffn_b_base"].astype(np.float32)
    bb_d = np.ascontiguousarray(bb.reshape(32, 128).T)
    rep = np.zeros((8, 128), np.float32)
    for e in range(8):
        rep[e, e * 16:(e + 1) * 16] = 1.0
    gnf = [np.zeros((D, S), np.float32) for _ in range(B)]
    for c in range(NCORES):
        b, h0 = c // 4, (c % 4) * 16
        g = np.asarray(gn_cores[c], np.float32).reshape(16, 128, NCH, 64)
        for p in range(16):
            h = h0 + p
            gnf[b][h * 64:(h + 1) * 64, :] = g[p].transpose(1, 0, 2).reshape(S, 64).T
    in_maps = []
    for c in range(NCORES):
        b, sl = c // 4, slice((c % 4) * TC, (c % 4) * TC + TC)
        gslice = kvrg_cores[c][3 * D:4 * D]
        in_maps.append({
            "gsl": np.ascontiguousarray(gslice),
            "gnsl": np.ascontiguousarray(gnf[b][:, sl]).astype(ml_dtypes.bfloat16),
            "xn1": xn1_cores[c], "x1f": x1_cores[c],
            "wout_d": wout_d, "wbase_d": wbase_d, "aflat_d": aflat_d,
            "bflat_d": bflat_d, "wr8_d": wr8, "bb_d": bb_d,
            "rep_d": rep.astype(ml_dtypes.bfloat16),
            "i8_d": np.eye(8, dtype=np.float32),
            "i128_d": np.eye(128, dtype=np.float32),
        })
    return in_maps



def kernel(**inputs):
    C = list(range(NCORES))
    rA = run_bass_kernel_spmd(build_launch_A(), host_prep_A(inputs), core_ids=C)
    qkv = [np.asarray(r["o_qkv"]) for r in rA.results]
    xn0 = [np.asarray(r["o_xn"]) for r in rA.results]
    rB = run_bass_kernel_spmd(build_launch_B(), host_prep_B(qkv), core_ids=C)
    att = [np.asarray(r["o_att"]) for r in rB.results]
    rC1 = run_bass_kernel_spmd(build_launch_C1(), host_prep_C1(inputs, att), core_ids=C)
    x1p = [np.asarray(r["o_x1p"]) for r in rC1.results]
    rC2 = run_bass_kernel_spmd(build_launch_C2(), host_prep_C2(inputs, x1p, xn0), core_ids=C)
    x1 = [np.asarray(r["o_x1"]) for r in rC2.results]
    xn1 = [np.asarray(r["o_xn1"]) for r in rC2.results]
    rD = run_bass_kernel_spmd(build_launch_D(), host_prep_D(inputs, xn1), core_ids=C)
    kvrg = [np.asarray(r["o_kvrg"]) for r in rD.results]
    rE = run_bass_kernel_spmd(build_launch_E(), host_prep_E(inputs, kvrg), core_ids=C)
    gn = [np.asarray(r["o_gn"]) for r in rE.results]
    rF = run_bass_kernel_spmd(build_launch_F(), host_prep_F(inputs, x1, xn1, kvrg, gn), core_ids=C)
    full = np.zeros((B, S, D), np.float32)
    for c in range(NCORES):
        b, sl = c // 4, slice((c % 4) * TC, (c % 4) * TC + TC)
        full[b, sl, :] = np.asarray(rF.results[c]["o_out"], np.float32).T
    return full


# version 03
# speedup vs baseline: 1.0749x; 1.0749x over previous
import math
import numpy as np
import ml_dtypes
import concourse.bass as bass
import concourse.mybir as mybir
from concourse.bass_utils import run_bass_kernel_spmd
from contextlib import ExitStack

F32 = mybir.dt.float32
BF16 = mybir.dt.bfloat16
I32 = mybir.dt.int32
AF = mybir.ActivationFunctionType
ALU = mybir.AluOpType
AX = mybir.AxisListType

NCORES = 8
D = 4096
S = 4096
B = 2
TC = 1024
NKT = D // 128
NEG = float(np.finfo(np.float32).min)
PI = float(np.pi)
TWO_PI = 2.0 * PI
K_TOP = 410


class Mi(dict):
    strict = False

    def __missing__(self, key):
        if self.strict:
            raise KeyError(key)
        return 0


class FakeInstr:
    def then_inc(self, *a, **k):
        return self


class FakeCEng:

    def __init__(self):
        self.n = 0

    def wait(self, other, val=None):
        pass

    def un(self, name):
        return lambda *a, **k: FakeInstr()

    def __getattr__(self, name):
        def call(*a, **k):
            self.n += 1
            return FakeInstr()
        return call


class CEng:

    def __init__(self, e, semh, auto=False):
        self.__dict__["_e"] = e
        self.__dict__["sem"] = semh
        self.__dict__["n"] = 0
        self.__dict__["auto"] = auto

    def wait(self, other, val=None):
        if isinstance(other, CEng):
            self._e.wait_ge(other.sem, other.n if val is None else val)
        else:
            self._e.wait_ge(other, val)

    def un(self, name):
        return getattr(self._e, name)

    def __getattr__(self, name):
        fn = getattr(self._e, name)

        def call(*a, **k):
            if self.auto and self.n > 0:
                self._e.wait_ge(self.sem, self.n)
            instr = fn(*a, **k)
            instr.then_inc(self.sem, 1)
            self.__dict__["n"] += 1
            return instr
        return call


def _w_prep_lhsT(WT):
    K, M = WT.shape
    assert M % 128 == 0 and K % 128 == 0
    v = WT.reshape(K // 128, 128, M // 128, 128).transpose(2, 1, 0, 3)
    return np.ascontiguousarray(v.reshape(M // 128, 128, K)).astype(np.float32)


def _range_reduce(V, x, shift, out, qint, qfl):
    V.tensor_scalar(out=out[:], in0=x[:], scalar1=float(shift), scalar2=1.0 / TWO_PI,
                    op0=ALU.add, op1=ALU.mult)
    V.wait(V, V.n)
    V.tensor_copy(out=qint[:], in_=out[:])
    V.wait(V, V.n)
    V.tensor_copy(out=qfl[:], in_=qint[:])
    V.wait(V, V.n)
    V.tensor_scalar(out=qfl[:], in0=qfl[:], scalar1=-TWO_PI, scalar2=None, op0=ALU.mult)
    V.tensor_scalar(out=out[:], in0=x[:], scalar1=float(shift), scalar2=None, op0=ALU.add)
    V.wait(V, V.n)
    V.tensor_tensor(out=out[:], in0=out[:], in1=qfl[:], op=ALU.add)
    V.wait(V, V.n)
    V.tensor_scalar(out=qfl[:], in0=out[:], scalar1=PI, scalar2=-TWO_PI, op0=ALU.is_gt, op1=ALU.mult)
    V.wait(V, V.n)
    V.tensor_tensor(out=out[:], in0=out[:], in1=qfl[:], op=ALU.add)
    V.wait(V, V.n)
    V.tensor_scalar(out=qfl[:], in0=out[:], scalar1=-PI, scalar2=TWO_PI, op0=ALU.is_lt, op1=ALU.mult)
    V.wait(V, V.n)
    V.tensor_tensor(out=out[:], in0=out[:], in1=qfl[:], op=ALU.add)



def build_launch_A():
    nc = bass.Bass("TRN2", target_bir_lowering=False)
    x_sl = nc.dram_tensor("x_sl", [D, TC], F32, kind="ExternalInput").ap()
    wqkv = nc.dram_tensor("wqkv", [48, 128, D], F32, kind="ExternalInput").ap()
    tb = nc.dram_tensor("tb", [128, TC], F32, kind="ExternalInput").ap()
    jot = nc.dram_tensor("jot", [128, 1], F32, kind="ExternalInput").ap()
    onesk2 = nc.dram_tensor("onesk2", [2, 128], F32, kind="ExternalInput").ap()
    onescol = nc.dram_tensor("onescol", [128, 1], F32, kind="ExternalInput").ap()
    o_xn = nc.dram_tensor("o_xn", [D, TC], BF16, kind="ExternalOutput").ap()
    o_qkv = nc.dram_tensor("o_qkv", [6144, TC], BF16, kind="ExternalOutput").ap()

    NM = 48
    with ExitStack() as ctx:
        sbt = lambda name, shape, dt=F32: ctx.enter_context(nc.sbuf_tensor(name, shape, dt))
        xr = [sbt(f"xr{i}", [128, TC]) for i in range(2)]
        scr = [sbt(f"scr{i}", [128, TC]) for i in range(2)]
        xnbf = sbt("xnbf", [128, NKT * TC], BF16)
        murow = sbt("murow", [1, TC]); varrow = sbt("varrow", [1, TC])
        sqmu = sbt("sqmu", [1, TC]); rstdrow = sbt("rstdrow", [1, TC])
        rows2 = sbt("rows2", [2, TC])
        mub = sbt("mub", [128, TC]); rstdb = sbt("rstdb", [128, TC])
        ones2 = sbt("ones2", [2, 128]); onec = sbt("onec", [128, 1])
        tbt = sbt("tbt", [128, TC]); jt = sbt("jt", [128, 1])
        invc = sbt("invc", [128, 1])
        cost = sbt("cost", [128, TC]); sint = sbt("sint", [128, TC])
        qint = sbt("qint", [128, TC], I32)
        wst = sbt("wst", [128, D])
        wbf = [sbt(f"wbf{i}", [128, D], BF16) for i in range(2)]
        rott = sbt("rott", [128, TC]); arg = rott
        tmpa = sbt("tmpa", [128, TC]); tmpb = sbt("tmpb", [128, TC])
        outbf = [sbt(f"outbf{i}", [128, TC], BF16) for i in range(2)]

        ps_b = ctx.enter_context(nc.psum_tensor("ps_b", [128, TC], F32))
        ps_o = [ctx.enter_context(nc.psum_tensor(f"ps_o{i}", [128, TC], F32)) for i in range(2)]
        ps_mu = ps_o[0][0:1, :]
        ps_sq = ps_o[1][0:1, :]

        sem = lambda name: ctx.enter_context(nc.semaphore(name))
        ld1 = sem("ld1")
        lx = [sem("lxa"), sem("lxb")]; l2 = [sem("l2a"), sem("l2b")]
        stq = [sem("stqa"), sem("stqb")]; stdx = sem("stdx"); wld = sem("wld")
        vsem = sem("vsem"); ssem = sem("ssem"); tsem = sem("tsem")
        block = ctx.enter_context(nc.Block())
        mi = Mi()

        def emit_scalar(E):
            for kt in range(NKT):
                E.wait(lx[kt % 2], 16 * (kt // 2 + 1))
                if kt >= 2:
                    E.wait(tsem, mi[("sqmm", kt - 2)])
                E.activation(out=scr[kt % 2][:], in_=xr[kt % 2][:], func=AF.Square)
                mi[("sq", kt)] = E.n
            E.wait(vsem, mi["murow"])
            E.activation(out=sqmu[:], in_=murow[:], func=AF.Square)
            mi["sqmu"] = E.n
            E.wait(vsem, mi["varrow"])
            E.activation(out=varrow[:], in_=varrow[:], func=AF.Ln)
            E.wait(E, E.n)
            E.activation(out=rstdrow[:], in_=varrow[:], func=AF.Exp, scale=-0.5)
            mi["rstd"] = E.n
            E.wait(ld1, 64)
            E.activation(out=invc[:], in_=jt[:], func=AF.Exp,
                         scale=-2.0 * math.log(10000.0) / 128.0)
            mi["invc"] = E.n
            E.wait(vsem, mi["sarg"])
            E.activation(out=sint[:], in_=arg[:], func=AF.Sin)
            mi["sint"] = E.n
            E.wait(vsem, mi["carg"])
            E.activation(out=cost[:], in_=arg[:], func=AF.Sin)
            mi["cost"] = E.n

        def emit_tensor(E):
            E.wait(ld1, 64)
            for kt in range(NKT):
                E.wait(lx[kt % 2], 16 * (kt // 2 + 1))
                E.un("matmul")(ps_mu[:, 0:512], onec[:], xr[kt % 2][:, 0:512],
                               start=(kt == 0), stop=(kt == NKT - 1))
                E.matmul(ps_mu[:, 512:1024], onec[:], xr[kt % 2][:, 512:1024],
                         start=(kt == 0), stop=(kt == NKT - 1))
                mi[("mu", kt)] = E.n
                E.wait(ssem, mi[("sq", kt)])
                E.un("matmul")(ps_sq[:, 0:512], onec[:], scr[kt % 2][:, 0:512],
                               start=(kt == 0), stop=(kt == NKT - 1))
                E.matmul(ps_sq[:, 512:1024], onec[:], scr[kt % 2][:, 512:1024],
                         start=(kt == 0), stop=(kt == NKT - 1))
                mi[("sqmm", kt)] = E.n
            E.wait(vsem, mi["negmu"])
            E.un("matmul")(ps_b[:, 0:512], ones2[:], rows2[:, 0:512], start=True, stop=True)
            E.matmul(ps_b[:, 512:1024], ones2[:], rows2[:, 512:1024], start=True, stop=True)
            mi["bc_mu"] = E.n
            E.wait(vsem, mi["rows_rstd"])
            E.un("matmul")(ps_b[:, 0:512], ones2[:], rows2[:, 0:512], start=True, stop=True)
            E.matmul(ps_b[:, 512:1024], ones2[:], rows2[:, 512:1024], start=True, stop=True)
            mi["bc_rstd"] = E.n
            E.wait(vsem, mi[("xn", NKT - 1)])
            for m in range(NM):
                E.wait(vsem, mi[("wcast", m)])
                if m >= 2:
                    E.wait(vsem, mi[("evac", m - 2)])
                pso = ps_o[m % 2]
                for h in range(2):
                    for kt in range(NKT):
                        last = (h == 1 and kt == NKT - 1)
                        f = E.matmul if last else E.un("matmul")
                        f(pso[:, h * 512:(h + 1) * 512],
                          wbf[m % 2][:, kt * 128:(kt + 1) * 128],
                          xnbf[:, kt * TC + h * 512:kt * TC + (h + 1) * 512],
                          start=(kt == 0), stop=(kt == NKT - 1))
                mi[("mm", m)] = E.n

        def emit_vector(E):
            E.wait(tsem, mi[("mu", NKT - 1)])
            E.tensor_scalar(out=murow[:], in0=ps_mu[:], scalar1=1.0 / D, scalar2=None, op0=ALU.mult)
            mi["murow"] = E.n
            E.wait(ssem, mi["sqmu"])
            E.wait(tsem, mi[("sqmm", NKT - 1)])
            E.tensor_scalar(out=varrow[:], in0=ps_sq[:], scalar1=1.0 / D, scalar2=None, op0=ALU.mult)
            E.wait(E, E.n)
            E.tensor_tensor(out=varrow[:], in0=varrow[:], in1=sqmu[:], op=ALU.subtract)
            E.wait(E, E.n)
            E.tensor_scalar(out=varrow[:], in0=varrow[:], scalar1=1e-5, scalar2=None, op0=ALU.add)
            mi["varrow"] = E.n
            E.memset(rows2[:], 0.0)
            E.wait(E, E.n)
            E.tensor_scalar(out=rows2[0:1, :], in0=murow[:], scalar1=-1.0, scalar2=None, op0=ALU.mult)
            mi["negmu"] = E.n
            E.wait(tsem, mi["bc_mu"])
            E.tensor_copy(out=mub[:], in_=ps_b[:])
            mi["mub"] = E.n
            E.wait(ssem, mi["rstd"])
            E.wait(E, E.n)
            E.tensor_copy(out=rows2[0:1, :], in_=rstdrow[:])
            mi["rows_rstd"] = E.n
            E.wait(tsem, mi["bc_rstd"])
            E.tensor_copy(out=rstdb[:], in_=ps_b[:])
            mi["stats_done"] = E.n
            E.wait(ssem, mi["invc"])
            E.wait(ld1, 64)
            E.tensor_scalar(out=tbt[:], in0=tbt[:], scalar1=invc[:], scalar2=None, op0=ALU.mult)
            E.wait(E, E.n)
            _range_reduce(E, tbt, 0.0, arg, qint, tmpb)
            mi["sarg"] = E.n
            E.wait(ssem, mi["sint"])
            _range_reduce(E, tbt, PI / 2, arg, qint, tmpb)
            mi["carg"] = E.n
            for kt in range(NKT):
                E.wait(l2[kt % 2], 16 * (kt // 2 + 1))
                E.tensor_tensor(out=tmpa[:], in0=xr[kt % 2][:], in1=mub[:], op=ALU.add)
                E.wait(E, E.n)
                E.tensor_tensor(out=xnbf[:, kt * TC:(kt + 1) * TC], in0=tmpa[:], in1=rstdb[:],
                                op=ALU.mult)
                mi[("xn", kt)] = E.n
            E.wait(ssem, mi["cost"])

            def evac_rope(m):
                E.wait(tsem, mi[("mm", m)])
                if m >= 2:
                    E.wait(stq[m % 2], 16 * (m // 2))
                pso = ps_o[m % 2]
                E.tensor_copy(out=scr[m % 2][:], in_=pso[:])
                mi[("evac", m)] = E.n
                q = scr[m % 2]
                E.wait(E, E.n)
                if m < 40:
                    E.tensor_scalar(out=rott[0:64, :], in0=q[64:128, :], scalar1=-1.0,
                                    scalar2=None, op0=ALU.mult)
                    E.tensor_copy(out=rott[64:128, :], in_=q[0:64, :])
                    E.tensor_tensor(out=tmpa[:], in0=q[:], in1=cost[:], op=ALU.mult)
                    E.wait(E, E.n)
                    E.tensor_tensor(out=tmpb[:], in0=rott[:], in1=sint[:], op=ALU.mult)
                    E.wait(E, E.n)
                    E.tensor_tensor(out=outbf[m % 2][:], in0=tmpa[:], in1=tmpb[:], op=ALU.add)
                else:
                    E.tensor_copy(out=outbf[m % 2][:], in_=q[:])
                mi[("rope", m)] = E.n

            for m in range(NM):
                if m >= 2:
                    E.wait(tsem, mi[("mm", m - 2)])
                E.wait(wld, 16 * (m + 1))
                E.tensor_copy(out=wbf[m % 2][:], in_=wst[:])
                mi[("wcast", m)] = E.n
                if m >= 1:
                    evac_rope(m - 1)
            evac_rope(NM - 1)

        emit_scalar(FakeCEng()); emit_tensor(FakeCEng()); emit_vector(FakeCEng())
        mi.strict = True

        @block.sync
        def _(e):
            e.dma_start(out=ones2[:], in_=onesk2).then_inc(ld1, 16)
            e.dma_start(out=onec[:], in_=onescol).then_inc(ld1, 16)
            e.dma_start(out=tbt[:], in_=tb).then_inc(ld1, 16)
            e.dma_start(out=jt[:], in_=jot).then_inc(ld1, 16)
            e.dma_start(out=wst[:], in_=wqkv[0]).then_inc(wld, 16)
            for kt in range(NKT):
                if kt >= 2:
                    e.wait_ge(tsem, mi[("mu", kt - 2)])
                    e.wait_ge(ssem, mi[("sq", kt - 2)])
                e.dma_start(out=xr[kt % 2][:], in_=x_sl[kt * 128:(kt + 1) * 128, :]).then_inc(lx[kt % 2], 16)
            e.wait_ge(vsem, mi["stats_done"])
            for kt in range(NKT):
                if kt >= 2:
                    e.wait_ge(vsem, mi[("xn", kt - 2)])
                e.dma_start(out=xr[kt % 2][:], in_=x_sl[kt * 128:(kt + 1) * 128, :]).then_inc(l2[kt % 2], 16)
            for kt in range(NKT):
                e.wait_ge(vsem, mi[("xn", kt)])
                e.dma_start(out=o_xn[kt * 128:(kt + 1) * 128, :],
                            in_=xnbf[:, kt * TC:(kt + 1) * TC]).then_inc(stdx, 16)
            for m in range(1, NM + 1):
                if m < NM:
                    e.wait_ge(vsem, mi[("wcast", m - 1)])
                    e.dma_start(out=wst[:], in_=wqkv[m]).then_inc(wld, 16)
                sm = m - 1
                e.wait_ge(vsem, mi[("rope", sm)])
                e.dma_start(out=o_qkv[sm * 128:(sm + 1) * 128, :],
                            in_=outbf[sm % 2][:]).then_inc(stq[sm % 2], 16)
            e.wait_ge(stdx, 16 * NKT)
            e.wait_ge(stq[0], 16 * (NM // 2))
            e.wait_ge(stq[1], 16 * (NM // 2))

        @block.scalar
        def _(e):
            emit_scalar(CEng(e, ssem, auto=True))

        @block.tensor
        def _(e):
            emit_tensor(CEng(e, tsem))

        @block.vector
        def _(e):
            emit_vector(CEng(e, vsem, auto=True))
    return nc


def host_prep_A(inputs):
    x0 = inputs["L0_attn_hidden_states"]
    xT = np.ascontiguousarray(x0.reshape(B * S, D).T).astype(np.float32)
    Wq, Wk, Wv = inputs["L0_attn_Wq"], inputs["L0_attn_Wk"], inputs["L0_attn_Wv"]
    Wcat_T = np.concatenate([Wq.T, Wk.T, Wv.T], axis=1).astype(np.float32)
    wqkv = _w_prep_lhsT(Wcat_T)
    onesk2 = np.zeros((2, 128), np.float32); onesk2[0] = 1.0
    onescol = np.ones((128, 1), np.float32)
    jot = (np.arange(128) % 64).astype(np.float32).reshape(128, 1)
    in_maps = []
    for c in range(NCORES):
        s0 = (c % 4) * TC
        tbv = np.broadcast_to(np.arange(s0, s0 + TC, dtype=np.float32), (128, TC)).copy()
        in_maps.append({
            "x_sl": np.ascontiguousarray(xT[:, c * TC:(c + 1) * TC]),
            "wqkv": wqkv, "tb": tbv, "jot": jot, "onesk2": onesk2, "onescol": onescol,
        })
    return in_maps



SCALE_ATT = 1.0 / math.sqrt(128.0)
NIT_SEL = 6
GAMMA_SEL = [1.0, 1.0, 0.85, 0.7, 0.6, 0.55, 0.5, 0.45, 0.4, 0.35, 0.3, 0.3,
             0.3, 0.3, 0.3, 0.3, 0.3, 0.3, 0.3, 0.3]
BIGM = 1.0e37
NEGF = -1.0e38


def build_launch_B(npairs=8, nit=NIT_SEL, cut=0):
    nc = bass.Bass("TRN2", target_bir_lowering=False)
    qh = nc.dram_tensor("qh", [8, 128, S], BF16, kind="ExternalInput").ap()
    kh = nc.dram_tensor("kh", [2, 128, S], BF16, kind="ExternalInput").ap()
    ktokp = nc.dram_tensor("ktokp", [2, 128, S], BF16, kind="ExternalInput").ap()
    vtokp = nc.dram_tensor("vtokp", [2, 128, S], BF16, kind="ExternalInput").ap()
    cmaskd = nc.dram_tensor("cmaskd", [128, 4 * 512], F32, kind="ExternalInput").ap()
    cmask2d = nc.dram_tensor("cmask2d", [128, 4 * 512], F32, kind="ExternalInput").ap()
    ztgt_d = nc.dram_tensor("ztgt_d", [128, 32], F32, kind="ExternalInput").ap()
    gain_d = nc.dram_tensor("gain_d", [128, 32], F32, kind="ExternalInput").ap()
    ninv_d = nc.dram_tensor("ninv_d", [128, 32], F32, kind="ExternalInput").ap()
    kamask_d = nc.dram_tensor("kamask_d", [128, 4], F32, kind="ExternalInput").ap()
    ones128b_d = nc.dram_tensor("ones128b_d", [128, 1], BF16, kind="ExternalInput").ap()
    onesk2f_d = nc.dram_tensor("onesk2f_d", [2, 128], F32, kind="ExternalInput").ap()
    i128f_d = nc.dram_tensor("i128f_d", [128, 128], F32, kind="ExternalInput").ap()
    o_att = nc.dram_tensor("o_att", [npairs * 128, S], BF16, kind="ExternalOutput").ap()

    NG = 8
    with ExitStack() as ctx:
        sbt = lambda name, shape, dt=F32: ctx.enter_context(nc.sbuf_tensor(name, shape, dt))
        qb = [sbt(f"qb{i}", [128, S], BF16) for i in range(2)]
        kb = sbt("kb", [128, S], BF16)
        ktk = sbt("ktk", [128, S], BF16)
        vtk = sbt("vtk", [128, S], BF16)
        srow = [sbt(f"srow{i}", [128, S]) for i in range(4)]
        junk = sbt("junk", [128, S])
        gsb = sbt("gsb", [128, 128]); gbf = sbt("gbf", [128, 128], BF16)
        ksum = sbt("ksum", [128, 1]); ksumb = sbt("ksumb", [128, 1], BF16)
        kpart = sbt("kpart", [128, 1])
        qg = sbt("qg", [128, 128], BF16)
        scol = sbt("scol", [128, 8])
        thcol = sbt("thcol", [128, 4]); g0col = sbt("g0col", [128, 4])
        mucol = sbt("mucol", [128, 4]); sgcol = sbt("sgcol", [128, 4])
        vrcol = sbt("vrcol", [128, 4]); sqmu4 = sbt("sqmu4", [128, 4])
        erra = sbt("erra", [128, 4]); errb = sbt("errb", [128, 4])
        dfrz = sbt("dfrz", [128, 4]); cnt = sbt("cnt", [128, 4])
        nthc = sbt("nthc", [128, 4])
        scr4 = sbt("scr4", [128, 4]); scr4b = sbt("scr4b", [128, 4])
        rmx = sbt("rmx", [128, 4]); ka4 = sbt("ka4", [128, 4])
        zt_s = sbt("zt_s", [128, 32]); gn_s = sbt("gn_s", [128, 32]); ni_s = sbt("ni_s", [128, 32])
        cm_s = sbt("cm_s", [128, 4 * 512]); cm2_s = sbt("cm2_s", [128, 4 * 512])
        ones1b = sbt("ones1b", [128, 1], BF16); ok2f = sbt("ok2f", [2, 128])
        i128f = sbt("i128f", [128, 128])
        throw = sbt("throw", [2, 512])
        thb = sbt("thb", [128, 512])
        argt = [sbt(f"argt{i}", [128, 1024]) for i in range(2)]
        mt = [sbt(f"mt{i}", [128, 1024]) for i in range(2)]
        pt = [sbt(f"pt{i}", [128, 1024], BF16) for i in range(2)]
        zb = sbt("zb", [128, 512])
        outb = [sbt(f"outb{i}", [128, 512], BF16) for i in range(2)]

        ps_s = ctx.enter_context(nc.psum_tensor("ps_s", [128, 1024], F32))
        ps_pv = ctx.enter_context(nc.psum_tensor("ps_pv", [128, 512], F32))
        ps_g = ctx.enter_context(nc.psum_tensor("ps_g", [128, 128], F32))
        ps_st = ctx.enter_context(nc.psum_tensor("ps_st", [128, 8], F32))
        ps_row = ctx.enter_context(nc.psum_tensor("ps_row", [1, 512], F32))
        ps_z = ctx.enter_context(nc.psum_tensor("ps_z", [1, 512], F32))
        ps_zb = ctx.enter_context(nc.psum_tensor("ps_zb", [128, 512], F32))

        sem = lambda name: ctx.enter_context(nc.semaphore(name))
        ldc = sem("ldc")
        lq = [sem("lqa"), sem("lqb")]; lkv = sem("lkv")
        stq = [sem("stqa"), sem("stqb")]
        vsem = sem("vsem"); ssem = sem("ssem"); tsem = sem("tsem")
        block = ctx.enter_context(nc.Block())
        mi = Mi()

        NLDC = 9
        sbank_last = {}

        def emit_tensor(E):
            sbank_last.clear()
            E.wait(ldc, 16 * NLDC)

            def phaseA_early(p, g):
                if g == 0:
                    E.wait(lq[p % 2], 16 * (p // 2 + 1))
                    if p % 4 == 0:
                        E.wait(lkv, 16 * 3 * (p // 4 + 1))
                E.wait(vsem, mi[("gfree", p, g)])
                for j in range(4 * g, 4 * g + 4):
                    E.matmul(ps_g[:], ktk[:, j * 128:(j + 1) * 128],
                             ktk[:, j * 128:(j + 1) * 128],
                             start=(j % 4 == 0), stop=(j % 4 == 3))
                mi[("gram", p, g)] = E.n
                for d in range(4):
                    qt = 4 * g + d
                    lhs = qb[p % 2][:, qt * 128:(qt + 1) * 128]
                    for ch in range(g + 1):
                        bank = ch % 2
                        if bank in sbank_last:
                            E.wait(vsem, mi[sbank_last[bank]])
                        E.matmul(ps_s[:, bank * 512:bank * 512 + 512], lhs,
                                 kb[:, ch * 512:(ch + 1) * 512], start=True, stop=True)
                        mi[("smm", p, g, d, ch)] = E.n
                        sbank_last[bank] = ("sevac", p, g, d, ch)
                E.wait(vsem, mi[("gcast", p, g)])
                for d in range(4):
                    qt = 4 * g + d
                    lhs = qb[p % 2][:, qt * 128:(qt + 1) * 128]
                    E.matmul(ps_st[:, d:d + 1], lhs, ksumb[:], start=True, stop=True)
                    mi[("r1", p, g, d)] = E.n
                    E.wait(vsem, mi[("qgfree", p, g, d)])
                    E.matmul(ps_g[:], gbf[:], lhs, start=True, stop=True)
                    mi[("gq", p, g, d)] = E.n
                    E.wait(vsem, mi[("qgm", p, g, d)])
                    E.matmul(ps_st[:, 4 + d:5 + d], qg[:], ones1b[:], start=True, stop=True)
                    mi[("r2", p, g, d)] = E.n

            def phaseA_late(p, g):
                for d in range(4):
                    E.wait(vsem, mi[("thdone", p, g)] if d == 0 else mi[("thsg", p, g, d - 1)])
                    E.matmul(ps_row[:, 0:128], thcol[:, d:d + 1],
                             i128f[:], start=True, stop=True)
                    mi[("thT", p, g, d)] = E.n
                E.wait(vsem, mi[("throw_asm", p, g)])
                E.matmul(ps_zb[:], ok2f[:], throw[:], start=True, stop=True)
                mi[("thbc", p, g)] = E.n

            def phaseB(p, g):
                njt = 4 * g + 4
                for j0 in range(0, njt, 2):
                    for jj in range(min(2, njt - j0)):
                        j = j0 + jj
                        bank = jj
                        if bank in sbank_last:
                            E.wait(vsem, mi[sbank_last[bank]])
                        E.matmul(ps_s[:, jj * 512:(jj + 1) * 512],
                                 kb[:, j * 128:(j + 1) * 128],
                                 qb[p % 2][:, g * 512:(g + 1) * 512],
                                 start=True, stop=True)
                        mi[("stmm", p, g, j)] = E.n
                        sbank_last[bank] = ("argev", p, g, j)
                    if j0 == 0 and (p, g) != (0, 0):
                        pp, pg = (p, g - 1) if g > 0 else (p - 1, NG - 1)
                        E.wait(vsem, mi[("outd", pp, pg)])
                    E.wait(ssem, mi[("exp", p, g, j0)])
                    for jj in range(min(2, njt - j0)):
                        j = j0 + jj
                        E.matmul(ps_pv[:], vtk[:, j * 128:(j + 1) * 128],
                                 pt[(j0 // 2) % 2][:, jj * 512:(jj + 1) * 512],
                                 start=(j == 0), stop=(j == njt - 1))
                        E.matmul(ps_z[:], ones1b[:],
                                 pt[(j0 // 2) % 2][:, jj * 512:(jj + 1) * 512],
                                 start=(j == 0), stop=(j == njt - 1))
                    mi[("pvz", p, g, j0)] = E.n
                E.wait(vsem, mi[("zrows", p, g)])
                E.matmul(ps_zb[:], ok2f[:], throw[:], start=True, stop=True)
                mi[("zbc", p, g)] = E.n
                if g == NG - 1:
                    mi[("pairdone", p)] = E.n

            pendB = None
            for p in range(npairs):
                for g in range(NG):
                    if g == 0 and p % 4 == 0 and pendB is not None:
                        phaseB(*pendB)
                        pendB = None
                    phaseA_early(p, g)
                    if pendB is not None:
                        phaseB(*pendB)
                        pendB = None
                    phaseA_late(p, g)
                    pendB = (p, g)
            phaseB(*pendB)

        def emit_scalar(E):
            expq = []

            def stats(p, g):
                E.wait(vsem, mi[("muc", p, g)])
                E.activation(out=sqmu4[:], in_=mucol[:], func=AF.Square)
                mi[("sqmu", p, g)] = E.n
                E.wait(vsem, mi[("varc", p, g)])
                E.activation(out=sgcol[:], in_=vrcol[:], func=AF.Ln)
                E.activation(out=sgcol[:], in_=sgcol[:], func=AF.Exp, scale=0.5)
                mi[("sig", p, g)] = E.n

            def counts(p, g, it):
                W = (g + 1) * 512
                E.wait(vsem, mi[("nwupd", p, g, it - 1)] if it > 0 else mi[("nwinit", p, g)])
                for d in range(4):
                    E.activation(out=junk[:, 0:W], in_=srow[d][:, 0:W], func=AF.Sign,
                                 bias=nthc[:, d:d + 1], accum_out=cnt[:, d:d + 1])
                mi[("cnt", p, g, it)] = E.n

            def mkexp(p, g, j0):
                def emitf():
                    nw = min(2, 4 * g + 4 - j0) * 512
                    if j0 >= 4:
                        E.wait(tsem, mi[("pvz", p, g, j0 - 4)])
                    E.wait(vsem, mi[("arg2", p, g, j0)])
                    E.activation(out=pt[(j0 // 2) % 2][:, 0:nw],
                                 in_=argt[(j0 // 2) % 2][:, 0:nw],
                                 func=AF.Exp, scale=SCALE_ATT)
                    mi[("exp", p, g, j0)] = E.n
                return emitf

            for p in range(npairs):
                for g in range(NG):
                    if g == 0 and p % 4 == 0:
                        while expq:
                            expq.pop(0)()
                    stats(p, g)
                    for it in range(nit):
                        counts(p, g, it)
                        for _ in range(3):
                            if expq:
                                expq.pop(0)()
                    while expq:
                        expq.pop(0)()
                    expq = [mkexp(p, g, j0) for j0 in range(0, 4 * g + 4, 2)]
            while expq:
                expq.pop(0)()

        def emit_vector(E):
            E.wait(ldc, 16 * NLDC)
            pend = []

            def phaseA_pre(p, g):
                if p % 4 == 0 and g == 0:
                    E.wait(lkv, 16 * 3 * (p // 4 + 1))
                for d in range(4):
                    for ch in range(g + 1):
                        E.wait(tsem, mi[("smm", p, g, d, ch)])
                        dst = srow[d][:, ch * 512:(ch + 1) * 512]
                        src = ps_s[:, (ch % 2) * 512:(ch % 2) * 512 + 512]
                        if ch == g:
                            E.tensor_tensor(out=dst, in0=src,
                                            in1=cm_s[:, d * 512:(d + 1) * 512], op=ALU.add)
                        else:
                            E.tensor_copy(out=dst, in_=src)
                        mi[("sevac", p, g, d, ch)] = E.n
                E.wait(tsem, mi[("gram", p, g)])
                if g == 0:
                    E.tensor_copy(out=gsb[:], in_=ps_g[:])
                else:
                    E.scalar_tensor_tensor(out=gsb[:], in0=ps_g[:], scalar=1.0,
                                           in1=gsb[:], op0=ALU.mult, op1=ALU.add)
                mi[("gevac", p, g)] = E.n
                E.reduce_sum(out=kpart[:], in_=kb[:, g * 512:(g + 1) * 512], axis=AX.X)
                if g == 0:
                    E.tensor_copy(out=ksum[:], in_=kpart[:])
                else:
                    E.tensor_tensor(out=ksum[:], in0=ksum[:], in1=kpart[:], op=ALU.add)
                E.tensor_copy(out=ksumb[:], in_=ksum[:])
                E.tensor_copy(out=gbf[:], in_=gsb[:])
                mi[("gcast", p, g)] = E.n
                for d in range(4):
                    E.wait(tsem, mi[("gq", p, g, d)])
                    E.tensor_tensor(out=qg[:], in0=ps_g[:],
                                    in1=qb[p % 2][:, (4 * g + d) * 128:(4 * g + d + 1) * 128],
                                    op=ALU.mult)
                    mi[("qgm", p, g, d)] = E.n
                    mi[("qgfree", p, g, d + 1)] = E.n
                mi[("qgfree", p, g, 0)] = mi[("gcast", p, g)]
                E.wait(tsem, mi[("r2", p, g, 3)])
                E.tensor_copy(out=scol[:], in_=ps_st[:])
                gs = slice(4 * g, 4 * g + 4)
                E.tensor_tensor(out=mucol[:], in0=scol[:, 0:4], in1=ni_s[:, gs], op=ALU.mult)
                mi[("muc", p, g)] = E.n
                E.tensor_tensor(out=vrcol[:], in0=scol[:, 4:8], in1=ni_s[:, gs], op=ALU.mult)
                E.wait(ssem, mi[("sqmu", p, g)])
                E.tensor_tensor(out=vrcol[:], in0=vrcol[:], in1=sqmu4[:], op=ALU.subtract)
                E.tensor_scalar(out=vrcol[:], in0=vrcol[:], scalar1=1e-4, scalar2=None,
                                op0=ALU.max)
                mi[("varc", p, g)] = E.n
                E.wait(ssem, mi[("sig", p, g)])
                E.tensor_tensor(out=g0col[:], in0=sgcol[:], in1=gn_s[:, gs], op=ALU.mult)
                E.tensor_tensor(out=thcol[:], in0=sgcol[:], in1=zt_s[:, gs], op=ALU.mult)
                E.tensor_tensor(out=thcol[:], in0=thcol[:], in1=mucol[:], op=ALU.add)
                E.tensor_scalar(out=nthc[:], in0=thcol[:], scalar1=-1.0, scalar2=None,
                                op0=ALU.mult)
                E.memset(dfrz[:], 0.0)
                mi[("nwinit", p, g)] = E.n

            def newton_it(p, g, it):
                W = (g + 1) * 512
                E.wait(ssem, mi[("cnt", p, g, it)])
                E.tensor_scalar(out=errb[:], in0=cnt[:], scalar1=0.5,
                                scalar2=float(W) / 2.0 - float(K_TOP),
                                op0=ALU.mult, op1=ALU.add)
                E.tensor_scalar(out=scr4[:], in0=errb[:], scalar1=0.0, scalar2=None,
                                op0=ALU.is_equal)
                E.tensor_tensor(out=dfrz[:], in0=dfrz[:], in1=scr4[:], op=ALU.max)
                E.tensor_scalar(out=scr4[:], in0=dfrz[:], scalar1=-1.0, scalar2=1.0,
                                op0=ALU.mult, op1=ALU.add)
                E.tensor_tensor(out=scr4b[:], in0=errb[:], in1=g0col[:], op=ALU.mult)
                E.tensor_scalar(out=scr4b[:], in0=scr4b[:],
                                scalar1=float(GAMMA_SEL[it]), scalar2=None, op0=ALU.mult)
                E.tensor_tensor(out=scr4b[:], in0=scr4b[:], in1=scr4[:], op=ALU.mult)
                E.tensor_tensor(out=thcol[:], in0=thcol[:], in1=scr4b[:], op=ALU.add)
                E.tensor_scalar(out=nthc[:], in0=thcol[:], scalar1=-1.0, scalar2=None,
                                op0=ALU.mult)
                mi[("nwupd", p, g, it)] = E.n

            def theta(p, g):
                if g == 0:
                    for d in range(4):
                        E.reduce_max(out=rmx[:, d:d + 1], in_=srow[d][:, 0:512], axis=AX.X)
                    E.tensor_scalar(out=rmx[:], in0=rmx[:], scalar1=-700.0, scalar2=None,
                                    op0=ALU.add)
                    E.tensor_tensor(out=rmx[:], in0=rmx[:], in1=ka4[:], op=ALU.mult)
                    E.tensor_scalar(out=scr4[:], in0=ka4[:], scalar1=-1.0, scalar2=1.0,
                                    op0=ALU.mult, op1=ALU.add)
                    E.tensor_tensor(out=thcol[:], in0=thcol[:], in1=scr4[:], op=ALU.mult)
                    E.tensor_tensor(out=thcol[:], in0=thcol[:], in1=rmx[:], op=ALU.add)
                mi[("thdone", p, g)] = E.n
                E.memset(throw[:], 0.0)
                for d in range(4):
                    E.wait(tsem, mi[("thT", p, g, d)])
                    E.tensor_scalar(out=throw[0:1, d * 128:(d + 1) * 128],
                                    in0=ps_row[0:1, 0:128],
                                    scalar1=1.0, scalar2=None, op0=ALU.mult)
                    mi[("thsg", p, g, d)] = E.n
                mi[("throw_asm", p, g)] = E.n
                E.wait(tsem, mi[("thbc", p, g)])
                E.tensor_copy(out=thb[:], in_=ps_zb[:])
                mi[("thbev", p, g)] = E.n

            def mkarg(p, g, j0):
                def emitf():
                    buf = (j0 // 2) % 2
                    if j0 >= 4:
                        E.wait(ssem, mi[("exp", p, g, j0 - 4)])
                    for jj in range(min(2, 4 * g + 4 - j0)):
                        j = j0 + jj
                        E.wait(tsem, mi[("stmm", p, g, j)])
                        E.tensor_tensor(out=argt[buf][:, jj * 512:(jj + 1) * 512],
                                        in0=ps_s[:, jj * 512:(jj + 1) * 512],
                                        in1=thb[:], op=ALU.subtract)
                        if j >= 4 * g:
                            dd = j - 4 * g
                            E.tensor_tensor(out=argt[buf][:, jj * 512:(jj + 1) * 512],
                                            in0=argt[buf][:, jj * 512:(jj + 1) * 512],
                                            in1=cm2_s[:, dd * 512:(dd + 1) * 512], op=ALU.add)
                        mi[("argev", p, g, j)] = E.n
                        E.tensor_scalar(out=mt[buf][:, jj * 512:(jj + 1) * 512],
                                        in0=argt[buf][:, jj * 512:(jj + 1) * 512],
                                        scalar1=0.0, scalar2=-BIGM,
                                        op0=ALU.is_lt, op1=ALU.mult)
                        E.tensor_tensor(out=argt[buf][:, jj * 512:(jj + 1) * 512],
                                        in0=argt[buf][:, jj * 512:(jj + 1) * 512],
                                        in1=mt[buf][:, jj * 512:(jj + 1) * 512], op=ALU.add)
                        mi[("arg2j", p, g, j)] = E.n
                    mi[("arg2", p, g, j0)] = E.n
                return emitf

            def mktail(p, g):
                def emitf():
                    njt = 4 * g + 4
                    E.wait(tsem, mi[("pvz", p, g, ((njt + 1) // 2 - 1) * 2)])
                    E.memset(throw[:], 0.0)
                    E.tensor_scalar(out=throw[0:1, :], in0=ps_z[0:1, :], scalar1=1.0,
                                    scalar2=None, op0=ALU.mult)
                    E.reciprocal(out=throw[0:1, :], in_=throw[0:1, :])
                    mi[("zrows", p, g)] = E.n
                    E.wait(tsem, mi[("zbc", p, g)])
                    E.tensor_copy(out=zb[:], in_=ps_zb[:])
                    nst = 4 * p + g // 2
                    if nst >= 1 and not (p == 0 and g < 2):
                        E.wait(stq[g % 2], 16 * nst)
                    E.tensor_copy(out=mt[0][:, 0:512], in_=ps_pv[:])
                    E.tensor_tensor(out=outb[g % 2][:], in0=mt[0][:, 0:512], in1=zb[:],
                                    op=ALU.mult)
                    mi[("outd", p, g)] = E.n
                return emitf

            for p in range(npairs):
                for g in range(NG):
                    if g == 0 and p % 4 == 0:
                        while pend:
                            pend.pop(0)()
                    phaseA_pre(p, g)
                    for it in range(nit):
                        newton_it(p, g, it)
                        for _ in range(3):
                            if pend:
                                pend.pop(0)()
                    while pend:
                        pend.pop(0)()
                    theta(p, g)
                    pend = [mkarg(p, g, j0) for j0 in range(0, 4 * g + 4, 2)]
                    pend.append(mktail(p, g))
            while pend:
                pend.pop(0)()

        def set_aliases():
            for p in range(npairs):
                for g in range(NG):
                    if p == 0 and g == 0:
                        mi[("gfree", p, g)] = 0
                    elif g == 0:
                        mi[("gfree", p, g)] = mi[("qgm", p - 1, NG - 1, 3)]
                    else:
                        mi[("gfree", p, g)] = mi[("qgm", p, g - 1, 3)]

        def emit_sync(e):
            e.dma_start(out=cm_s[:], in_=cmaskd).then_inc(ldc, 16)
            e.dma_start(out=cm2_s[:], in_=cmask2d).then_inc(ldc, 16)
            e.dma_start(out=zt_s[:], in_=ztgt_d).then_inc(ldc, 16)
            e.dma_start(out=gn_s[:], in_=gain_d).then_inc(ldc, 16)
            e.dma_start(out=ni_s[:], in_=ninv_d).then_inc(ldc, 16)
            e.dma_start(out=ka4[:], in_=kamask_d).then_inc(ldc, 16)
            e.dma_start(out=ones1b[:], in_=ones128b_d).then_inc(ldc, 16)
            e.dma_start(out=ok2f[:], in_=onesk2f_d).then_inc(ldc, 16)
            e.dma_start(out=i128f[:], in_=i128f_d).then_inc(ldc, 16)
            for p in range(min(2, npairs)):
                e.dma_start(out=qb[p][:], in_=qh[p]).then_inc(lq[p], 16)
            e.dma_start(out=kb[:], in_=kh[0]).then_inc(lkv, 16)
            e.dma_start(out=ktk[:], in_=ktokp[0]).then_inc(lkv, 16)
            e.dma_start(out=vtk[:], in_=vtokp[0]).then_inc(lkv, 16)
            for p in range(npairs):
                for g in range(NG):
                    e.wait_ge(vsem, mi[("outd", p, g)])
                    e.dma_start(out=o_att[p * 128:(p + 1) * 128, g * 512:(g + 1) * 512],
                                in_=outb[g % 2][:]).then_inc(stq[g % 2], 16)
                if p + 2 < npairs:
                    e.wait_ge(tsem, mi[("pairdone", p)])
                    e.dma_start(out=qb[p % 2][:], in_=qh[p + 2]).then_inc(lq[p % 2], 16)
                if p % 4 == 3 and p + 1 < npairs:
                    e.wait_ge(tsem, mi[("pairdone", p)])
                    kv = (p + 1) // 4
                    e.dma_start(out=kb[:], in_=kh[kv]).then_inc(lkv, 16)
                    e.dma_start(out=ktk[:], in_=ktokp[kv]).then_inc(lkv, 16)
                    e.dma_start(out=vtk[:], in_=vtokp[kv]).then_inc(lkv, 16)
            for i in range(2):
                tot = sum(1 for p in range(npairs) for g in range(NG) if g % 2 == i)
                e.wait_ge(stq[i], 16 * tot)

        for _ in range(2):
            emit_tensor(FakeCEng()); emit_scalar(FakeCEng()); emit_vector(FakeCEng())
            set_aliases()
        mi.strict = True

        @block.sync
        def _(e):
            emit_sync(e)

        @block.scalar
        def _(e):
            emit_scalar(CEng(e, ssem, auto=True))

        @block.tensor
        def _(e):
            emit_tensor(CEng(e, tsem))

        @block.vector
        def _(e):
            emit_vector(CEng(e, vsem, auto=True))
    return nc


def _ndtri(p):
    from statistics import NormalDist
    return NormalDist().inv_cdf(p)


def host_consts_B():
    cm = np.zeros((128, 4, 512), np.float32)
    cm2 = np.zeros((128, 4, 512), np.float32)
    for dloc in range(4):
        r = np.arange(128)[:, None]
        j = np.arange(512)[None, :]
        cm[:, dloc, :] = np.where(j > 128 * dloc + r, NEGF, 0.0)
        cm2[:, dloc, :] = np.where(j < 128 * dloc + r, NEGF, 0.0)
    ztgt = np.zeros((128, 32), np.float32)
    gain = np.zeros((128, 32), np.float32)
    ninv = np.zeros((128, 32), np.float32)
    for qt in range(32):
        for r in range(128):
            n = qt * 128 + r + 1
            ninv[r, qt] = 1.0 / n
            if n > K_TOP:
                z = _ndtri(1.0 - (K_TOP - 0.5) / n)
                ztgt[r, qt] = z
                phi = math.exp(-0.5 * z * z) / math.sqrt(2 * math.pi)
                gain[r, qt] = 1.0 / (n * phi)
            else:
                ztgt[r, qt] = 0.0
                gain[r, qt] = 0.0
    kam = np.zeros((128, 4), np.float32)
    for d in range(4):
        q = d * 128 + np.arange(128)
        kam[:, d] = (q < K_TOP).astype(np.float32)
    ones128b = np.ones((128, 1), np.float32).astype(ml_dtypes.bfloat16)
    onesk2f = np.zeros((2, 128), np.float32); onesk2f[0] = 1.0
    i128f = np.eye(128, dtype=np.float32)
    return {
        "cmaskd": np.ascontiguousarray(cm.reshape(128, 4 * 512)),
        "cmask2d": np.ascontiguousarray(cm2.reshape(128, 4 * 512)),
        "ztgt_d": ztgt, "gain_d": gain, "ninv_d": ninv, "kamask_d": kam,
        "ones128b_d": ones128b, "onesk2f_d": onesk2f, "i128f_d": i128f,
    }


def _tok_tiles(a):
    return np.ascontiguousarray(a.reshape(32, 128, 128).transpose(1, 0, 2).reshape(128, 4096))


def host_prep_B(qkv_cores):
    consts = host_consts_B()
    qf = [np.concatenate([qkv_cores[4 * b + i][:4096] for i in range(4)], axis=1)
          for b in range(B)]
    kf = [np.concatenate([qkv_cores[4 * b + i][4096:5120] for i in range(4)], axis=1)
          for b in range(B)]
    vf = [np.concatenate([qkv_cores[4 * b + i][5120:6144] for i in range(4)], axis=1)
          for b in range(B)]
    in_maps = []
    for c in range(NCORES):
        b, hs = c // 4, (c % 4) * 8
        kvs = hs // 4
        qh = np.ascontiguousarray(qf[b].reshape(32, 128, S)[hs:hs + 8])
        kh = np.ascontiguousarray(kf[b].reshape(8, 128, S)[kvs:kvs + 2])
        ktokp = np.stack([_tok_tiles(kh[i].T) for i in range(2)])
        vh = vf[b].reshape(8, 128, S)[kvs:kvs + 2]
        vtokp = np.stack([_tok_tiles(vh[i].T) for i in range(2)])
        m = dict(consts)
        m.update({"qh": qh, "kh": kh,
                  "ktokp": np.ascontiguousarray(ktokp),
                  "vtokp": np.ascontiguousarray(vtokp)})
        in_maps.append(m)
    return in_maps



def build_launch_C1():
    nc = bass.Bass("TRN2", target_bir_lowering=False)
    hT = nc.dram_tensor("hT", [D, TC], BF16, kind="ExternalInput").ap()
    x0f = nc.dram_tensor("x0f", [D, TC], F32, kind="ExternalInput").ap()
    wwo = nc.dram_tensor("wwo", [32, 128, D], F32, kind="ExternalInput").ap()
    o_x1p = nc.dram_tensor("o_x1p", [D, TC], F32, kind="ExternalOutput").ap()
    NM = 32
    with ExitStack() as ctx:
        sbt = lambda name, shape, dt=F32: ctx.enter_context(nc.sbuf_tensor(name, shape, dt))
        hb = sbt("hb", [128, NKT * TC], BF16)
        wst = sbt("wst", [128, D])
        wbf = [sbt(f"wbf{i}", [128, D], BF16) for i in range(2)]
        x0t = [sbt(f"x0t{i}", [128, TC]) for i in range(2)]
        x1f = [sbt(f"x1f{i}", [128, TC]) for i in range(2)]
        ps_o = [ctx.enter_context(nc.psum_tensor(f"ps_o{i}", [128, TC], F32)) for i in range(2)]
        sem = lambda name: ctx.enter_context(nc.semaphore(name))
        ldc = sem("ldc"); lw = sem("lw"); lx = [sem("lxa"), sem("lxb")]
        st1 = [sem("st1a"), sem("st1b")]
        vsem = sem("vsem"); ssem = sem("ssem"); tsem = sem("tsem")
        block = ctx.enter_context(nc.Block())
        mi = Mi()

        def emit_scalar(E):
            for m in range(NM):
                E.wait(lw, 16 * (m + 1))
                if m >= 2:
                    E.wait(tsem, mi[("mm", m - 1)])
                E.activation(out=wbf[m % 2][:], in_=wst[:], func=AF.Identity)
                mi[("wcast", m)] = E.n

        def emit_tensor(E):
            E.wait(ldc, 16 * NKT)
            for m in range(NM):
                E.wait(ssem, mi[("wcast", m)])
                if m >= 2:
                    E.wait(vsem, mi[("x1", m - 2)])
                pso = ps_o[m % 2]
                for h in range(2):
                    for kt in range(NKT):
                        last = (h == 1 and kt == NKT - 1)
                        f = E.matmul if last else E.un("matmul")
                        f(pso[:, h * 512:(h + 1) * 512],
                          wbf[m % 2][:, kt * 128:(kt + 1) * 128],
                          hb[:, kt * TC + h * 512:kt * TC + (h + 1) * 512],
                          start=(kt == 0), stop=(kt == NKT - 1))
                mi[("mm", m)] = E.n

        def emit_vector(E):
            for m in range(NM):
                E.wait(tsem, mi[("mm", m)])
                E.wait(lx[m % 2], 16 * (m // 2 + 1))
                if m >= 2:
                    E.wait(st1[m % 2], 16 * (m // 2))
                E.tensor_tensor(out=x1f[m % 2][:], in0=ps_o[m % 2][:], in1=x0t[m % 2][:],
                                op=ALU.add)
                mi[("x1", m)] = E.n

        def emit_sync(e):
            for i in range(NKT):
                e.dma_start(out=hb[:, i * TC:(i + 1) * TC],
                            in_=hT[i * 128:(i + 1) * 128, :]).then_inc(ldc, 16)
            e.dma_start(out=wst[:], in_=wwo[0]).then_inc(lw, 16)
            for m in range(NM):
                if m + 1 < NM:
                    e.wait_ge(ssem, mi[("wcast", m)])
                    e.dma_start(out=wst[:], in_=wwo[m + 1]).then_inc(lw, 16)
                if m >= 2:
                    e.wait_ge(vsem, mi[("x1", m - 2)])
                e.dma_start(out=x0t[m % 2][:], in_=x0f[m * 128:(m + 1) * 128, :]).then_inc(lx[m % 2], 16)
                e.wait_ge(vsem, mi[("x1", m)])
                e.dma_start(out=o_x1p[m * 128:(m + 1) * 128, :],
                            in_=x1f[m % 2][:]).then_inc(st1[m % 2], 16)
            e.wait_ge(st1[0], 16 * 16); e.wait_ge(st1[1], 16 * 16)

        for _ in range(2):
            emit_scalar(FakeCEng()); emit_tensor(FakeCEng()); emit_vector(FakeCEng())
        mi.strict = True

        @block.sync
        def _(e):
            emit_sync(e)

        @block.scalar
        def _(e):
            emit_scalar(CEng(e, ssem, auto=True))

        @block.tensor
        def _(e):
            emit_tensor(CEng(e, tsem))

        @block.vector
        def _(e):
            emit_vector(CEng(e, vsem, auto=True))
    return nc


def host_prep_C1(inputs, att_cores):
    x0 = inputs["L0_attn_hidden_states"]
    xT = np.ascontiguousarray(x0.reshape(B * S, D).T).astype(np.float32)
    wwo = _w_prep_lhsT(np.ascontiguousarray(inputs["L0_attn_Wo"].T).astype(np.float32))
    in_maps = []
    for c in range(NCORES):
        b, sl = c // 4, slice((c % 4) * TC, (c % 4) * TC + TC)
        hT = np.concatenate([att_cores[4 * b + i][:, sl] for i in range(4)], axis=0)
        in_maps.append({
            "hT": np.ascontiguousarray(hT),
            "x0f": np.ascontiguousarray(xT[:, c * TC:(c + 1) * TC]),
            "wwo": wwo,
        })
    return in_maps



def build_launch_C2():
    nc = bass.Bass("TRN2", target_bir_lowering=False)
    xn0 = nc.dram_tensor("xn0", [D, TC], BF16, kind="ExternalInput").ap()
    x1p = nc.dram_tensor("x1p", [D, TC], F32, kind="ExternalInput").ap()
    wkr = nc.dram_tensor("wkr", [32, 128, D], F32, kind="ExternalInput").ap()
    aeff_d = nc.dram_tensor("aeff_d", [16, 128, 2048], F32, kind="ExternalInput").ap()
    ashare_d = nc.dram_tensor("ashare_d", [128, 2048], F32, kind="ExternalInput").ap()
    beff_d = nc.dram_tensor("beff_d", [16, 32, D], F32, kind="ExternalInput").ap()
    bshare_d = nc.dram_tensor("bshare_d", [32, D], F32, kind="ExternalInput").ap()
    gate_d = nc.dram_tensor("gate_d", [128, 16], F32, kind="ExternalInput").ap()
    bias_d = nc.dram_tensor("bias_d", [128, 32], F32, kind="ExternalInput").ap()
    onescol = nc.dram_tensor("onescol", [128, 1], F32, kind="ExternalInput").ap()
    onesk2 = nc.dram_tensor("onesk2", [2, 128], F32, kind="ExternalInput").ap()
    o_x1 = nc.dram_tensor("o_x1", [D, TC], F32, kind="ExternalOutput").ap()
    o_xn1 = nc.dram_tensor("o_xn1", [D, TC], BF16, kind="ExternalOutput").ap()
    NM = 32
    with ExitStack() as ctx:
        sbt = lambda name, shape, dt=F32: ctx.enter_context(nc.sbuf_tensor(name, shape, dt))
        xnb = sbt("xnb", [128, NKT * TC], BF16)
        wst = sbt("wst", [128, D])
        wbf = [sbt(f"wbf{i}", [128, D], BF16) for i in range(2)]
        aeff = sbt("aeff", [128, 2048]); aeb = sbt("aeb", [128, 2048], BF16)
        xnout_t = aeb
        astg = sbt("astg", [128, 2048])
        beff = sbt("beff", [64, D]); beb = sbt("beb", [64, D], BF16)
        bstg = sbt("bstg", [32, D])
        gtv = sbt("gtv", [128, 16]); biasl = sbt("biasl", [128, 32])
        axb = sbt("axb", [64, TC], BF16)
        x1pt = [sbt(f"x1pt{i}", [128, TC]) for i in range(2)]
        x1f = [sbt(f"x1f{i}", [128, TC]) for i in range(2)]
        sq = sbt("sq", [128, TC])
        murow = sbt("murow", [1, TC]); varrow = sbt("varrow", [1, TC])
        sqmu = sbt("sqmu", [1, TC]); rstdrow = sbt("rstdrow", [1, TC])
        rows2 = sbt("rows2", [2, TC])
        mub = sbt("mub", [128, TC]); rstdb = sbt("rstdb", [128, TC])
        onec = sbt("onec", [128, 1]); ones2 = sbt("ones2", [2, 128])
        tmpa = sq
        xnout = [xnout_t[:, 0:TC], xnout_t[:, TC:2 * TC]]
        ps_o = [ctx.enter_context(nc.psum_tensor(f"ps_o{i}", [128, TC], F32)) for i in range(2)]
        ps_ax = ctx.enter_context(nc.psum_tensor("ps_ax", [64, TC], F32))
        ps_b = ctx.enter_context(nc.psum_tensor("ps_b", [128, TC], F32))
        ps_mu = ps_b[0:1, :]
        ps_sq = ps_ax[0:1, :]
        sem = lambda name: ctx.enter_context(nc.semaphore(name))
        ldc = sem("ldc"); lw = sem("lw"); lxn = sem("lxn"); la = sem("la"); lb2 = sem("lb2")
        lx = [sem("lxa"), sem("lxb")]; l2 = [sem("l2a"), sem("l2b")]
        st1 = [sem("st1a"), sem("st1b")]; st2 = [sem("st2a"), sem("st2b")]
        vsem = sem("vsem"); ssem = sem("ssem"); tsem = sem("tsem")
        block = ctx.enter_context(nc.Block())
        mi = Mi()

        def emit_scalar(E):
            def do_sq(m):
                E.wait(vsem, mi[("x1", m)])
                E.wait(tsem, mi[("stm", m - 1)] if m >= 1 else 0)
                E.activation(out=sq[:], in_=x1f[m % 2][:], func=AF.Square)
                mi[("sq", m)] = E.n
            for m in range(NM):
                E.wait(lw, 16 * (m + 1))
                if m >= 2:
                    E.wait(tsem, mi[("mm", m - 1)])
                E.activation(out=wbf[m % 2][:], in_=wst[:], func=AF.Identity)
                mi[("wcast", m)] = E.n
                if m >= 2:
                    do_sq(m - 2)
            do_sq(NM - 2); do_sq(NM - 1)
            E.wait(vsem, mi["murow"])
            E.activation(out=sqmu[:], in_=murow[:], func=AF.Square)
            mi["sqmu"] = E.n
            E.wait(vsem, mi["varrow"])
            E.activation(out=varrow[:], in_=varrow[:], func=AF.Ln)
            E.activation(out=rstdrow[:], in_=varrow[:], func=AF.Exp, scale=-0.5)
            mi["rstd"] = E.n

        def emit_tensor(E):
            E.wait(ldc, 16 * 4)
            E.wait(lxn, 16 * NKT)
            E.wait(vsem, mi["aeff"])
            for kt in range(NKT):
                for h in range(2):
                    last = (kt == NKT - 1 and h == 1)
                    f = E.matmul if last else E.un("matmul")
                    f(ps_ax[:, h * 512:(h + 1) * 512], aeb[:, kt * 64:(kt + 1) * 64],
                      xnb[:, kt * TC + h * 512:kt * TC + (h + 1) * 512],
                      start=(kt == 0), stop=(kt == NKT - 1))
            mi["axmm"] = E.n
            E.wait(vsem, mi["beff"])
            E.wait(vsem, mi["axb"])
            def do_stats(m):
                E.wait(vsem, mi[("x1", m)])
                E.un("matmul")(ps_mu[:, 0:512], onec[:], x1f[m % 2][:, 0:512],
                               start=(m == 0), stop=(m == NM - 1))
                E.matmul(ps_mu[:, 512:1024], onec[:], x1f[m % 2][:, 512:1024],
                         start=(m == 0), stop=(m == NM - 1))
                if m == NM - 1:
                    mi["mumm"] = E.n
                E.wait(ssem, mi[("sq", m)])
                E.un("matmul")(ps_sq[:, 0:512], onec[:], sq[:, 0:512],
                               start=(m == 0), stop=(m == NM - 1))
                E.matmul(ps_sq[:, 512:1024], onec[:], sq[:, 512:1024],
                         start=(m == 0), stop=(m == NM - 1))
                mi[("stm", m)] = E.n
                if m == NM - 1:
                    mi["sqmm"] = E.n
            for m in range(NM):
                E.wait(ssem, mi[("wcast", m)])
                if m >= 2:
                    E.wait(vsem, mi[("x1", m - 2)])
                pso = ps_o[m % 2]
                for h in range(2):
                    for kt in range(NKT):
                        E.un("matmul")(pso[:, h * 512:(h + 1) * 512],
                                       wbf[m % 2][:, kt * 128:(kt + 1) * 128],
                                       xnb[:, kt * TC + h * 512:kt * TC + (h + 1) * 512],
                                       start=(kt == 0), stop=False)
                    last = (h == 1)
                    f = E.matmul if last else E.un("matmul")
                    f(pso[:, h * 512:(h + 1) * 512], beb[:, m * 128:(m + 1) * 128],
                      axb[:, h * 512:(h + 1) * 512], start=False, stop=True)
                mi[("mm", m)] = E.n
                if m >= 1:
                    do_stats(m - 1)
            do_stats(NM - 1)
            E.wait(vsem, mi["negmu"])
            E.un("matmul")(ps_b[:, 0:512], ones2[:], rows2[:, 0:512], start=True, stop=True)
            E.matmul(ps_b[:, 512:1024], ones2[:], rows2[:, 512:1024], start=True, stop=True)
            mi["bc_mu"] = E.n
            E.wait(vsem, mi["rows_rstd"])
            E.un("matmul")(ps_b[:, 0:512], ones2[:], rows2[:, 0:512], start=True, stop=True)
            E.matmul(ps_b[:, 512:1024], ones2[:], rows2[:, 512:1024], start=True, stop=True)
            mi["bc_rstd"] = E.n

        def emit_vector(E):
            E.wait(ldc, 16 * 4)
            E.memset(aeff[:], 0.0)
            E.memset(beff[:], 0.0)
            for n in range(16):
                E.wait(la, 16 * (n + 1))
                op1 = ALU.bypass if n == 0 else ALU.add
                E.scalar_tensor_tensor(out=aeff[:], in0=astg[:], scalar=gtv[:, n:n + 1],
                                       in1=aeff[:], op0=ALU.mult, op1=op1)
                mi[("an", n)] = E.n
            E.wait(la, 16 * 17)
            E.tensor_tensor(out=aeff[:], in0=aeff[:], in1=astg[:], op=ALU.add)
            E.tensor_copy(out=aeb[:], in_=aeff[:])
            mi["aeff"] = E.n
            for n in range(16):
                E.wait(lb2, 16 * (n + 1))
                op1 = ALU.bypass if n == 0 else ALU.add
                E.scalar_tensor_tensor(out=beff[0:32, :], in0=bstg[:], scalar=gtv[0:32, n:n + 1],
                                       in1=beff[0:32, :], op0=ALU.mult, op1=op1)
                mi[("bn", n)] = E.n
            E.wait(lb2, 16 * 17)
            E.tensor_copy(out=beff[32:64, :], in_=bstg[:])
            E.tensor_copy(out=beb[:], in_=beff[:])
            mi["beff"] = E.n
            E.wait(tsem, mi["axmm"])
            E.tensor_copy(out=axb[:], in_=ps_ax[:])
            mi["axb"] = E.n
            for m in range(NM):
                E.wait(tsem, mi[("mm", m)])
                E.wait(lx[m % 2], 16 * (m // 2 + 1))
                if m >= 2:
                    E.wait(st1[m % 2], 16 * (m // 2))
                    E.wait(ssem, mi[("sq", m - 2)])
                    E.wait(tsem, mi[("stm", m - 2)])
                E.tensor_scalar(out=x1f[m % 2][:], in0=ps_o[m % 2][:], scalar1=biasl[:, m:m + 1],
                                scalar2=None, op0=ALU.add)
                E.tensor_tensor(out=x1f[m % 2][:], in0=x1f[m % 2][:], in1=x1pt[m % 2][:],
                                op=ALU.add)
                mi[("x1", m)] = E.n
            E.wait(tsem, mi["mumm"])
            E.tensor_scalar(out=murow[:], in0=ps_mu[:], scalar1=1.0 / D, scalar2=None, op0=ALU.mult)
            mi["murow"] = E.n
            E.wait(ssem, mi["sqmu"])
            E.wait(tsem, mi["sqmm"])
            E.tensor_scalar(out=varrow[:], in0=ps_sq[:], scalar1=1.0 / D, scalar2=None, op0=ALU.mult)
            E.tensor_tensor(out=varrow[:], in0=varrow[:], in1=sqmu[:], op=ALU.subtract)
            E.tensor_scalar(out=varrow[:], in0=varrow[:], scalar1=1e-5, scalar2=None, op0=ALU.add)
            mi["varrow"] = E.n
            E.memset(rows2[:], 0.0)
            E.tensor_scalar(out=rows2[0:1, :], in0=murow[:], scalar1=-1.0, scalar2=None, op0=ALU.mult)
            mi["negmu"] = E.n
            E.wait(tsem, mi["bc_mu"])
            E.tensor_copy(out=mub[:], in_=ps_b[:])
            mi["mub"] = E.n
            E.wait(ssem, mi["rstd"])
            E.tensor_copy(out=rows2[0:1, :], in_=rstdrow[:])
            mi["rows_rstd"] = E.n
            E.wait(tsem, mi["bc_rstd"])
            E.tensor_copy(out=rstdb[:], in_=ps_b[:])
            mi["stats_done"] = E.n
            for m in range(NM):
                E.wait(l2[m % 2], 16 * (m // 2 + 1))
                if m >= 2:
                    E.wait(st2[m % 2], 16 * (m // 2))
                E.tensor_tensor(out=tmpa[:], in0=x1pt[m % 2][:], in1=mub[:], op=ALU.add)
                E.tensor_tensor(out=xnout[m % 2], in0=tmpa[:], in1=rstdb[:], op=ALU.mult)
                mi[("xn1", m)] = E.n

        def emit_sync(e):
            e.dma_start(out=onec[:], in_=onescol).then_inc(ldc, 16)
            e.dma_start(out=ones2[:], in_=onesk2).then_inc(ldc, 16)
            e.dma_start(out=gtv[:], in_=gate_d).then_inc(ldc, 16)
            e.dma_start(out=biasl[:], in_=bias_d).then_inc(ldc, 16)
            for i in range(NKT):
                e.dma_start(out=xnb[:, i * TC:(i + 1) * TC],
                            in_=xn0[i * 128:(i + 1) * 128, :]).then_inc(lxn, 16)
            for n in range(16):
                if n >= 1:
                    e.wait_ge(vsem, mi[("an", n - 1)])
                e.dma_start(out=astg[:], in_=aeff_d[n]).then_inc(la, 16)
            e.wait_ge(vsem, mi[("an", 15)])
            e.dma_start(out=astg[:], in_=ashare_d).then_inc(la, 16)
            for n in range(16):
                e.wait_ge(vsem, mi[("bn", n - 1)] if n >= 1 else mi["aeff"])
                e.dma_start(out=bstg[:], in_=beff_d[n]).then_inc(lb2, 16)
            e.wait_ge(vsem, mi[("bn", 15)])
            e.dma_start(out=bstg[:], in_=bshare_d).then_inc(lb2, 16)
            e.dma_start(out=wst[:], in_=wkr[0]).then_inc(lw, 16)
            for m in range(NM):
                if m + 1 < NM:
                    e.wait_ge(ssem, mi[("wcast", m)])
                    e.dma_start(out=wst[:], in_=wkr[m + 1]).then_inc(lw, 16)
                e.wait_ge(vsem, mi[("x1", m - 1)] if m >= 1 else 0)
                e.dma_start(out=x1pt[m % 2][:], in_=x1p[m * 128:(m + 1) * 128, :]).then_inc(lx[m % 2], 16)
                e.wait_ge(vsem, mi[("x1", m)])
                e.dma_start(out=o_x1[m * 128:(m + 1) * 128, :],
                            in_=x1f[m % 2][:]).then_inc(st1[m % 2], 16)
            e.wait_ge(vsem, mi["stats_done"])
            e.wait_ge(st1[0], 16 * 16); e.wait_ge(st1[1], 16 * 16)
            for m in range(NM + 2):
                if m < NM:
                    if m >= 2:
                        e.wait_ge(vsem, mi[("xn1", m - 2)])
                    e.dma_start(out=x1pt[m % 2][:], in_=o_x1[m * 128:(m + 1) * 128, :]).then_inc(l2[m % 2], 16)
                if m >= 2:
                    sm = m - 2
                    e.wait_ge(vsem, mi[("xn1", sm)])
                    e.dma_start(out=o_xn1[sm * 128:(sm + 1) * 128, :],
                                in_=xnout[sm % 2]).then_inc(st2[sm % 2], 16)
            e.wait_ge(st1[0], 16 * 16); e.wait_ge(st1[1], 16 * 16)
            e.wait_ge(st2[0], 16 * 16); e.wait_ge(st2[1], 16 * 16)

        for _ in range(2):
            emit_scalar(FakeCEng()); emit_tensor(FakeCEng()); emit_vector(FakeCEng())
        mi.strict = True

        @block.sync
        def _(e):
            emit_sync(e)

        @block.scalar
        def _(e):
            emit_scalar(CEng(e, ssem, auto=True))

        @block.tensor
        def _(e):
            emit_tensor(CEng(e, tsem))

        @block.vector
        def _(e):
            emit_vector(CEng(e, vsem, auto=True))
    return nc


def host_prep_C2(inputs, x1p_cores, xn_cores):
    kern = inputs["L0_ffn_kernel"].astype(np.float32)
    wkr = _w_prep_lhsT(kern)
    lora_A = inputs["L0_ffn_lora_A"].astype(np.float32)
    lora_B = inputs["L0_ffn_lora_B"].astype(np.float32)
    ash = inputs["L0_ffn_lora_A_shared"].astype(np.float32)
    bsh = inputs["L0_ffn_lora_B_shared"].astype(np.float32)
    ae = np.zeros((16, 128, NKT, 64), np.float32)
    la = lora_A.transpose(0, 2, 1).reshape(16, NKT, 128, 32)
    ae[:, :, :, 0:32] = la.transpose(0, 2, 1, 3)
    aeff_d = np.ascontiguousarray(ae.reshape(16, 128, NKT * 64))
    ash_t = np.zeros((128, NKT, 64), np.float32)
    ash_t[:, :, 32:64] = ash.T.reshape(NKT, 128, 32).transpose(1, 0, 2)
    ashare_d = np.ascontiguousarray(ash_t.reshape(128, NKT * 64))
    beff_d = np.ascontiguousarray(lora_B.transpose(0, 2, 1))
    bshare_d = np.ascontiguousarray(bsh.T)
    gate = inputs["L0_ffn_gate"].astype(np.float32)
    bias = inputs["L0_ffn_bias"].astype(np.float32)
    bias_d = np.ascontiguousarray(bias.reshape(32, 128).T)
    onescol = np.ones((128, 1), np.float32)
    onesk2 = np.zeros((2, 128), np.float32); onesk2[0] = 1.0
    in_maps = []
    for c in range(NCORES):
        in_maps.append({
            "xn0": xn_cores[c], "x1p": x1p_cores[c],
            "wkr": wkr, "aeff_d": aeff_d, "ashare_d": ashare_d,
            "beff_d": beff_d, "bshare_d": bshare_d,
            "gate_d": np.ascontiguousarray(np.broadcast_to(gate[c], (128, 16)).copy()),
            "bias_d": bias_d, "onescol": onescol, "onesk2": onesk2,
        })
    return in_maps



def build_launch_D():
    nc = bass.Bass("TRN2", target_bir_lowering=False)
    xn1 = nc.dram_tensor("xn1", [D, TC], BF16, kind="ExternalInput").ap()
    xprev = nc.dram_tensor("xprev", [128, NKT], BF16, kind="ExternalInput").ap()
    wkvrg = nc.dram_tensor("wkvrg", [128, 128, D], F32, kind="ExternalInput").ap()
    mixd = nc.dram_tensor("mixd", [128, 128], F32, kind="ExternalInput").ap()
    o_kvrg = nc.dram_tensor("o_kvrg", [4 * D, TC], BF16, kind="ExternalOutput").ap()
    NM = 32
    TH = 512
    with ExitStack() as ctx:
        sbt = lambda name, shape, dt=F32: ctx.enter_context(nc.sbuf_tensor(name, shape, dt))
        xnb = sbt("xnb", [128, NKT * TH], BF16)
        db = sbt("db", [128, NKT * TH], BF16)
        mixw = sbt("mixw", [128, NKT * TH], BF16)
        xpv = sbt("xpv", [128, NKT], BF16)
        mixc = sbt("mixc", [128, 128])
        xstg = [sbt(f"xstg{i}", [128, TH], BF16) for i in range(2)]
        wst = sbt("wst", [128, D])
        wbf = [sbt(f"wbf{i}", [128, D], BF16) for i in range(2)]
        tmp = sbt("tmp", [128, TH])
        outb = [sbt(f"outb{i}", [128, TH], BF16) for i in range(2)]
        ps_o = [ctx.enter_context(nc.psum_tensor(f"ps_o{i}", [128, TH], F32)) for i in range(2)]
        sem = lambda name: ctx.enter_context(nc.semaphore(name))
        ldc = sem("ldc"); lw = sem("lw"); lx = [sem("lxa"), sem("lxb")]
        stq = [sem("stqa"), sem("stqb")]
        vsem = sem("vsem"); ssem = sem("ssem"); tsem = sem("tsem")
        block = ctx.enter_context(nc.Block())
        mi = Mi()

        def emit_vector(E):
            E.wait(ldc, 16 * 2)
            for hf in range(2):
                for kt in range(NKT):
                    i = hf * NKT + kt
                    E.wait(lx[i % 2], 16 * (i // 2 + 1))
                    st = xstg[i % 2]
                    if hf == 0:
                        E.tensor_tensor(out=db[:, kt * TH:kt * TH + 1],
                                        in0=st[:, 0:1], in1=xpv[:, kt:kt + 1], op=ALU.subtract)
                    else:
                        E.tensor_tensor(out=db[:, kt * TH:kt * TH + 1],
                                        in0=st[:, 0:1],
                                        in1=xnb[:, kt * TH + TH - 1:kt * TH + TH],
                                        op=ALU.subtract)
                    E.tensor_tensor(out=db[:, kt * TH + 1:(kt + 1) * TH],
                                    in0=st[:, 1:TH], in1=st[:, 0:TH - 1], op=ALU.subtract)
                    E.tensor_copy(out=xnb[:, kt * TH:(kt + 1) * TH], in_=st[:, 0:TH])
                    mi[("d", hf, kt)] = E.n
                if hf == 1:
                    E.wait(tsem, mi[("mm", 127)])
                for w in range(4):
                    for kt in range(NKT):
                        E.scalar_tensor_tensor(out=mixw[:, kt * TH:(kt + 1) * TH],
                                               in0=db[:, kt * TH:(kt + 1) * TH],
                                               scalar=mixc[:, kt * 4 + w:kt * 4 + w + 1],
                                               in1=xnb[:, kt * TH:(kt + 1) * TH],
                                               op0=ALU.mult, op1=ALU.add)
                    mi[("mix", hf, w)] = E.n
                    if w < 3:
                        for m in range(NM):
                            gm = hf * 128 + w * NM + m
                            E.wait(tsem, mi[("mm", gm)])
                            if gm >= 2:
                                E.wait(stq[gm % 2], 16 * (gm // 2))
                            E.tensor_copy(out=outb[gm % 2][:], in_=ps_o[gm % 2][:])
                            mi[("out", gm)] = E.n

        def emit_scalar(E):
            def outg(gm):
                E.wait(tsem, mi[("mm", gm)])
                if gm >= 2:
                    E.wait(stq[gm % 2], 16 * (gm // 2))
                E.activation(out=outb[gm % 2][:], in_=ps_o[gm % 2][:], func=AF.Silu)
                mi[("outg", gm)] = E.n
            for hf in range(2):
                for w in range(4):
                    for m in range(NM):
                        gm = hf * 128 + w * NM + m
                        E.wait(lw, 16 * (gm + 1))
                        if gm >= 2:
                            E.wait(tsem, mi[("mm", gm - 1)])
                        E.activation(out=wbf[gm % 2][:], in_=wst[:], func=AF.Identity)
                        mi[("wcast", gm)] = E.n
                        if w == 3 and m >= 1:
                            outg(gm - 1)
                outg(hf * 128 + 127)

        def emit_tensor(E):
            E.wait(ldc, 16 * 2)
            for hf in range(2):
                for w in range(4):
                    E.wait(vsem, mi[("mix", hf, w)])
                    for m in range(NM):
                        gm = hf * 128 + w * NM + m
                        E.wait(ssem, mi[("wcast", gm)])
                        if gm >= 2:
                            if (gm - 2) % 128 < 3 * NM:
                                E.wait(vsem, mi[("out", gm - 2)])
                            else:
                                E.wait(ssem, mi[("outg", gm - 2)])
                        pso = ps_o[gm % 2]
                        for kt in range(NKT):
                            last = (kt == NKT - 1)
                            f = E.matmul if last else E.un("matmul")
                            f(pso[:], wbf[gm % 2][:, kt * 128:(kt + 1) * 128],
                              mixw[:, kt * TH:(kt + 1) * TH],
                              start=(kt == 0), stop=(kt == NKT - 1))
                        mi[("mm", gm)] = E.n

        def emit_sync(e):
            e.dma_start(out=mixc[:], in_=mixd).then_inc(ldc, 16)
            e.dma_start(out=xpv[:], in_=xprev).then_inc(ldc, 16)
            e.dma_start(out=wst[:], in_=wkvrg[0]).then_inc(lw, 16)
            def xload(i):
                hf, kt = i // NKT, i % NKT
                if i >= 2:
                    e.wait_ge(vsem, mi[("d", (i - 2) // NKT, (i - 2) % NKT)])
                e.dma_start(out=xstg[i % 2][:, 0:TH],
                            in_=xn1[kt * 128:(kt + 1) * 128, hf * TH:(hf + 1) * TH]).then_inc(lx[i % 2], 16)
            for i in range(NKT + 2):
                xload(i)
            for gm2 in range(256):
                if 96 <= gm2 < 96 + NKT - 2:
                    xload(NKT + 2 + (gm2 - 96))
                if gm2 + 1 < 256:
                    w_idx = (gm2 + 1) % 128
                    e.wait_ge(ssem, mi[("wcast", gm2)])
                    e.dma_start(out=wst[:], in_=wkvrg[w_idx]).then_inc(lw, 16)
                hf, gl = gm2 // 128, gm2 % 128
                if gl < 3 * NM:
                    e.wait_ge(vsem, mi[("out", gm2)])
                else:
                    e.wait_ge(ssem, mi[("outg", gm2)])
                e.dma_start(out=o_kvrg[gl * 128:(gl + 1) * 128, hf * TH:(hf + 1) * TH],
                            in_=outb[gm2 % 2][:]).then_inc(stq[gm2 % 2], 16)
            e.wait_ge(stq[0], 16 * 128); e.wait_ge(stq[1], 16 * 128)

        for _ in range(2):
            emit_vector(FakeCEng()); emit_scalar(FakeCEng()); emit_tensor(FakeCEng())
        mi.strict = True

        @block.sync
        def _(e):
            emit_sync(e)

        @block.scalar
        def _(e):
            emit_scalar(CEng(e, ssem, auto=True))

        @block.tensor
        def _(e):
            emit_tensor(CEng(e, tsem))

        @block.vector
        def _(e):
            emit_vector(CEng(e, vsem, auto=True))
    return nc


def host_prep_D(inputs, xn1_cores):
    Wk = inputs["L1_attn_w_key"].astype(np.float32)
    Wv = inputs["L1_attn_w_value"].astype(np.float32)
    Wr = inputs["L1_attn_w_receptance"].astype(np.float32)
    Wg = inputs["L1_attn_w_gate"].astype(np.float32)
    wk = np.concatenate([_w_prep_lhsT(W.T) for W in (Wk, Wv, Wr, Wg)], axis=0)
    mixs = [inputs[f"L1_attn_time_mix_{n}"].astype(np.float32)
            for n in ("key", "value", "receptance", "gate")]
    mixd = np.zeros((128, 128), np.float32)
    for kt in range(NKT):
        for w in range(4):
            mixd[:, kt * 4 + w] = mixs[w][kt * 128:(kt + 1) * 128] - 1.0
    in_maps = []
    for c in range(NCORES):
        if c % 4 == 0:
            xprev = np.zeros((128, NKT), np.float32).astype(ml_dtypes.bfloat16)
        else:
            xprev = np.ascontiguousarray(xn1_cores[c - 1][:, -1].reshape(NKT, 128).T)
        in_maps.append({"xn1": xn1_cores[c], "xprev": xprev, "wkvrg": wk, "mixd": mixd})
    return in_maps


CHUNK = 128
NCH = S // CHUNK


def build_launch_E():
    nc = bass.Bass("TRN2", target_bir_lowering=False)
    kvr = nc.dram_tensor("kvr", [16, 3, 64, S], BF16, kind="ExternalInput").ap()
    tdc = nc.dram_tensor("tdc", [64, 16], F32, kind="ExternalInput").ap()
    tfc = nc.dram_tensor("tfc", [64, 16], F32, kind="ExternalInput").ap()
    idxp = nc.dram_tensor("idxp", [64, 14 * CHUNK], F32, kind="ExternalInput").ap()
    masksd = nc.dram_tensor("masksd", [128, 24 * CHUNK], F32, kind="ExternalInput").ap()
    idb_d = nc.dram_tensor("idb_d", [64, 64], BF16, kind="ExternalInput").ap()
    ones64_d = nc.dram_tensor("ones64_d", [64, 1], BF16, kind="ExternalInput").ap()
    gnw_d = nc.dram_tensor("gnw_d", [16, 128, 64], BF16, kind="ExternalInput").ap()
    gnb_d = nc.dram_tensor("gnb_d", [16, 128, 64], BF16, kind="ExternalInput").ap()
    o_gn = nc.dram_tensor("o_gn", [16, 128, NCH * 64], BF16, kind="ExternalOutput").ap()

    (IX_INTER, IX_ST, IX_D4R, IX_D4K, IX_K4, IX_R4, IX_K8, IX_R8,
     IX_K16, IX_R16, IX_K32, IX_R32, IX_K64, IX_R64) = range(14)
    with ExitStack() as ctx:
        sbt = lambda name, shape, dt=F32: ctx.enter_context(nc.sbuf_tensor(name, shape, dt))
        kf = [sbt(f"kf{i}", [64, S], BF16) for i in range(2)]
        vf = [sbt(f"vf{i}", [64, S], BF16) for i in range(2)]
        rf = [sbt(f"rf{i}", [64, S], BF16) for i in range(2)]
        scl = {q: sbt(f"scl{q}", [64, S], BF16)
               for q in (IX_INTER, IX_D4R, IX_D4K, IX_K4, IX_R4, IX_K8, IX_R8,
                         IX_K16, IX_R16, IX_K32, IX_R32, IX_K64, IX_R64)}
        tdl = sbt("tdl", [64, 16]); tfl = sbt("tfl", [64, 16])
        ecol = sbt("ecol", [64, 1]); dec128 = sbt("dec128", [64, 1])
        idxl = sbt("idxl", [64, 14 * CHUNK])
        maskl = sbt("maskl", [128, 24 * CHUNK])
        idb = sbt("idb", [64, 64], BF16); ones64 = sbt("ones64", [64, 1], BF16)
        ptmp = sbt("ptmp", [64, CHUNK])
        pst = sbt("pst", [64, CHUNK], BF16)
        pstt = sbt("pstt", [128, 64], BF16)
        vtk = sbt("vtk", [128, 64], BF16); ktk = sbt("ktk", [128, 64], BF16)
        ktt = sbt("ktt", [128, 64], BF16)
        rk = sbt("rk", [64, CHUNK], BF16)
        wcol = sbt("wcol", [128, 1])
        acc = sbt("acc", [128, 4 * CHUNK]); tmm = sbt("tmm", [128, 4 * CHUNK])
        accb = sbt("accb", [128, 4 * CHUNK], BF16)
        sfp = sbt("sfp", [64, 64]); sbf = sbt("sbf", [64, 64], BF16)
        opair = sbt("opair", [128, NCH * 64])
        osq = sbt("osq", [128, NCH * 64])
        gsum = sbt("gsum", [128, NCH]); gsq = sbt("gsq", [128, NCH])
        gmu = sbt("gmu", [128, NCH]); gms = sbt("gms", [128, NCH]); grs = sbt("grs", [128, NCH])
        gnwl = sbt("gnwl", [128, 64], BF16); gnbl = sbt("gnbl", [128, 64], BF16)
        ogn = sbt("ogn", [128, NCH * 64], BF16)

        ps_a = [ctx.enter_context(nc.psum_tensor(f"ps_a{i}", [128, 4 * CHUNK], F32)) for i in range(4)]
        ps_o = ctx.enter_context(nc.psum_tensor("ps_o", [128, 64], F32))
        ps_tt = ctx.enter_context(nc.psum_tensor("ps_tt", [64, 64], BF16))
        ps_t2 = ctx.enter_context(nc.psum_tensor("ps_t2", [128, 128], F32))
        ps_s = ctx.enter_context(nc.psum_tensor("ps_s", [64, 64], F32))

        sem = lambda name: ctx.enter_context(nc.semaphore(name))
        ldc = sem("ldc"); lp = [sem("lpa"), sem("lpb")]; lg = sem("lg")
        stq = sem("stq")
        vsem = sem("vsem"); ssem = sem("ssem"); tsem = sem("tsem")
        block = ctx.enter_context(nc.Block())
        mi = Mi()

        SC32 = float(2.0 ** -32)

        def rep_ch(ap128):
            return ap128.unsqueeze(1).broadcast_to((64, NCH, CHUNK))

        def v3(t):
            return t[:].rearrange("p (c k) -> p c k", k=CHUNK)

        def emit_scalar(E):
            for p in range(16):
                E.wait(ldc, 16 * 6)
                if p > 0:
                    E.wait(vsem, mi[("scl_done", p - 1)])
                E.activation(out=ecol[:], in_=tdl[:, p:p + 1], func=AF.Exp)
                E.wait(E, E.n)
                E.activation(out=dec128[:], in_=ecol[:], func=AF.Exp, scale=-float(CHUNK))
                mi[("ecol", p)] = E.n
                for q in range(14):
                    E.wait(vsem, mi[("pt_t", p, q)])
                    if q == IX_ST:
                        E.activation(out=pst[:], in_=ptmp[:], func=AF.Exp, scale=-1.0)
                    else:
                        E.activation(out=ptmp[:], in_=ptmp[:], func=AF.Exp, scale=-1.0)
                    mi[("pt_e", p, q)] = E.n
                E.wait(vsem, mi[("oev_all", p)])
                E.activation(out=osq[:], in_=opair[:], func=AF.Square)
                mi[("osqd", p)] = E.n
                E.wait(vsem, mi[("gmu", p)])
                E.activation(out=gms[:], in_=gmu[:], func=AF.Square)
                mi[("gmusq", p)] = E.n
                E.wait(vsem, mi[("gvar", p)])
                E.activation(out=gsq[:], in_=gsq[:], func=AF.Ln)
                E.wait(E, E.n)
                E.activation(out=grs[:], in_=gsq[:], func=AF.Exp, scale=-0.5)
                mi[("grstd", p)] = E.n

        def emit_tensor(E):
            E.wait(ldc, 16 * 6)
            for p in range(16):
                E.wait(lp[p % 2], 16 * 3 * (p // 2 + 1))
                E.wait(ssem, mi[("pt_e", p, IX_ST)])
                E.transpose(ps_tt[:], pst[:, 0:64], idb[:])
                mi[("ptT1", p)] = E.n
                E.wait(vsem, mi[("ptTev1", p)])
                E.transpose(ps_tt[:], pst[:, 64:128], idb[:])
                mi[("ptT2", p)] = E.n
                for c in range(NCH):
                    sl = slice(c * CHUNK, (c + 1) * CHUNK)
                    E.wait(vsem, mi[("ptTev2", p)] if c == 0 else mi[("kT2e", p, c - 1)])
                    E.transpose(ps_tt[:], vf[p % 2][:, c * CHUNK:c * CHUNK + 64], idb[:])
                    mi[("vT1", p, c)] = E.n
                    E.wait(vsem, mi[("vT1e", p, c)])
                    E.transpose(ps_tt[:], vf[p % 2][:, c * CHUNK + 64:(c + 1) * CHUNK], idb[:])
                    mi[("vT2", p, c)] = E.n
                    E.wait(vsem, mi[("vT2e", p, c)])
                    E.transpose(ps_tt[:], kf[p % 2][:, c * CHUNK:c * CHUNK + 64], idb[:])
                    mi[("kT1", p, c)] = E.n
                    E.wait(vsem, mi[("kT1e", p, c)])
                    E.transpose(ps_tt[:], kf[p % 2][:, c * CHUNK + 64:(c + 1) * CHUNK], idb[:])
                    mi[("kT2", p, c)] = E.n
                    if c % 4 == 0:
                        cg = c // 4
                        E.wait(vsem, mi[("accprev", p, cg)])
                        for lv, (ka, ra) in enumerate(((IX_D4K, IX_D4R), (IX_K4, IX_R4),
                                                       (IX_K8, IX_R8), (IX_K16, IX_R16))):
                            for c4 in range(4):
                                slc = slice((c + c4) * CHUNK, (c + c4 + 1) * CHUNK)
                                f = E.matmul if (lv == 3 and c4 == 3) else E.un("matmul")
                                f(ps_a[lv][:, c4 * CHUNK:(c4 + 1) * CHUNK],
                                  scl[ka][:, slc], scl[ra][:, slc], start=True, stop=True)
                        mi[("amm", p, cg)] = E.n
                        E.wait(vsem, mi[("mrg01", p, cg)])
                        for lv, (ka, ra) in enumerate(((IX_K32, IX_R32), (IX_K64, IX_R64))):
                            for c4 in range(4):
                                slc = slice((c + c4) * CHUNK, (c + c4 + 1) * CHUNK)
                                f = E.matmul if (lv == 1 and c4 == 3) else E.un("matmul")
                                f(ps_a[lv][:, c4 * CHUNK:(c4 + 1) * CHUNK],
                                  scl[ka][:, slc], scl[ra][:, slc], start=True, stop=True)
                        mi[("amm2", p, cg)] = E.n
                    E.wait(vsem, mi[("rktf", p, c)])
                    E.matmul(ps_o[:, 0:1], rk[:], ones64[:], start=True, stop=True)
                    mi[("wmm", p, c)] = E.n
                    E.wait(vsem, mi[("accb", p, c // 4)])
                    E.wait(vsem, mi[("wcole", p, c)])
                    E.matmul(ps_o[:], accb[:, (c % 4) * CHUNK:(c % 4 + 1) * CHUNK], vtk[:],
                             start=True, stop=True)
                    mi[("ointra", p, c)] = E.n
                    E.wait(vsem, mi[("sbf", p, c)])
                    E.wait(vsem, mi[("oev1", p, c)])
                    E.matmul(ps_t2[:, 0:64], scl[IX_INTER][:, sl], sbf[:], start=True, stop=True)
                    mi[("ointer", p, c)] = E.n
                    E.wait(vsem, mi[("ktt", p, c)])
                    E.wait(vsem, mi[("sdecay", p, c)])
                    E.matmul(ps_s[:], ktt[:], vtk[:], start=True, stop=True)
                    mi[("smm", p, c)] = E.n
                mi[("pairdone_t", p)] = E.n

        def emit_vector(E):
            E.wait(ldc, 16 * 6)
            for p in range(16):
                E.wait(lp[p % 2], 16 * 3 * (p // 2 + 1))
                E.wait(lg, 16 * 2 * (p + 1))
                for q in range(14):
                    E.wait(ssem, mi[("ecol", p)])
                    if q > 0:
                        E.wait(ssem, mi[("pt_e", p, q - 1)])
                    E.tensor_scalar(out=ptmp[:], in0=idxl[:, q * CHUNK:(q + 1) * CHUNK],
                                    scalar1=ecol[:], scalar2=None, op0=ALU.mult)
                    mi[("pt_t", p, q)] = E.n
                    E.wait(ssem, mi[("pt_e", p, q)])
                    if q == IX_ST:
                        pass
                    else:
                        if q in (IX_D4R, IX_D4K):
                            E.tensor_scalar(out=ptmp[:], in0=ptmp[:], scalar1=SC32,
                                            scalar2=None, op0=ALU.mult)
                        src = rf if q in (IX_INTER, IX_D4R, IX_R4, IX_R8,
                                          IX_R16, IX_R32, IX_R64) else kf
                        E.tensor_tensor(out=v3(scl[q]), in0=v3(src[p % 2]),
                                        in1=rep_ch(ptmp[:]), op=ALU.mult)
                    mi[("pt_full", p, q)] = E.n
                mi[("scl_done", p)] = E.n
                E.wait(tsem, mi[("ptT1", p)])
                E.tensor_copy(out=pstt[0:64, :], in_=ps_tt[:])
                mi[("ptTev1", p)] = E.n
                E.wait(tsem, mi[("ptT2", p)])
                E.tensor_copy(out=pstt[64:128, :], in_=ps_tt[:])
                mi[("ptTev2", p)] = E.n
                E.memset(sfp[:], 0.0)
                E.memset(sbf[:], 0.0)
                mi[("sbf", p, 0)] = E.n
                mi[("sdecay", p, 0)] = E.n
                for c in range(NCH):
                    sl = slice(c * CHUNK, (c + 1) * CHUNK)
                    E.wait(tsem, mi[("vT1", p, c)])
                    E.tensor_copy(out=vtk[0:64, :], in_=ps_tt[:])
                    mi[("vT1e", p, c)] = E.n
                    E.wait(tsem, mi[("vT2", p, c)])
                    E.tensor_copy(out=vtk[64:128, :], in_=ps_tt[:])
                    mi[("vT2e", p, c)] = E.n
                    E.wait(tsem, mi[("kT1", p, c)])
                    E.tensor_copy(out=ktk[0:64, :], in_=ps_tt[:])
                    mi[("kT1e", p, c)] = E.n
                    E.wait(tsem, mi[("kT2", p, c)])
                    E.tensor_copy(out=ktk[64:128, :], in_=ps_tt[:])
                    mi[("kT2e", p, c)] = E.n
                    if c % 4 == 0:
                        cg = c // 4
                        B4 = 4 * CHUNK
                        mi[("accprev", p, cg)] = mi[("accb", p, cg - 1)] if cg > 0 else E.n
                        E.wait(tsem, mi[("amm", p, cg)])
                        E.tensor_tensor(out=acc[:], in0=ps_a[0][:], in1=maskl[:, 0:B4], op=ALU.mult)
                        E.tensor_tensor(out=tmm[:], in0=ps_a[1][:],
                                        in1=maskl[:, B4:2 * B4], op=ALU.mult)
                        E.tensor_tensor(out=acc[:], in0=acc[:], in1=tmm[:], op=ALU.add)
                        mi[("mrg01", p, cg)] = E.n
                        for lv in range(2):
                            E.tensor_tensor(out=tmm[:], in0=ps_a[lv + 2][:],
                                            in1=maskl[:, (lv + 2) * B4:(lv + 3) * B4], op=ALU.mult)
                            E.tensor_tensor(out=acc[:], in0=acc[:], in1=tmm[:], op=ALU.add)
                        E.wait(tsem, mi[("amm2", p, cg)])
                        for lv in range(2):
                            E.tensor_tensor(out=tmm[:], in0=ps_a[lv][:],
                                            in1=maskl[:, (lv + 4) * B4:(lv + 5) * B4], op=ALU.mult)
                            E.tensor_tensor(out=acc[:], in0=acc[:], in1=tmm[:], op=ALU.add)
                        E.tensor_copy(out=accb[:], in_=acc[:])
                        mi[("accb", p, cg)] = E.n
                    E.tensor_tensor(out=ktt[:], in0=ktk[:], in1=pstt[:], op=ALU.mult)
                    mi[("ktt", p, c)] = E.n
                    E.tensor_tensor(out=rk[:], in0=rf[p % 2][:, sl], in1=kf[p % 2][:, sl], op=ALU.mult)
                    E.tensor_scalar(out=rk[:], in0=rk[:], scalar1=tfl[:, p:p + 1], scalar2=None,
                                    op0=ALU.mult)
                    mi[("rktf", p, c)] = E.n
                    E.wait(tsem, mi[("wmm", p, c)])
                    E.tensor_copy(out=wcol[:], in_=ps_o[:, 0:1])
                    mi[("wcole", p, c)] = E.n
                    E.wait(tsem, mi[("ointra", p, c)])
                    E.tensor_copy(out=opair[:, c * 64:(c + 1) * 64], in_=ps_o[:])
                    mi[("oev1", p, c)] = E.n
                    E.wait(tsem, mi[("ointer", p, c)])
                    E.scalar_tensor_tensor(out=opair[0:128, c * 64:(c + 1) * 64], in0=ps_t2[:, 0:64],
                                           scalar=1.0, in1=opair[:, c * 64:(c + 1) * 64],
                                           op0=ALU.mult, op1=ALU.add)
                    E.scalar_tensor_tensor(out=opair[:, c * 64:(c + 1) * 64], in0=vtk[:],
                                           scalar=wcol[:], in1=opair[:, c * 64:(c + 1) * 64],
                                           op0=ALU.mult, op1=ALU.add)
                    mi[("oev", p, c)] = E.n
                    E.tensor_scalar(out=sfp[:], in0=sfp[:], scalar1=dec128[:], scalar2=None,
                                    op0=ALU.mult)
                    mi[("sdecay", p, c + 1)] = E.n
                    E.wait(tsem, mi[("smm", p, c)])
                    E.scalar_tensor_tensor(out=sfp[:], in0=ps_s[:], scalar=1.0, in1=sfp[:],
                                           op0=ALU.mult, op1=ALU.add)
                    E.tensor_copy(out=sbf[:], in_=sfp[:])
                    mi[("sbf", p, c + 1)] = E.n
                mi[("oev_all", p)] = E.n
                E.tensor_reduce(out=gsum[:], in_=opair[:].rearrange("p (c e) -> p c e", e=64),
                                axis=AX.X, op=ALU.add)
                E.tensor_scalar(out=gmu[:], in0=gsum[:], scalar1=1.0 / 64, scalar2=None, op0=ALU.mult)
                mi[("gmu", p)] = E.n
                E.wait(ssem, mi[("osqd", p)])
                E.tensor_reduce(out=gsq[:], in_=osq[:].rearrange("p (c e) -> p c e", e=64),
                                axis=AX.X, op=ALU.add)
                E.tensor_scalar(out=gsq[:], in0=gsq[:], scalar1=1.0 / 64, scalar2=None, op0=ALU.mult)
                E.wait(ssem, mi[("gmusq", p)])
                E.tensor_tensor(out=gsq[:], in0=gsq[:], in1=gms[:], op=ALU.subtract)
                E.tensor_scalar(out=gsq[:], in0=gsq[:], scalar1=1e-5, scalar2=None, op0=ALU.add)
                mi[("gvar", p)] = E.n
                E.wait(ssem, mi[("grstd", p)])
                if p >= 1:
                    E.wait(stq, 16 * p)
                op3 = opair[:].rearrange("p (c e) -> p c e", e=64)
                E.tensor_tensor(out=op3, in0=op3,
                                in1=gmu[:].unsqueeze(2).broadcast_to((128, NCH, 64)), op=ALU.subtract)
                E.tensor_tensor(out=op3, in0=op3,
                                in1=grs[:].unsqueeze(2).broadcast_to((128, NCH, 64)), op=ALU.mult)
                E.tensor_tensor(out=op3, in0=op3,
                                in1=gnwl[:].unsqueeze(1).broadcast_to((128, NCH, 64)), op=ALU.mult)
                E.tensor_tensor(out=ogn[:].rearrange("p (c e) -> p c e", e=64), in0=op3,
                                in1=gnbl[:].unsqueeze(1).broadcast_to((128, NCH, 64)), op=ALU.add)
                mi[("ogn", p)] = E.n

        def emit_sync(e):
            e.dma_start(out=tdl[:], in_=tdc).then_inc(ldc, 16)
            e.dma_start(out=tfl[:], in_=tfc).then_inc(ldc, 16)
            e.dma_start(out=idxl[:], in_=idxp).then_inc(ldc, 16)
            e.dma_start(out=maskl[:], in_=masksd).then_inc(ldc, 16)
            e.dma_start(out=idb[:], in_=idb_d).then_inc(ldc, 16)
            e.dma_start(out=ones64[:], in_=ones64_d).then_inc(ldc, 16)
            for p in range(2):
                e.dma_start(out=kf[p][:], in_=kvr[p, 0]).then_inc(lp[p], 16)
                e.dma_start(out=vf[p][:], in_=kvr[p, 1]).then_inc(lp[p], 16)
                e.dma_start(out=rf[p][:], in_=kvr[p, 2]).then_inc(lp[p], 16)
            e.dma_start(out=gnwl[:], in_=gnw_d[0]).then_inc(lg, 16)
            e.dma_start(out=gnbl[:], in_=gnb_d[0]).then_inc(lg, 16)
            for p in range(16):
                e.wait_ge(vsem, mi[("ogn", p)])
                e.dma_start(out=o_gn[p], in_=ogn[:]).then_inc(stq, 16)
                if p + 2 < 16:
                    e.wait_ge(tsem, mi[("pairdone_t", p)])
                    e.dma_start(out=kf[p % 2][:], in_=kvr[p + 2, 0]).then_inc(lp[p % 2], 16)
                    e.dma_start(out=vf[p % 2][:], in_=kvr[p + 2, 1]).then_inc(lp[p % 2], 16)
                    e.dma_start(out=rf[p % 2][:], in_=kvr[p + 2, 2]).then_inc(lp[p % 2], 16)
                if p + 1 < 16:
                    e.wait_ge(vsem, mi[("ogn", p)])
                    e.dma_start(out=gnwl[:], in_=gnw_d[p + 1]).then_inc(lg, 16)
                    e.dma_start(out=gnbl[:], in_=gnb_d[p + 1]).then_inc(lg, 16)
            e.wait_ge(stq, 16 * 16)

        for _ in range(2):
            emit_scalar(FakeCEng()); emit_tensor(FakeCEng()); emit_vector(FakeCEng())
        mi.strict = True

        @block.sync
        def _(e):
            emit_sync(e)

        @block.scalar
        def _(e):
            emit_scalar(CEng(e, ssem, auto=True))

        @block.tensor
        def _(e):
            emit_tensor(CEng(e, tsem))

        @block.vector
        def _(e):
            emit_vector(CEng(e, vsem, auto=True))
    return nc


def host_consts_E():
    i = np.arange(CHUNK, dtype=np.float32)
    pats = [i, (CHUNK - 1) - i, (i % 4) - 2, 1 - (i % 4),
            3 - (i % 4), i % 4, 7 - (i % 8), i % 8,
            15 - (i % 16), i % 16, 31 - (i % 32), i % 32, 63 - (i % 64), i % 64]
    idxp = np.broadcast_to(np.concatenate(pats).reshape(1, 14 * CHUNK), (64, 14 * CHUNK)).copy()
    jj = np.arange(CHUNK)[:, None]; ii = np.arange(CHUNK)[None, :]
    same4 = (jj // 4) == (ii // 4)
    mD = np.where(same4 & (ii > jj), float(2.0 ** 64), 0.0)
    mlv = [np.where(((jj // L) % 2 == 0) & ((ii // L) == (jj // L) + 1), 1.0, 0.0)
           for L in (4, 8, 16, 32, 64)]
    masksd = np.concatenate([np.tile(m, (1, 4)) for m in [mD] + mlv],
                            axis=1).astype(np.float32)
    idb = np.eye(64, dtype=np.float32).astype(ml_dtypes.bfloat16)
    ones64 = np.ones((64, 1), np.float32).astype(ml_dtypes.bfloat16)
    return {"idxp": idxp.astype(np.float32), "masksd": masksd, "idb_d": idb, "ones64_d": ones64}


def host_prep_E(inputs, kvrg_cores):
    consts = host_consts_E()
    td = inputs["L1_attn_time_decay"].astype(np.float32)
    tf = inputs["L1_attn_time_faaaa"].astype(np.float32)
    gnw = inputs["L1_attn_gn_weight"].astype(np.float32)
    gnb = inputs["L1_attn_gn_bias"].astype(np.float32)
    kA = [np.concatenate([kvrg_cores[4 * b + i][0 * D:1 * D] for i in range(4)], axis=1) for b in range(B)]
    vA = [np.concatenate([kvrg_cores[4 * b + i][1 * D:2 * D] for i in range(4)], axis=1) for b in range(B)]
    rA = [np.concatenate([kvrg_cores[4 * b + i][2 * D:3 * D] for i in range(4)], axis=1) for b in range(B)]
    in_maps = []
    for c in range(NCORES):
        b, h0 = c // 4, (c % 4) * 16
        kvrl = np.zeros((16, 3, 64, S), np.float32).astype(ml_dtypes.bfloat16)
        for p in range(16):
            h = h0 + p
            kvrl[p, 0] = kA[b][h * 64:(h + 1) * 64]
            kvrl[p, 1] = vA[b][h * 64:(h + 1) * 64]
            kvrl[p, 2] = rA[b][h * 64:(h + 1) * 64]
        tdcl = np.ascontiguousarray(td[h0:h0 + 16].T)
        tfcl = np.ascontiguousarray(tf[h0:h0 + 16].T)
        gnwl = np.zeros((16, 128, 64), np.float32)
        gnbl = np.zeros((16, 128, 64), np.float32)
        for p in range(16):
            h = h0 + p
            gnwl[p] = np.broadcast_to(gnw[h * 64:(h + 1) * 64], (128, 64))
            gnbl[p] = np.broadcast_to(gnb[h * 64:(h + 1) * 64], (128, 64))
        m = dict(consts)
        m.update({"kvr": kvrl, "tdc": tdcl, "tfc": tfcl,
                  "gnw_d": gnwl.astype(ml_dtypes.bfloat16),
                  "gnb_d": gnbl.astype(ml_dtypes.bfloat16)})
        in_maps.append(m)
    return in_maps



def build_launch_F(dbg=0):
    nc = bass.Bass("TRN2", target_bir_lowering=False)
    gsl = nc.dram_tensor("gsl", [D, TC], BF16, kind="ExternalInput").ap()
    gnsl = nc.dram_tensor("gnsl", [D, TC], BF16, kind="ExternalInput").ap()
    xn1 = nc.dram_tensor("xn1", [D, TC], BF16, kind="ExternalInput").ap()
    x1f = nc.dram_tensor("x1f", [D, TC], F32, kind="ExternalInput").ap()
    wout_d = nc.dram_tensor("wout_d", [32, 128, D], F32, kind="ExternalInput").ap()
    wbase_d = nc.dram_tensor("wbase_d", [32, 128, D], F32, kind="ExternalInput").ap()
    aflat_d = nc.dram_tensor("aflat_d", [128, D], F32, kind="ExternalInput").ap()
    bflat_d = nc.dram_tensor("bflat_d", [128, D], F32, kind="ExternalInput").ap()
    wr8_d = nc.dram_tensor("wr8_d", [128, 256], F32, kind="ExternalInput").ap()
    bb_d = nc.dram_tensor("bb_d", [128, 32], F32, kind="ExternalInput").ap()
    rep_d = nc.dram_tensor("rep_d", [8, 128], BF16, kind="ExternalInput").ap()
    i8_d = nc.dram_tensor("i8_d", [8, 8], F32, kind="ExternalInput").ap()
    i128_d = nc.dram_tensor("i128_d", [128, 128], F32, kind="ExternalInput").ap()
    o_out = nc.dram_tensor("o_out", [D, TC], F32, kind="ExternalOutput").ap()
    if dbg:
        o_dbg = nc.dram_tensor("o_dbg", [384, TC], BF16, kind="ExternalOutput").ap()
        o_dbg2 = nc.dram_tensor("o_dbg2", [8, TC], F32, kind="ExternalOutput").ap()
        o_dbg3 = nc.dram_tensor("o_dbg3", [256, D], BF16, kind="ExternalOutput").ap()
        o_dbg4 = nc.dram_tensor("o_dbg4", [128, TC], BF16, kind="ExternalOutput").ap()
    NM = 32
    with ExitStack() as ctx:
        sbt = lambda name, shape, dt=F32: ctx.enter_context(nc.sbuf_tensor(name, shape, dt))
        xnb = sbt("xnb", [128, NKT * TC], BF16)
        h1 = sbt("h1", [128, NKT * TC], BF16)
        gstg = [sbt(f"gstg{i}", [128, TC], BF16) for i in range(2)]
        gnstg = [sbt(f"gnstg{i}", [128, TC], BF16) for i in range(2)]
        wst = sbt("wst", [128, D]); wst2 = wst
        _wbf = sbt("wbf", [128, D], BF16); wbf = [_wbf, _wbf]
        _wbf2 = sbt("wbf2", [128, D], BF16); wbf2 = [_wbf2, _wbf2]
        afb = sbt("afb", [128, D], BF16); bfb = sbt("bfb", [128, D], BF16)
        wr8 = sbt("wr8", [128, 256]); wr8b = sbt("wr8b", [128, 256], BF16)
        bbl = sbt("bbl", [128, 32])
        repm = sbt("repm", [8, 128], BF16); i8 = sbt("i8", [8, 8]); i128f = sbt("i128f", [128, 128])
        lgrow = sbt("lgrow", [8, TC])
        wet = lgrow[:].bitcast(BF16)[:, 0:TC]
        lgt = sbt("lgt", [128, 8])
        l1c = sbt("l1c", [128, 1]); l2c = sbt("l2c", [128, 1])
        eq1 = sbt("eq1", [128, 8]); eq2 = sbt("eq2", [128, 8]); tmp8 = sbt("tmp8", [128, 8])
        w1c = sbt("w1c", [128, 1]); w2c = sbt("w2c", [128, 1])
        web = gstg[0]
        hb = gnstg[0]
        hw = gstg[1]
        x1t = [sbt(f"x1t{i}", [128, TC]) for i in range(2)]
        outf = [sbt(f"outf{i}", [128, TC]) for i in range(2)]
        ps_o = [ctx.enter_context(nc.psum_tensor(f"ps_o{i}", [128, TC], F32)) for i in range(2)]
        ps_b = ctx.enter_context(nc.psum_tensor("ps_b", [128, TC], F32))
        sem = lambda name: ctx.enter_context(nc.semaphore(name))
        ldc = sem("ldc"); lgn = [sem("lgna"), sem("lgnb")]; lxn = sem("lxn")
        laf = [sem("lafa"), sem("lafb")]
        lw = sem("lw"); lw2 = sem("lw2"); lx = [sem("lxa"), sem("lxb")]
        st1 = [sem("st1a"), sem("st1b")]
        vsem = sem("vsem"); ssem = sem("ssem"); tsem = sem("tsem")
        block = ctx.enter_context(nc.Block())
        mi = Mi()
        SCALING = 2.0

        def emit_vector(E):
            E.wait(ldc, 16 * 5)
            E.tensor_copy(out=wr8b[:], in_=wr8[:])
            mi["wr8b"] = E.n
            for q in range(4):
                E.wait(laf[q % 2], 16 * (q // 2 + 1))
                E.tensor_copy(out=afb[:, q * 1024:(q + 1) * 1024], in_=outf[q % 2][:])
                mi[("afq", q)] = E.n
            mi["afb"] = E.n
            for q in range(4):
                g = 4 + q
                E.wait(laf[g % 2], 16 * (g // 2 + 1))
                E.tensor_copy(out=bfb[:, q * 1024:(q + 1) * 1024], in_=outf[g % 2][:])
                mi[("bfq", q)] = E.n
            mi["bfb"] = E.n
            for kt in range(NKT):
                E.wait(lgn[kt % 2], 16 * 2 * (kt // 2 + 1))
                E.tensor_tensor(out=h1[:, kt * TC:(kt + 1) * TC], in0=gstg[kt % 2][:],
                                in1=gnstg[kt % 2][:], op=ALU.mult)
                mi[("h1", kt)] = E.n
            E.wait(tsem, mi["rmm"])
            E.tensor_copy(out=lgrow[:], in_=ps_o[0][0:8, :])
            mi["lgrow"] = E.n
            for t in range(8):
                E.wait(tsem, mi[("lgT", t)])
                E.tensor_copy(out=lgt[:], in_=ps_b[:, 0:8])
                E.tensor_reduce(out=l1c[:], in_=lgt[:], axis=AX.X, op=ALU.max)
                E.tensor_scalar(out=eq1[:], in0=lgt[:], scalar1=l1c[:], scalar2=None, op0=ALU.is_equal)
                E.tensor_scalar(out=tmp8[:], in0=eq1[:], scalar1=-1e30, scalar2=None, op0=ALU.mult)
                E.tensor_tensor(out=tmp8[:], in0=lgt[:], in1=tmp8[:], op=ALU.add)
                E.tensor_reduce(out=l2c[:], in_=tmp8[:], axis=AX.X, op=ALU.max)
                E.tensor_scalar(out=eq2[:], in0=lgt[:], scalar1=l2c[:], scalar2=None, op0=ALU.is_equal)
                E.tensor_tensor(out=w1c[:], in0=l2c[:], in1=l1c[:], op=ALU.subtract)
                mi[("d21", t)] = E.n
                E.wait(ssem, mi[("expw", t)])
                E.tensor_scalar(out=w1c[:], in0=w1c[:], scalar1=1.0, scalar2=None, op0=ALU.add)
                E.reciprocal(out=w1c[:], in_=w1c[:])
                E.tensor_scalar(out=w1c[:], in0=w1c[:], scalar1=SCALING, scalar2=None, op0=ALU.mult)
                E.tensor_scalar(out=w2c[:], in0=w1c[:], scalar1=-1.0, scalar2=SCALING,
                                op0=ALU.mult, op1=ALU.add)
                E.tensor_scalar(out=eq1[:], in0=eq1[:], scalar1=w1c[:], scalar2=None, op0=ALU.mult)
                E.tensor_scalar(out=eq2[:], in0=eq2[:], scalar1=w2c[:], scalar2=None, op0=ALU.mult)
                E.tensor_tensor(out=eq1[:], in0=eq1[:], in1=eq2[:], op=ALU.add)
                mi[("we", t)] = E.n
                E.wait(tsem, mi[("wetT", t)])
                E.tensor_copy(out=wet[:, t * 128:(t + 1) * 128], in_=ps_b[0:8, 0:128])
                mi[("wetev", t)] = E.n
            E.wait(tsem, mi["webmm"])
            E.tensor_copy(out=web[:], in_=ps_b[:])
            mi["webev"] = E.n
            E.wait(tsem, mi["hmm"])
            E.tensor_copy(out=hb[:], in_=ps_o[1][:])
            E.tensor_tensor(out=hw[:], in0=hb[:], in1=web[:], op=ALU.mult)
            mi["hw"] = E.n
            for m in range(NM):
                E.wait(tsem, mi[("mm", m)])
                E.wait(lx[m % 2], 16 * (m // 2 + 1))
                if m >= 2:
                    E.wait(st1[m % 2], 16 * (m // 2))
                E.tensor_scalar(out=outf[m % 2][:], in0=ps_o[m % 2][:], scalar1=bbl[:, m:m + 1],
                                scalar2=None, op0=ALU.add)
                E.tensor_tensor(out=outf[m % 2][:], in0=outf[m % 2][:], in1=x1t[m % 2][:], op=ALU.add)
                mi[("out", m)] = E.n

        def emit_scalar(E):
            for t in range(8):
                E.wait(vsem, mi[("d21", t)])
                E.activation(out=w1c[:], in_=w1c[:], func=AF.Exp)
                mi[("expw", t)] = E.n
            for m in range(NM):
                E.wait(lw, 16 * (2 * m + 1))
                if m >= 1:
                    E.wait(tsem, mi[("mm", m - 1)])
                E.activation(out=wbf[m % 2][:], in_=wst[:], func=AF.Identity)
                mi[("wcast", m)] = E.n
                E.wait(lw, 16 * (2 * m + 2))
                E.activation(out=wbf2[m % 2][:], in_=wst2[:], func=AF.Identity)
                mi[("wcast2", m)] = E.n

        def emit_tensor(E):
            E.wait(ldc, 16 * 5)
            E.wait(lxn, 16 * NKT)
            E.wait(vsem, mi["wr8b"])
            for kt in range(NKT):
                for h in range(2):
                    last = (kt == NKT - 1 and h == 1)
                    f = E.matmul if last else E.un("matmul")
                    f(ps_o[0][0:8, h * 512:(h + 1) * 512], wr8b[:, kt * 8:(kt + 1) * 8],
                      xnb[:, kt * TC + h * 512:kt * TC + (h + 1) * 512],
                      start=(kt == 0), stop=(kt == NKT - 1))
            mi["rmm"] = E.n
            E.wait(vsem, mi["afb"])
            for kt in range(NKT):
                for h in range(2):
                    last = (kt == NKT - 1 and h == 1)
                    f = E.matmul if last else E.un("matmul")
                    f(ps_o[1][:, h * 512:(h + 1) * 512], afb[:, kt * 128:(kt + 1) * 128],
                      xnb[:, kt * TC + h * 512:kt * TC + (h + 1) * 512],
                      start=(kt == 0), stop=(kt == NKT - 1))
            mi["hmm"] = E.n
            for t in range(8):
                E.wait(vsem, mi["lgrow"] if t == 0 else mi[("wetev", t - 1)])
                E.matmul(ps_b[:, 0:8], lgrow[:, t * 128:(t + 1) * 128], i8[:], start=True, stop=True)
                mi[("lgT", t)] = E.n
                E.wait(vsem, mi[("we", t)])
                E.matmul(ps_b[0:8, 0:128], eq1[:], i128f[:], start=True, stop=True)
                mi[("wetT", t)] = E.n
            E.wait(vsem, mi[("wetev", 7)])
            E.un("matmul")(ps_b[:, 0:512], repm[:], wet[:, 0:512], start=True, stop=True)
            E.matmul(ps_b[:, 512:1024], repm[:], wet[:, 512:1024], start=True, stop=True)
            mi["webmm"] = E.n
            E.wait(vsem, mi["hw"])
            E.wait(vsem, mi["bfb"])
            for m in range(NM):
                E.wait(ssem, mi[("wcast2", m)])
                if m >= 2:
                    E.wait(vsem, mi[("out", m - 2)])
                pso = ps_o[m % 2]
                for h in range(2):
                    E.un("matmul")(pso[:, h * 512:(h + 1) * 512],
                                   bfb[:, m * 128:(m + 1) * 128],
                                   hw[:, h * 512:(h + 1) * 512], start=True, stop=False)
                    for kt in range(NKT):
                        E.un("matmul")(pso[:, h * 512:(h + 1) * 512],
                                       wbf[m % 2][:, kt * 128:(kt + 1) * 128],
                                       h1[:, kt * TC + h * 512:kt * TC + (h + 1) * 512],
                                       start=False, stop=False)
                    for kt in range(NKT):
                        last = (h == 1 and kt == NKT - 1)
                        f = E.matmul if last else E.un("matmul")
                        f(pso[:, h * 512:(h + 1) * 512],
                          wbf2[m % 2][:, kt * 128:(kt + 1) * 128],
                          xnb[:, kt * TC + h * 512:kt * TC + (h + 1) * 512],
                          start=False, stop=(kt == NKT - 1))
                mi[("mm", m)] = E.n

        def emit_vector_post(E):
            pass

        def emit_sync(e):
            e.dma_start(out=wr8[:], in_=wr8_d).then_inc(ldc, 16)
            e.dma_start(out=bbl[:], in_=bb_d).then_inc(ldc, 16)
            e.dma_start(out=repm[:], in_=rep_d).then_inc(ldc, 16)
            e.dma_start(out=i8[:], in_=i8_d).then_inc(ldc, 16)
            e.dma_start(out=i128f[:], in_=i128_d).then_inc(ldc, 16)
            for q in range(4):
                if q >= 2:
                    e.wait_ge(vsem, mi[("afq", q - 2)])
                e.dma_start(out=outf[q % 2][:], in_=aflat_d[:, q * 1024:(q + 1) * 1024]).then_inc(laf[q % 2], 16)
            for q in range(4):
                g = 4 + q
                if q >= 2:
                    e.wait_ge(vsem, mi[("bfq", q - 2)])
                else:
                    e.wait_ge(vsem, mi[("afq", q + 2)])
                e.dma_start(out=outf[g % 2][:], in_=bflat_d[:, q * 1024:(q + 1) * 1024]).then_inc(laf[g % 2], 16)
            for i in range(NKT):
                e.dma_start(out=xnb[:, i * TC:(i + 1) * TC],
                            in_=xn1[i * 128:(i + 1) * 128, :]).then_inc(lxn, 16)
            for kt in range(NKT):
                if kt >= 2:
                    e.wait_ge(vsem, mi[("h1", kt - 2)])
                e.dma_start(out=gstg[kt % 2][:],
                            in_=gsl[kt * 128:(kt + 1) * 128, :]).then_inc(lgn[kt % 2], 16)
                e.dma_start(out=gnstg[kt % 2][:],
                            in_=gnsl[kt * 128:(kt + 1) * 128, :]).then_inc(lgn[kt % 2], 16)
            e.dma_start(out=wst[:], in_=wout_d[0]).then_inc(lw, 16)
            e.wait_ge(ssem, mi[("wcast", 0)])
            e.dma_start(out=wst[:], in_=wbase_d[0]).then_inc(lw, 16)
            for m in range(NM):
                if m + 1 < NM:
                    e.wait_ge(ssem, mi[("wcast2", m)])
                    e.dma_start(out=wst[:], in_=wout_d[m + 1]).then_inc(lw, 16)
                    e.wait_ge(ssem, mi[("wcast", m + 1)])
                    e.dma_start(out=wst[:], in_=wbase_d[m + 1]).then_inc(lw, 16)
                e.wait_ge(vsem, mi[("out", m - 1)] if m >= 1 else 0)
                e.dma_start(out=x1t[m % 2][:], in_=x1f[m * 128:(m + 1) * 128, :]).then_inc(lx[m % 2], 16)
                e.wait_ge(vsem, mi[("out", m)])
                e.dma_start(out=o_out[m * 128:(m + 1) * 128, :],
                            in_=outf[m % 2][:]).then_inc(st1[m % 2], 16)
            if dbg:
                e.wait_ge(vsem, mi["hw"])
                e.dma_start(out=o_dbg[0:128], in_=hb[:]).then_inc(ldc, 16)
                e.dma_start(out=o_dbg[128:256], in_=web[:]).then_inc(ldc, 16)
                e.dma_start(out=o_dbg[256:384], in_=hw[:]).then_inc(ldc, 16)
                e.dma_start(out=o_dbg2, in_=lgrow[:]).then_inc(ldc, 16)
                e.dma_start(out=o_dbg3[0:128], in_=afb[:]).then_inc(ldc, 16)
                e.dma_start(out=o_dbg3[128:256], in_=bfb[:]).then_inc(ldc, 16)
                e.dma_start(out=o_dbg4, in_=h1[:, 0:TC]).then_inc(ldc, 16)
                e.wait_ge(ldc, 16 * 12)
            e.wait_ge(st1[0], 16 * 16); e.wait_ge(st1[1], 16 * 16)

        def emit_vector_full(E):
            emit_vector(E)

        _orig_emit_vector = emit_vector

        def emit_vector2(E):
            _orig_emit_vector(E)


        for _ in range(2):
            emit_vector(FakeCEng()); emit_scalar(FakeCEng()); emit_tensor(FakeCEng())
        mi.strict = True

        @block.sync
        def _(e):
            emit_sync(e)

        @block.scalar
        def _(e):
            emit_scalar(CEng(e, ssem, auto=True))

        @block.tensor
        def _(e):
            emit_tensor(CEng(e, tsem))

        @block.vector
        def _(e):
            emit_vector(CEng(e, vsem, auto=True))
    return nc


def host_prep_F(inputs, x1_cores, xn1_cores, kvrg_cores, gn_cores):
    Wout = inputs["L1_attn_w_output"].astype(np.float32)
    wout_d = _w_prep_lhsT(Wout.T)
    Wb = inputs["L1_ffn_W_base"].astype(np.float32)
    wbase_d = _w_prep_lhsT(Wb.T)
    A = inputs["L1_ffn_A"].astype(np.float32)
    aflat_d = np.ascontiguousarray(A.reshape(128, D).T)
    aflat_d = _w_prep_lhsT(A.reshape(128, D).T)[0]
    Bm = inputs["L1_ffn_B"].astype(np.float32)
    bflat_d = np.ascontiguousarray(Bm.transpose(0, 2, 1).reshape(128, D))
    Wr = inputs["L1_ffn_W_router"].astype(np.float32)
    wr8 = np.ascontiguousarray(Wr.T.reshape(NKT, 128, 8).transpose(1, 0, 2).reshape(128, 256))
    bb = inputs["L1_ffn_b_base"].astype(np.float32)
    bb_d = np.ascontiguousarray(bb.reshape(32, 128).T)
    rep = np.zeros((8, 128), np.float32)
    for e in range(8):
        rep[e, e * 16:(e + 1) * 16] = 1.0
    gnf = [np.zeros((D, S), np.float32) for _ in range(B)]
    for c in range(NCORES):
        b, h0 = c // 4, (c % 4) * 16
        g = np.asarray(gn_cores[c], np.float32).reshape(16, 128, NCH, 64)
        for p in range(16):
            h = h0 + p
            gnf[b][h * 64:(h + 1) * 64, :] = g[p].transpose(1, 0, 2).reshape(S, 64).T
    in_maps = []
    for c in range(NCORES):
        b, sl = c // 4, slice((c % 4) * TC, (c % 4) * TC + TC)
        gslice = kvrg_cores[c][3 * D:4 * D]
        in_maps.append({
            "gsl": np.ascontiguousarray(gslice),
            "gnsl": np.ascontiguousarray(gnf[b][:, sl]).astype(ml_dtypes.bfloat16),
            "xn1": xn1_cores[c], "x1f": x1_cores[c],
            "wout_d": wout_d, "wbase_d": wbase_d, "aflat_d": aflat_d,
            "bflat_d": bflat_d, "wr8_d": wr8, "bb_d": bb_d,
            "rep_d": rep.astype(ml_dtypes.bfloat16),
            "i8_d": np.eye(8, dtype=np.float32),
            "i128_d": np.eye(128, dtype=np.float32),
        })
    return in_maps



def kernel(**inputs):
    C = list(range(NCORES))
    rA = run_bass_kernel_spmd(build_launch_A(), host_prep_A(inputs), core_ids=C)
    qkv = [np.asarray(r["o_qkv"]) for r in rA.results]
    xn0 = [np.asarray(r["o_xn"]) for r in rA.results]
    rB = run_bass_kernel_spmd(build_launch_B(), host_prep_B(qkv), core_ids=C)
    att = [np.asarray(r["o_att"]) for r in rB.results]
    rC1 = run_bass_kernel_spmd(build_launch_C1(), host_prep_C1(inputs, att), core_ids=C)
    x1p = [np.asarray(r["o_x1p"]) for r in rC1.results]
    rC2 = run_bass_kernel_spmd(build_launch_C2(), host_prep_C2(inputs, x1p, xn0), core_ids=C)
    x1 = [np.asarray(r["o_x1"]) for r in rC2.results]
    xn1 = [np.asarray(r["o_xn1"]) for r in rC2.results]
    rD = run_bass_kernel_spmd(build_launch_D(), host_prep_D(inputs, xn1), core_ids=C)
    kvrg = [np.asarray(r["o_kvrg"]) for r in rD.results]
    rE = run_bass_kernel_spmd(build_launch_E(), host_prep_E(inputs, kvrg), core_ids=C)
    gn = [np.asarray(r["o_gn"]) for r in rE.results]
    rF = run_bass_kernel_spmd(build_launch_F(), host_prep_F(inputs, x1, xn1, kvrg, gn), core_ids=C)
    full = np.zeros((B, S, D), np.float32)
    for c in range(NCORES):
        b, sl = c // 4, slice((c % 4) * TC, (c % 4) * TC + TC)
        full[b, sl, :] = np.asarray(rF.results[c]["o_out"], np.float32).T
    return full
